# version 05
# baseline (speedup 1.0000x reference)
import numpy as np
import ml_dtypes
import concourse.bass as bass
import concourse.mybir as mybir
from concourse.bass_utils import run_bass_kernel_spmd
from contextlib import ExitStack

F32 = mybir.dt.float32
BF16 = mybir.dt.bfloat16
U32 = mybir.dt.uint32
AF = mybir.ActivationFunctionType
ALU = mybir.AluOpType
AX = mybir.AxisListType
IOA = bass.IndirectOffsetOnAxis

D = 1024
S = 2048
NB = 2
T = NB * S
CT = D // 128
TCH = 512

ENGINES = ("tensor", "vector", "scalar", "gpsimd", "sync")
SAFE_SAME = {"tensor"}


class Op:
    __slots__ = ("eng", "fn", "is_dma", "deps", "inc", "cum", "dmacum", "label")

    def __init__(self, eng, fn, is_dma, label):
        self.eng, self.fn, self.is_dma, self.label = eng, fn, is_dma, label
        self.deps = []
        self.inc = False
        self.cum = 0
        self.dmacum = 0


class Buf:
    __slots__ = ("t", "name", "last_w", "readers")

    def __init__(self, t, name):
        self.t, self.name = t, name
        self.last_w = None
        self.readers = []

    def __getitem__(self, sl):
        return self.t[sl]

    @property
    def ap(self):
        return self.t[:]


class SubBuf:
    __slots__ = ("parent", "c0", "width", "pp", "name", "last_w", "readers",
                 "overlaps")

    def __init__(self, parent, c0, width, pp, name):
        self.parent, self.c0, self.width, self.pp = parent, c0, width, pp
        self.name = name
        self.last_w = None
        self.readers = []
        self.overlaps = []

    def __getitem__(self, sl):
        if not isinstance(sl, tuple):
            return self.parent.t[sl, self.c0:self.c0 + self.width]
        ps, fs = sl
        f0 = 0 if fs.start is None else fs.start
        f1 = self.width if fs.stop is None else fs.stop
        return self.parent.t[ps, slice(f0 + self.c0, f1 + self.c0)]

    @property
    def ap(self):
        return self.parent.t[0:self.pp, self.c0:self.c0 + self.width]


class Prog:
    def __init__(self, nc, stack):
        self.nc, self.stack = nc, stack
        self.ops = {e: [] for e in ENGINES}
        self.alias = {}
        self.pools = {}

    def mkpool(self, name, cols, dt=BF16):
        self.pools[name] = Buf(self.stack.enter_context(
            self.nc.sbuf_tensor(name, [128, cols], dt)), name)
        return self.pools[name]

    def sb(self, name, shape, dt=F32):
        if name in self.alias:
            if not hasattr(self, "_aviews"):
                self._aviews = {}
            if name in self._aviews:
                v = self._aviews[name]
                assert shape[1] <= v.width, f"alias {name} resized"
                return v
            pool, c0, w = self.alias[name]
            b = self.pools[pool]
            assert c0 + shape[1] <= b.t.shape[1], \
                f"alias {name}: [{c0},{c0 + shape[1]}) exceeds pool {pool}"
            assert shape[1] <= w, f"alias {name} too small: {shape[1]} > {w}"
            assert b.t.dtype == dt, f"alias {name}: dtype mismatch vs pool {pool}"
            v = SubBuf(b, c0, shape[1], shape[0], name)
            for o in self._aviews.values():
                if o.parent is b and o.c0 < c0 + shape[1] and c0 < o.c0 + o.width:
                    o.overlaps.append(v)
                    v.overlaps.append(o)
            self._aviews[name] = v
            return v
        return Buf(self.stack.enter_context(self.nc.sbuf_tensor(name, list(shape), dt)), name)

    def init_consts(self, ncols=12):
        self.cpool = self.sb("cpool", [128, ncols], F32)
        self.cvals = {}

    def constf(self, val):
        val = float(val)
        if val not in self.cvals:
            j = len(self.cvals)
            self.cvals[val] = j
            ap = self.cpool[:, j:j + 1]
            self.nc.const_aps.register(F32, val, ap)
            self.emit("gpsimd", lambda h, a=ap, v=val: h.memset(a, v),
                      writes=(self.cpool,), label=f"const_{val}")
        return self.cpool

    def sb_shared(self, name, shape, dt=F32):
        if not hasattr(self, "_shared"):
            self._shared = {}
        if name not in self._shared:
            self._shared[name] = self.sb(name, shape, dt)
        return self._shared[name]

    def ps(self, name, shape, dt=F32):
        return Buf(self.stack.enter_context(self.nc.psum_tensor(name, list(shape), dt)), name)

    def emit(self, eng, fn, reads=(), writes=(), dma=False, label="", sreads=()):
        op = Op(eng, fn, dma, label)
        self.ops[eng].append(op)

        def dep(p, force=False):
            if p is None or p is op:
                return
            if (not force) and (not p.is_dma) and (not op.is_dma)                     and p.eng == eng and eng in SAFE_SAME:
                return
            if p not in op.deps:
                op.deps.append(p)

        for r in sreads:
            if r is None:
                continue
            dep(r.last_w, force=True)
            for ov in getattr(r, "overlaps", ()):
                dep(ov.last_w, force=True)
            r.readers.append(op)
        for r in reads:
            if r is None:
                continue
            dep(r.last_w)
            for ov in getattr(r, "overlaps", ()):
                dep(ov.last_w)
            r.readers.append(op)
        for w in writes:
            if w is None:
                continue
            dep(w.last_w)
            for rd in w.readers:
                dep(rd)
            for ov in getattr(w, "overlaps", ()):
                dep(ov.last_w)
                for rd in ov.readers:
                    dep(rd)
            w.last_w = op
            w.readers = []
        return op

    def dma(self, queue, out, in_, reads=(), writes=(), label="dma", **kw):
        return self.emit(queue, lambda e: e.dma_start(out, in_, **kw),
                         reads=reads, writes=writes, dma=True, label=label)

    def finalize(self):
        nc = self.nc
        for e in ENGINES:
            for o in self.ops[e]:
                if o.is_dma:
                    o.inc = True
                for pdep in o.deps:
                    pdep.inc = True
        for e in ENGINES:
            c = d = 0
            for o in self.ops[e]:
                if o.is_dma:
                    d += 16
                    o.dmacum = d
                elif o.inc:
                    c += 1
                o.cum = c
        esem = {e: self.stack.enter_context(nc.semaphore(f"p_{e}"))
                for e in ENGINES if any((not o.is_dma) and o.inc for o in self.ops[e])}
        dsem = {e: self.stack.enter_context(nc.semaphore(f"d_{e}"))
                for e in ENGINES if any(o.is_dma for o in self.ops[e])}
        block = self.stack.enter_context(nc.Block())
        handles = {"tensor": nc.tensor, "vector": nc.vector, "scalar": nc.scalar,
                   "gpsimd": nc.gpsimd, "sync": nc.sync}

        dma_final = {e: 16 * sum(1 for o in self.ops[e] if o.is_dma)
                     for e in dsem}
        term_eng = next((e for e in ("sync", "gpsimd", "vector", "scalar",
                                     "tensor") if self.ops[e]), None)

        def make_replay(eng):
            def replay(_h):
                h = handles[eng]
                waited = {}
                for o in self.ops[eng]:
                    for pd in o.deps:
                        if pd.is_dma:
                            key, val, s = ("d", pd.eng), pd.dmacum, dsem[pd.eng]
                        else:
                            key, val, s = ("e", pd.eng), pd.cum, esem[pd.eng]
                        if waited.get(key, 0) >= val:
                            continue
                        h.wait_ge(s, val)
                        waited[key] = val
                    inst = o.fn(h)
                    if o.is_dma:
                        inst.then_inc(dsem[eng], 16)
                    elif o.inc:
                        if inst is None:
                            raise RuntimeError(f"op '{o.label}' on {eng} needs inc but returned None")
                        inst.then_inc(esem[eng], 1)
                if eng == term_eng:
                    for q, fin in dma_final.items():
                        if fin > waited.get(("d", q), 0):
                            h.wait_ge(dsem[q], fin)
            return replay

        for eng in ENGINES:
            if self.ops[eng]:
                getattr(block, eng)(make_replay(eng))


def eact(p, out_ap, in_ap, func, bias=None, scale=None, accum=None,
         reads=(), writes=(), label="act"):
    kw = {}
    reads = list(reads)
    if bias is not None:
        kw["bias"] = bias
        if isinstance(bias, (int, float)) and float(bias) not in (0.0,):
            reads.append(p.constf(bias))
    if scale is not None:
        kw["scale"] = scale
        if isinstance(scale, (int, float)) and float(scale) not in (0.0, 1.0):
            reads.append(p.constf(scale))
    if accum is not None:
        kw["accum_out"] = accum
    return p.emit("scalar", lambda s: s.activation(out_ap, in_ap, func, **kw),
                  reads=reads, writes=writes, label=label)


def ev(p, fn, reads=(), writes=(), label="vec"):
    return p.emit("vector", fn, reads=reads, writes=writes, label=label)


def ev2(p, fn, reads=(), sreads=(), writes=(), label="vec"):
    return p.emit("vector", fn, reads=reads, sreads=sreads, writes=writes, label=label)


def et(p, fn, reads=(), writes=(), label="mm"):
    return p.emit("tensor", fn, reads=reads, writes=writes, label=label)


class Weights:

    def __init__(self, buf, K, N, stride=None):
        self.buf, self.K, self.N = buf, K, N
        self.stride = stride if stride is not None else N
        self.KB = (K + 127) // 128

    def tile(self, kb, n0, nw):
        return self.buf[:, kb * self.stride + n0: kb * self.stride + n0 + nw]


def load_weight(p, name, dram_t, K, N, queue="gpsimd", buf=None):
    KB = (K + 127) // 128
    if buf is None:
        buf = p.sb(f"w_{name}", [128, KB * N], BF16)
    if K % 128 != 0:
        raise ValueError(f"K={K} not multiple of 128 for {name}")
    ap_in = dram_t.ap().rearrange("(kb q) n -> q kb n", q=128)
    ap_out = buf.ap.rearrange("q (kb n) -> q kb n", kb=KB)
    p.dma(queue, ap_out, ap_in, writes=(buf,), label=f"ldw_{name}")
    return Weights(buf, K, N)


def load_weight_cols(p, wbuf, dram_t, K, n0, nw, queue="gpsimd", name="w",
                     stride=None):
    KB = K // 128
    ap_in = dram_t.ap()[:, n0:n0 + nw].rearrange("(kb q) n -> q kb n", q=128)
    ap_out = wbuf.ap.rearrange("q (kb n) -> q kb n", kb=KB)[:, :, 0:nw]
    p.dma(queue, ap_out, ap_in, reads=(wbuf,), writes=(wbuf,), label=f"ldw_{name}")
    if stride is None:
        wcols = wbuf.width if isinstance(wbuf, SubBuf) else wbuf.t.shape[1]
        stride = wcols // KB
    return Weights(wbuf, K, nw, stride=stride)


def load_weight_rows(p, wbuf, dram_t, k0, kh, N, queue="gpsimd", name="w"):
    ap_in = dram_t.ap()[k0:k0 + kh, :]
    p.dma(queue, wbuf[0:kh, 0:N], ap_in, reads=(wbuf,), writes=(wbuf,),
          label=f"ldw_{name}")
    return Weights(wbuf, kh, N)


def matmul_T(p, W, rhs_fn, out_cb, psums, K=None, N=None, Tlen=T, tch=TCH,
             n0=0, reads=(), label="proj"):
    K = K if K is not None else W.K
    N = N if N is not None else W.N
    nt = (Tlen + tch - 1) // tch
    KB = (K + 127) // 128
    npb = len(psums)
    co = 0
    while co < N:
        cw = min(128, N - co)
        for tg0 in range(0, nt, npb):
            tis = range(tg0, min(tg0 + npb, nt))
            for kb in range(KB):
                wtile = W.tile(kb, n0 + co, cw)
                for ti in tis:
                    t0 = ti * tch
                    tw = min(tch, Tlen - t0)
                    pb = psums[ti - tg0]
                    rhs_ap = rhs_fn(kb, t0, tw)
                    et(p, (lambda tt, o=pb[0:cw, 0:tw], w=wtile, r=rhs_ap,
                           st=(kb == 0), sp=(kb == KB - 1):
                           tt.matmul(o, w, r, start=st, stop=sp)),
                       reads=(W.buf,) + tuple(reads), writes=(pb,), label=label)
            for ti in tis:
                t0 = ti * tch
                tw = min(tch, Tlen - t0)
                out_cb(psums[ti - tg0], co, cw, t0, tw)
        co += cw


class Ctx:
    pass


def alloc_common(p, need_xn=True, nR=3, need_ident=True):
    c = Ctx()
    c.p = p
    p.init_consts()
    c.XT = p.sb("XT", [128, CT * T], BF16)
    c.XN = p.sb("XN", [128, CT * T], BF16) if need_xn else None
    c.ones_col_bf = p.sb("ones_col_bf", [128, 1], BF16)
    c.ones_row_f = p.sb("ones_row_f", [1, 128], F32)
    c.ones_row_bf = p.sb("ones_row_bf", [1, 128], BF16)
    c.recip_bf = p.sb("recip_bf", [128, 1], BF16)
    c.R = [p.sb(f"R{i}", [128, TCH], F32) for i in range(nR)]
    c.sqc = p.sb("sqc", [128, 2 * TCH], BF16)
    c.psums = [p.ps(f"pp{i}", [128, 512], F32) for i in range(6)]
    c.prow = [p.ps(f"pr{i}", [1, 512], F32) for i in range(2)]
    g = "gpsimd"
    if need_ident:
        c.ident_bf = p.sb("ident_bf", [128, 128], BF16)
        p.emit(g, lambda h: h.iota(c.ident_bf.ap, [[-1, 128]], base=0,
                                   channel_multiplier=1,
                                   allow_small_or_imprecise_dtypes=True),
               writes=(c.ident_bf,), label="ident_iota")
        ev(p, lambda v: v.tensor_scalar(c.ident_bf.ap, c.ident_bf.ap, 0.0, None,
                                        ALU.is_equal),
           reads=(c.ident_bf,), writes=(c.ident_bf,), label="ident_eq")
    p.emit(g, lambda h: h.memset(c.ones_col_bf.ap, 1.0), writes=(c.ones_col_bf,), label="c1")
    p.emit(g, lambda h: h.memset(c.ones_row_f.ap, 1.0), writes=(c.ones_row_f,), label="c2")
    p.emit(g, lambda h: h.memset(c.ones_row_bf.ap, 1.0), writes=(c.ones_row_bf,), label="c2b")
    p.emit(g, lambda h: h.memset(c.recip_bf.ap, 1.0 / D), writes=(c.recip_bf,), label="c3")
    return c


def load_const_sb(p, name, dram_t, shape, dt=BF16, queue="gpsimd", shared=False):
    if shared:
        if not hasattr(p, "_cshared"):
            p._cshared = {}
        if name in p._cshared:
            return p._cshared[name]
    buf = p.sb("cst_" + name, shape, dt)
    p.dma(queue, buf.ap, dram_t.ap(), writes=(buf,), label="ldc_" + name)
    if shared:
        p._cshared[name] = buf
    return buf


def emit_load_xt(p, buf, x_dram, queue="sync"):
    for b in range(NB):
        for ct in range(CT):
            dst = buf[:, ct * T + b * S: ct * T + (b + 1) * S]
            src = x_dram.ap()[b, ct * 128:(ct + 1) * 128, :]
            p.dma(queue, dst, src, writes=(buf,), label=f"ldx{b}_{ct}")


def emit_store_xt(p, buf, y_dram, queue="sync"):
    for b in range(NB):
        for ct in range(CT):
            src = buf[:, ct * T + b * S: ct * T + (b + 1) * S]
            dst = y_dram.ap()[b, ct * 128:(ct + 1) * 128, :]
            p.dma(queue, dst, src, reads=(buf,), label=f"stx{b}_{ct}")


def emit_rowsums(p, c, src, ti, dstrow, scale_col, square=False, ctiles=CT,
                 prow_i=0, srcT=T, label="rs"):
    pb = c.prow[prow_i]
    tw = min(TCH, srcT - ti * TCH)
    for ct in range(ctiles):
        sl = slice(ct * srcT + ti * TCH, ct * srcT + ti * TCH + tw)
        if square:
            sqsl = c.sqc[:, (ct % 2) * TCH: (ct % 2) * TCH + tw]
            eact(p, sqsl, src[:, sl], AF.Square, reads=(src,), writes=(c.sqc,),
                 label=label + "_sq")
            ap, rd = sqsl, (c.sqc, scale_col)
        else:
            ap, rd = src[:, sl], (src, scale_col)
        et(p, (lambda tt, o=pb[0:1, 0:tw], w=scale_col.ap, r=ap,
               st=(ct == 0), sp=(ct == ctiles - 1):
               tt.matmul(o, w, r, start=st, stop=sp)),
           reads=rd, writes=(pb,), label=label)
    eact(p, dstrow[0:1, 0:tw], pb[0:1, 0:tw], AF.Copy,
         reads=(pb,), writes=(dstrow,), label=label + "_ev")


def emit_bcast_chunk(p, c, row_buf, pb, tw=TCH, label="bc"):
    et(p, (lambda tt, o=pb[:, 0:tw], w=c.ones_row_f.ap, r=row_buf[0:1, 0:tw]:
           tt.matmul(o, w, r, start=True, stop=True)),
       reads=(row_buf, c.ones_row_f), writes=(pb,), label=label)


def emit_ln(p, c, src, dst, eps=1e-5, gcol=None, bcol=None,
            s1rep=None, s1eps=None, label="ln"):
    R0, R1, R2 = c.R
    if s1rep is not None and not hasattr(c, "R3"):
        c.R3 = p.sb("R3", [128, TCH], F32)
    R3 = getattr(c, "R3", None)
    for ti in range(T // TCH):
        emit_rowsums(p, c, src, ti, R0, c.recip_bf, prow_i=0, label=label + "_m")
        emit_rowsums(p, c, src, ti, R1, c.recip_bf, square=True, prow_i=1,
                     label=label + "_e")
        ev(p, lambda v: v.tensor_tensor(R2.ap, R0.ap, R0.ap, ALU.mult),
           reads=(R0,), writes=(R2,), label=label + "_m2")
        ev(p, lambda v: v.scalar_tensor_tensor(R1.ap, R2.ap, -1.0, R1.ap, ALU.mult, ALU.add),
           reads=(R2, R1), writes=(R1,), label=label + "_var")
        eact(p, R2.ap, R1.ap, AF.Ln, bias=float(eps), reads=(R1,), writes=(R2,),
             label=label + "_lnA")
        eact(p, R2.ap, R2.ap, AF.Exp, scale=-0.5, reads=(R2,), writes=(R2,),
             label=label + "_expA")
        ev(p, lambda v: v.tensor_tensor(R0.ap, R0.ap, R2.ap, ALU.mult),
           reads=(R0, R2), writes=(R0,), label=label + "_B")
        if s1rep is not None:
            ev(p, lambda v: v.tensor_tensor(R3.ap, R1.ap, R2.ap, ALU.mult),
               reads=(R1, R2), writes=(R3,), label=label + "_s1a")
            ev(p, lambda v: v.tensor_tensor(R3.ap, R3.ap, R2.ap, ALU.mult),
               reads=(R3, R2), writes=(R3,), label=label + "_s1b")
            eact(p, R3.ap, R3.ap, AF.Ln, bias=float(s1eps), reads=(R3,), writes=(R3,),
                 label=label + "_s1l")
            eact(p, R3.ap, R3.ap, AF.Exp, scale=-0.5, reads=(R3,), writes=(R3,),
                 label=label + "_s1e")
            pS = c.psums[2]
            emit_bcast_chunk(p, c, R3, pS, label=label + "_s1bc")
            eact(p, s1rep[:, ti * TCH:(ti + 1) * TCH], pS[:, :], AF.Copy,
                 reads=(pS,), writes=(s1rep,), label=label + "_s1ev")
        pA, pB = c.psums[0], c.psums[1]
        emit_bcast_chunk(p, c, R2, pA, label=label + "_bcA")
        emit_bcast_chunk(p, c, R0, pB, label=label + "_bcB")
        if dst is not None:
            for ct in range(CT):
                sl = slice(ct * T + ti * TCH, ct * T + (ti + 1) * TCH)
                def app(v, o=dst[:, sl], x=src[:, sl], a=pA[:, :], b=pB[:, :],
                        g=(gcol[:, ct:ct + 1] if gcol is not None else None),
                        bb=(bcol[:, ct:ct + 1] if bcol is not None else None)):
                    v.tensor_tensor(o, x, a, ALU.mult)
                    i2 = v.tensor_tensor(o, o, b, ALU.subtract)
                    if g is not None:
                        i2 = v.tensor_scalar(o, o, g, bb, ALU.mult, ALU.add)
                    return i2
                rd = [src, pA, pB] + ([gcol] if gcol is not None else [])                     + ([bcol] if bcol is not None else [])
                ev(p, app, reads=rd, writes=(dst,), label=label + "_app")


def emit_rms_rep(p, c, src, rep, eps, wcol=None, label="rms"):
    R1 = c.R[1]
    for ti in range(T // TCH):
        emit_rowsums(p, c, src, ti, R1, c.recip_bf, square=True, prow_i=0,
                     label=label + "_e")
        eact(p, R1.ap, R1.ap, AF.Ln, bias=float(eps), reads=(R1,), writes=(R1,),
             label=label + "_l")
        eact(p, R1.ap, R1.ap, AF.Exp, scale=-0.5, reads=(R1,), writes=(R1,),
             label=label + "_x")
        pS = c.psums[3]
        emit_bcast_chunk(p, c, R1, pS, label=label + "_bc")
        eact(p, rep[:, ti * TCH:(ti + 1) * TCH], pS[:, :], AF.Copy,
             reads=(pS,), writes=(rep,), label=label + "_ev")


def emit_prenorm(p, c, s1rep=None, s1eps=None):
    emit_ln(p, c, c.XT, c.XN, eps=1e-5, s1rep=s1rep, s1eps=s1eps, label="pn")


def alloc_flash(p, c, dh_max=96, nkt_max=16):
    f = Ctx()
    f.qT = p.sb("fl_qT", [128, S], BF16)
    f.kT = p.sb("fl_kT", [128, S], BF16)
    f.vx = p.sb("fl_vx", [128, 16 * 130], BF16)
    f.att = p.sb("fl_att", [128, S], BF16)
    f.ncol = p.sb("fl_ncol", [128, 16], BF16)
    f.mcol = p.sb("fl_mcol", [128, 4], F32)
    f.nrow = p.sb("fl_nrow", [128, S], BF16)
    f.ms = p.sb("fl_ms", [128, 512], F32)
    f.st = f.ms
    f.rl = f.ms
    f.pt = [p.sb(f"fl_pt{i}", [128, 512], BF16) for i in range(2)]
    f.ob = p.sb("fl_ob", [128, 512], BF16)
    return f


def emit_flash_head(p, c, f, qT, kT, qbase, h_in_g, dh, att_out, atbase,
                    a_spec, c_spec, label="fl"):
    NQT = S // 128
    for qt in range(NQT):
        chunks = a_spec(qt)
        pb = c.psums[4]
        for j, (k0, kw, mk) in enumerate(chunks):
            et(p, (lambda tt, o=pb[0:128, 0:kw], w=qT[qbase:qbase + dh, qt * 128:(qt + 1) * 128],
                   r=kT[qbase:qbase + dh, k0:k0 + kw]:
                   tt.matmul(o, w, r, start=True, stop=True)),
               reads=(qT, kT), writes=(pb,), label=label + "_sa")
            if mk is not None:
                ev(p, (lambda v, o=f.ms[:, 0:kw], a=pb[0:128, 0:kw], m=mk[0][0:128, mk[1]:mk[1] + kw]:
                       v.tensor_tensor(o, a, m, ALU.add)),
                   reads=(pb, mk[0]), writes=(f.ms,), label=label + "_samask")
                srcb, srcap = f.ms, f.ms[:, 0:kw]
            else:
                srcb, srcap = pb, pb[0:128, 0:kw]
            ev(p, (lambda v, o=f.mcol[:, j:j + 1], i=srcap:
                   v.reduce_max(o, i, AX.X)),
               reads=(srcb,), writes=(f.mcol,), label=label + "_sared")
        nj = len(chunks)
        ev(p, (lambda v, o=f.ncol[:, qt:qt + 1], i=f.mcol[:, 0:nj]:
               v.reduce_max(o, i, AX.X, negate=True)),
           reads=(f.mcol,), writes=(f.ncol,), label=label + "_saneg")
    for g0 in range(0, NQT, 4):
        pb = c.prow[0]
        for j in range(4):
            qt = g0 + j
            et(p, (lambda tt, o=pb[0:1, j * 128:(j + 1) * 128],
                   w=f.ncol[:, qt:qt + 1], r=c.ident_bf.ap,
                   st=(j == 0), sp=(j == 3):
                   tt.matmul(o, w, r, start=st, stop=sp)),
               reads=(f.ncol, c.ident_bf), writes=(pb,), label=label + "_sbrow")
        eact(p, f.nrow[0:1, g0 * 128:(g0 + 4) * 128], pb[0:1, 0:512], AF.Copy,
             reads=(pb,), writes=(f.nrow,), label=label + "_sbev")
    for qc in range(S // 512):
        kts = c_spec(qc)
        po = c.psums[2]
        for ki, (kt0, mkT) in enumerate(kts):
            ps_ = c.psums[ki % 2]
            et(p, (lambda tt, o=ps_[0:128, 0:512],
                   w=kT[qbase:qbase + dh, kt0:kt0 + 128],
                   r=qT[qbase:qbase + dh, qc * 512:(qc + 1) * 512]:
                   tt.matmul(o, w, r, start=True, stop=False)),
               reads=(kT, qT), writes=(ps_,), label=label + "_st")
            et(p, (lambda tt, o=ps_[0:128, 0:512], w=c.ones_row_bf.ap,
                   r=f.nrow[0:1, qc * 512:(qc + 1) * 512]:
                   tt.matmul(o, w, r, start=False, stop=True)),
               reads=(f.nrow, c.ones_row_bf), writes=(ps_,), label=label + "_stm")
            ptb = f.pt[ki % 2]
            if mkT is not None:
                mb, mo = mkT
                ev(p, (lambda v, o=f.st.ap, a=ps_[0:128, 0:512], m=mb[0:128, mo:mo + 512]:
                       v.tensor_tensor(o, a, m, ALU.add)),
                   reads=(ps_, mb), writes=(f.st,), label=label + "_cmask")
                eact(p, ptb.ap, f.st.ap, AF.Exp, reads=(f.st,), writes=(ptb,),
                     label=label + "_exp")
            else:
                eact(p, ptb.ap, ps_[0:128, 0:512], AF.Exp, reads=(ps_,), writes=(ptb,),
                     label=label + "_exp")
            tt_idx = kt0 // 128
            vsl = f.vx[:, tt_idx * 130 + h_in_g * 65: tt_idx * 130 + h_in_g * 65 + 65]
            et(p, (lambda tt, o=po[0:65, 0:512], w=vsl, r=ptb.ap,
                   st=(ki == 0), sp=(ki == len(kts) - 1):
                   tt.matmul(o, w, r, start=st, stop=sp)),
               reads=(f.vx, ptb), writes=(po,), label=label + "_pv")
        ev(p, (lambda v, o=f.rl[0:1, :], i=po[64:65, 0:512]:
               v.reciprocal(o, i)),
           reads=(po,), writes=(f.rl,), label=label + "_rl")
        pb2 = c.psums[3]
        et(p, (lambda tt, o=pb2[0:64, 0:512], w=c.ones_row_f[0:1, 0:64],
               r=f.rl[0:1, 0:512]:
               tt.matmul(o, w, r, start=True, stop=True)),
           reads=(f.rl, c.ones_row_f), writes=(pb2,), label=label + "_rlbc")
        eact(p, f.ob[0:64, 0:512], po[0:65, 0:512][0:64, 0:512], AF.Copy,
             reads=(po,), writes=(f.ob,), label=label + "_oev")
        ev(p, (lambda v, o=att_out[atbase:atbase + 64, qc * 512:(qc + 1) * 512],
               a=f.ob[0:64, 0:512], bq=pb2[0:64, 0:512]:
               v.tensor_tensor(o, a, bq, ALU.mult)),
           reads=(f.ob, pb2), writes=(att_out,), label=label + "_onorm")


def emit_proj_group_T(p, c, W, dst, b, n0, nw=128, scale=None, func=AF.Copy,
                      src=None, label="projT"):
    src = src if src is not None else c.XN

    def rhs(kb, t0, tw):
        return src[:, kb * T + b * S + t0: kb * T + b * S + t0 + tw]

    def cb(pb, co, cw, t0, tw):
        eact(p, dst[co: co + cw, t0:t0 + tw], pb[0:cw, 0:tw], func, scale=scale,
             reads=(pb,), writes=(dst,), label=label + "_ev")

    matmul_T(p, W, rhs, cb, c.psums[0:2], K=W.K, N=nw, Tlen=S, n0=n0,
             reads=(src,), label=label)


def emit_proj_group_tok(p, c, W, f, b, n0, label="projV"):
    ev(p, lambda v: v.memset(f.vx.ap, 1.0), writes=(f.vx,), label=label + "_ones")
    KB = W.KB
    for tt_i in range(16):
        pb = c.psums[2 + (tt_i % 2)]
        for kb in range(KB):
            lhs = c.XN[:, kb * T + b * S + tt_i * 128: kb * T + b * S + (tt_i + 1) * 128]
            et(p, (lambda tt, o=pb[0:128, 0:128], w=lhs, r=W.tile(kb, n0, 128),
                   st=(kb == 0), sp=(kb == KB - 1):
                   tt.matmul(o, w, r, start=st, stop=sp)),
               reads=(c.XN, W.buf), writes=(pb,), label=label)
        for hh in range(2):
            eact(p, f.vx[:, tt_i * 130 + hh * 65: tt_i * 130 + hh * 65 + 64],
                 pb[0:128, hh * 64:(hh + 1) * 64], AF.Copy,
                 reads=(pb,), writes=(f.vx,), label=label + "_ev")


def emit_resid_from_att(p, c, Wo, f, b, g, label="wo"):
    for co in range(CT):
        for tch_i in range(S // 512):
            pb = c.psums[4 + (tch_i % 2)]
            et(p, (lambda tt, o=pb[0:128, 0:512], w=Wo.tile(g, co * 128, 128),
                   r=f.att[:, tch_i * 512:(tch_i + 1) * 512]:
                   tt.matmul(o, w, r, start=True, stop=True)),
               reads=(Wo.buf, f.att), writes=(pb,), label=label)
            sl = slice(co * T + b * S + tch_i * 512, co * T + b * S + (tch_i + 1) * 512)
            ev(p, (lambda v, o=c.XT[:, sl], x=c.XT[:, sl], a=pb[0:128, 0:512]:
                   v.tensor_tensor(o, x, a, ALU.add)),
               reads=(c.XT, pb), writes=(c.XT,), label=label + "_add")


def emit_l1_attn(p, c, f, wd, masks, label="l1a"):
    wq = [p.sb(f"l1wq{i}", [128, 8 * 128], BF16) for i in range(2)]
    wk = [p.sb(f"l1wk{i}", [128, 8 * 128], BF16) for i in range(2)]
    wv = [p.sb(f"l1wv{i}", [128, 8 * 128], BF16) for i in range(2)]
    wo = [p.sb(f"l1wo{i}", [128, D], BF16) for i in range(2)]
    for g in range(CT):
        i = g % 2
        Wq = load_weight_cols(p, wq[i], wd["q"], D, g * 128, 128, name=f"{label}q{g}")
        Wk = load_weight_cols(p, wk[i], wd["k"], D, g * 128, 128, name=f"{label}k{g}")
        Wv = load_weight_cols(p, wv[i], wd["v"], D, g * 128, 128, name=f"{label}v{g}")
        Wo = load_weight_rows(p, wo[i], wd["o"], g * 128, 128, D, name=f"{label}o{g}")
        for b in range(NB):
            emit_proj_group_T(p, c, Wq, f.qT, b, 0, scale=0.125, label=f"{label}q{b}{g}")
            emit_proj_group_T(p, c, Wk, f.kT, b, 0, label=f"{label}k{b}{g}")
            emit_proj_group_tok(p, c, Wv, f, b, 0, label=f"{label}v{b}{g}")

            def a_spec(qt):
                if qt == 0:
                    return [(0, 128, (masks["a0"], 0))]
                return [((qt - 1) * 128, 256, (masks["a1"], 0))]

            def c_spec(qc):
                out = []
                for j in range(-1, 4):
                    kt = qc * 4 + j
                    if kt < 0:
                        continue
                    out.append((kt * 128, (masks["cm"], 384 - 128 * j)))
                return out

            for hh in range(2):
                emit_flash_head(p, c, f, f.qT, f.kT, hh * 64, hh, 64,
                                f.att, hh * 64, a_spec, c_spec,
                                label=f"{label}h{b}{g}{hh}")
            emit_resid_from_att(p, c, Wo, f, b, 0, label=f"{label}o{b}{g}")


def emit_l1_ffn(p, c, wd, label="l1f"):
    Win = load_weight(p, label + "win", wd["win"], D, 64)
    Wout_b = p.sb(label + "wout", [128, D], BF16)
    Wout = load_weight_rows(p, Wout_b, wd["wout"], 0, 64, D, name=label + "wo")
    W1 = load_const_sb(p, label + "w1", wd["w1"], [64, 16], F32)
    B1 = load_const_sb(p, label + "b1", wd["b1"], [64, 16], F32)
    W2 = load_const_sb(p, label + "w2", wd["w2"], [64, 16], F32)
    B2 = load_const_sb(p, label + "b2", wd["b2"], [64, 1], F32)
    BIN = load_const_sb(p, label + "bin", wd["bin"], [64, 1], F32)
    BOUT = load_const_sb(p, label + "bout", wd["bout"], [128, CT], F32)
    TH = T // 2
    SS = p.sb(label + "ss", [128, TH], BF16)
    H = p.sb(label + "h", [128, TH], BF16)
    G = p.sb(label + "g", [128, TH], BF16)
    OB = p.sb(label + "ob", [128, 512], BF16)

    for half in range(2):
        toff = half * TH

        def rhs(kb, t0, tw, toff=toff):
            return c.XN[:, kb * T + toff + t0: kb * T + toff + t0 + tw]

        def cb(pb, co, cw, t0, tw):
            eact(p, SS[0:cw, t0:t0 + tw], pb[0:cw, 0:tw], AF.Identity, bias=BIN.ap,
                 reads=(pb, BIN), writes=(SS,), label=label + "_sev")

        matmul_T(p, Win, rhs, cb, c.psums[0:2], Tlen=TH, reads=(c.XN,),
                 label=label + "_s")
        eact(p, G[0:64, :], SS[0:64, :], AF.Identity, bias=B2.ap, scale=0.0,
             reads=(SS, B2), writes=(G,), label=label + "_g0")
        for i in range(16):
            eact(p, H[0:64, :], SS[0:64, :], AF.Relu, scale=W1[0:64, i:i + 1],
                 bias=B1[0:64, i:i + 1], reads=(SS, W1, B1), writes=(H,),
                 label=label + f"_h{i}")
            ev(p, (lambda v, o=G[0:64, :], hh=H[0:64, :], w2=W2[0:64, i:i + 1],
                   g=G[0:64, :]: v.scalar_tensor_tensor(o, hh, w2, g, ALU.mult, ALU.add)),
               reads=(H, W2, G), writes=(G,), label=label + f"_a{i}")
        for co in range(CT):
            for tch_i in range(TH // 512):
                pb = c.psums[2 + (tch_i % 2)]
                et(p, (lambda tt, o=pb[0:128, 0:512],
                       w=Wout.buf[0:64, co * 128:(co + 1) * 128],
                       r=G[0:64, tch_i * 512:(tch_i + 1) * 512]:
                       tt.matmul(o, w, r, start=True, stop=True)),
                   reads=(Wout.buf, G), writes=(pb,), label=label + "_om")
                eact(p, OB[0:128, 0:512], pb[0:128, 0:512], AF.Identity,
                     bias=BOUT[:, co:co + 1], reads=(pb, BOUT), writes=(OB,),
                     label=label + "_oev")
                sl = slice(co * T + toff + tch_i * 512, co * T + toff + (tch_i + 1) * 512)
                ev(p, (lambda v, o=c.XT[:, sl], x=c.XT[:, sl], a=OB[0:128, 0:512]:
                       v.tensor_tensor(o, x, a, ALU.add)),
                   reads=(c.XT, OB), writes=(c.XT,), label=label + "_oadd")


def l1_masks_host():
    out = {}
    m0 = np.zeros((128, 128), np.float32)
    for qf in range(128):
        for kp in range(128):
            d = qf - kp
            if not (0 <= d <= 63):
                m0[qf, kp] = -1e30
    out["a0"] = m0
    m1 = np.zeros((128, 256), np.float32)
    for qf in range(128):
        for kp in range(256):
            d = qf + 128 - kp
            if not (0 <= d <= 63):
                m1[qf, kp] = -1e30
    out["a1"] = m1
    M = np.zeros((128, 1024), np.float32)
    for kp in range(128):
        for vv in range(1024):
            d = (vv - 384) - kp
            if not (0 <= d <= 63):
                M[kp, vv] = -1e30
    out["cm"] = M
    return out


def emit_l3_attn(p, c, f, wd, masks, label="l3a"):
    CKV = p.sb(label + "ckv", [128, S], BF16)
    QROT = p.sb(label + "qrot", [128, 2 * S], BF16)
    KROT = p.sb(label + "krot", [128, 2 * S], BF16)
    QCAT = p.sb(label + "qcat", [128, S], BF16)
    KCAT = p.sb(label + "kcat", [128, S], BF16)
    wkvd = load_weight(p, label + "kvd", wd["kv_down"], D, 128)
    wq_sl = [p.sb(label + "wq0", [128, 8 * 128], BF16)] * 2
    wku_sl = [p.sb(label + "wku0", [128, 128], BF16)] * 2
    wvu_sl = [p.sb(label + "wvu0", [128, 128], BF16)] * 2
    wo_sl = [p.sb(label + "wo0", [128, D], BF16)] * 2
    wqr = load_weight(p, label + "wqr", wd["q_rope"], D, 32)
    wkr_b = p.sb(label + "wkr", [128, 32], BF16)
    wkr = load_weight_rows(p, wkr_b, wd["k_rope"], 0, 128, 32, name=label + "wkr")
    SC = 1.0 / float(np.sqrt(96.0))

    def rope_apply(src, dst, tabsign, bb, lab):
        cos = f.att[0:16, 0:S]
        sin = f.vx[0:16, 0:S]
        for ch in range(S // 512):
            u = slice(ch * 512, (ch + 1) * 512)
            s1 = QCAT[0:16, ch * 512:(ch + 1) * 512]
            s2 = KCAT[0:16, ch * 512:(ch + 1) * 512]
            d1 = dst[0:16, ch * 512:(ch + 1) * 512]
            d2 = dst[0:16, S + ch * 512: S + (ch + 1) * 512]
            co_ = cos[0:16, u] if False else f.att[0:16, u]
            si_ = f.vx[0:16, u]
            tm = f.ob[0:16, 0:512]
            ev(p, (lambda v, tm=tm, s2=s2, si_=si_: v.tensor_tensor(tm, s2, si_, ALU.mult)),
               reads=(QCAT, KCAT, f.vx), writes=(f.ob,), label=lab + "1")
            ev(p, (lambda v, d1=d1, s1=s1, co_=co_: v.tensor_tensor(d1, s1, co_, ALU.mult)),
               reads=(QCAT, KCAT, f.att), writes=(dst_b,), label=lab + "2")
            ev(p, (lambda v, d1=d1, tm=tm: v.tensor_tensor(d1, d1, tm, ALU.subtract)),
               reads=(dst_b, f.ob), writes=(dst_b,), label=lab + "3")
            ev(p, (lambda v, tm=tm, s1=s1, si_=si_: v.tensor_tensor(tm, s1, si_, ALU.mult)),
               reads=(QCAT, KCAT, f.vx), writes=(f.ob,), label=lab + "4")
            ev(p, (lambda v, d2=d2, s2=s2, co_=co_: v.tensor_tensor(d2, s2, co_, ALU.mult)),
               reads=(QCAT, KCAT, f.att), writes=(dst_b,), label=lab + "5")
            ev(p, (lambda v, d2=d2, tm=tm: v.tensor_tensor(d2, d2, tm, ALU.add)),
               reads=(dst_b, f.ob), writes=(dst_b,), label=lab + "6")

    def a_spec(qt):
        kend = (qt + 1) * 128
        out = []
        k0 = 0
        while k0 < kend:
            kw = min(512, kend - k0)
            if k0 + kw >= kend:
                d0 = qt * 128 - k0
                out.append((k0, kw, (masks["camA"], 384 - d0)))
            else:
                out.append((k0, kw, None))
            k0 += kw
        return out

    def c_spec(qc):
        out = []
        for kt in range(0, qc * 4 + 4):
            dd = kt - 4 * qc
            if dd >= 0:
                out.append((kt * 128, (masks["camC"], 384 - 128 * dd)))
            else:
                out.append((kt * 128, None))
        return out

    for b in range(NB):
        p.dma("gpsimd", f.att[0:16, 0:S], wd["rope_t"].ap()[0:16, 0:S],
              reads=(f.att,), writes=(f.att,), label=label + "_ldcos")
        p.dma("gpsimd", f.vx[0:16, 0:S], wd["rope_t"].ap()[0:16, S:2 * S],
              reads=(f.vx,), writes=(f.vx,), label=label + "_ldsin")
        def cb_ckv(pb, co, cw, t0, tw):
            eact(p, CKV[co:co + cw, t0:t0 + tw], pb[0:cw, 0:tw], AF.Copy,
                 reads=(pb,), writes=(CKV,), label=label + "_ckvev")

        matmul_T(p, wkvd, (lambda kb, t0, tw, b=b:
                           c.XN[:, kb * T + b * S + t0: kb * T + b * S + t0 + tw]),
                 cb_ckv, c.psums[0:2], Tlen=S, reads=(c.XN,), label=label + "_ckv")
        for hf in range(2):
            def cb_qr(pb, co, cw, t0, tw, hf=hf):
                dstb = QCAT if hf == 0 else KCAT
                eact(p, dstb[0:16, t0: t0 + tw], pb[0:16, 0:tw],
                     AF.Copy, scale=SC, reads=(pb,), writes=(dstb,),
                     label=label + "_qrev")
            matmul_T(p, wqr, (lambda kb, t0, tw, b=b:
                              c.XN[:, kb * T + b * S + t0: kb * T + b * S + t0 + tw]),
                     cb_qr, c.psums[2:4], N=16, n0=hf * 16, Tlen=S,
                     reads=(c.XN,), label=label + "_qr")
        dst_b = QROT
        rope_apply(None, QROT, +1, b, label + "_qrope")
        for hf in range(2):
            def cb_kr(pb, co, cw, t0, tw, hf=hf):
                dstb = QCAT if hf == 0 else KCAT
                eact(p, dstb[0:16, t0: t0 + tw], pb[0:16, 0:tw],
                     AF.Copy, reads=(pb,), writes=(dstb,), label=label + "_krev")
            matmul_T(p, wkr, (lambda kb, t0, tw:
                              CKV[:, t0: t0 + tw]),
                     cb_kr, c.psums[2:4], N=16, n0=hf * 16, Tlen=S,
                     reads=(CKV,), label=label + "_kr")
        dst_b = KROT
        rope_apply(None, KROT, +1, b, label + "_krope")
        ev(p, lambda v: v.memset(QCAT.ap, 0.0), writes=(QCAT,), label=label + "_qz")
        ev(p, lambda v: v.memset(KCAT.ap, 0.0), writes=(KCAT,), label=label + "_kz")
        ev(p, lambda v: v.tensor_copy(QCAT[0:16, :], QROT[0:16, 0:S]),
           reads=(QROT,), writes=(QCAT,), label=label + "_qc1")
        ev(p, lambda v: v.tensor_copy(QCAT[32:48, :], QROT[0:16, S:2 * S]),
           reads=(QROT,), writes=(QCAT,), label=label + "_qc2")
        ev(p, lambda v: v.tensor_copy(KCAT[0:16, :], KROT[0:16, 0:S]),
           reads=(KROT,), writes=(KCAT,), label=label + "_kc1")
        ev(p, lambda v: v.tensor_copy(KCAT[32:48, :], KROT[0:16, S:2 * S]),
           reads=(KROT,), writes=(KCAT,), label=label + "_kc2")
        for g in range(CT):
            i = g % 2
            Wq = load_weight_cols(p, wq_sl[i], wd["q"], D, g * 128, 128,
                                  name=label + f"q{g}")
            Wku = load_weight_cols(p, wku_sl[i], wd["k_up"], 128, g * 128, 128,
                                   name=label + f"ku{g}")
            Wvu = load_weight_cols(p, wvu_sl[i], wd["v_up"], 128, g * 128, 128,
                                   name=label + f"vu{g}")
            Wo = load_weight_rows(p, wo_sl[i], wd["o"], g * 128, 128, D,
                                  name=label + f"o{g}")
            emit_proj_group_T(p, c, Wq, f.qT, b, 0, scale=SC, label=f"{label}q{b}{g}")
            matmul_T(p, Wku, (lambda kb, t0, tw: CKV[:, t0:t0 + tw]),
                     (lambda pb, co, cw, t0, tw:
                      eact(p, f.kT[co:co + cw, t0:t0 + tw], pb[0:cw, 0:tw], AF.Copy,
                           reads=(pb,), writes=(f.kT,), label=label + "_kev")),
                     c.psums[0:2], Tlen=S, reads=(CKV,), label=label + "_k")
            ev(p, lambda v: v.memset(f.vx.ap, 1.0), writes=(f.vx,), label=label + "_vo")
            for tt_i in range(16):
                pb = c.psums[2 + (tt_i % 2)]
                et(p, (lambda tt, o=pb[0:128, 0:128],
                       w=CKV[:, tt_i * 128:(tt_i + 1) * 128],
                       r=Wvu.tile(0, 0, 128):
                       tt.matmul(o, w, r, start=True, stop=True)),
                   reads=(CKV, Wvu.buf), writes=(pb,), label=label + "_v")
                for hh in range(2):
                    eact(p, f.vx[:, tt_i * 130 + hh * 65: tt_i * 130 + hh * 65 + 64],
                         pb[0:128, hh * 64:(hh + 1) * 64], AF.Copy,
                         reads=(pb,), writes=(f.vx,), label=label + "_vev")
            for hh in range(2):
                ev(p, (lambda v, o=QCAT[64:128, :], i_=f.qT[hh * 64:hh * 64 + 64, :]:
                       v.tensor_copy(o, i_)),
                   reads=(f.qT,), writes=(QCAT,), label=label + "_qh")
                ev(p, (lambda v, o=KCAT[64:128, :], i_=f.kT[hh * 64:hh * 64 + 64, :]:
                       v.tensor_copy(o, i_)),
                   reads=(f.kT,), writes=(KCAT,), label=label + "_kh")
                emit_flash_head(p, c, f, QCAT, KCAT, 0, hh, 128,
                                f.att, hh * 64, a_spec, c_spec,
                                label=f"{label}f{b}{g}{hh}")
            emit_resid_from_att(p, c, Wo, f, b, 0, label=f"{label}r{b}{g}")


def l3_masks_host():
    camA = np.zeros((128, 896), np.float32)
    for qf in range(128):
        for v in range(896):
            if (v - 384) > qf:
                camA[qf, v] = -1e30
    camC = np.zeros((128, 896), np.float32)
    for kp in range(128):
        for v in range(896):
            if kp > (v - 384):
                camC[kp, v] = -1e30
    return {"camA": camA, "camC": camC}


def l3_rope_host():
    d_R = 32
    inv = 1.0 / 10000.0 ** (np.arange(0, d_R, 2) / d_R)
    t = np.arange(S)
    fr = np.outer(t, inv)
    out = np.zeros((128, 2 * S), np.float32)
    out[0:16, 0:S] = np.cos(fr).T
    out[0:16, S:2 * S] = np.sin(fr).T
    return out


def emit_phi(p, c, buf, sl, tmp, label):
    ev(p, (lambda v, t=tmp, x=buf[sl]: v.tensor_scalar_min(t, x, 0.0)),
       reads=(buf,), writes=(c.sqc,), label=label + "m")
    eact(p, tmp, tmp, AF.Exp, reads=(c.sqc,), writes=(c.sqc,), label=label + "e")
    ev(p, (lambda v, o=buf[sl], x=buf[sl], t=tmp:
           v.scalar_tensor_tensor(o, x, 1.0, t, ALU.add, ALU.max)),
       reads=(buf, c.sqc), writes=(buf,), label=label + "x")


def emit_l2_attn(p, c, f, wd, label="l2a"):
    wq_sl = p.sb(label + "wq", [128, 8 * 128], BF16)
    wk_sl = p.sb(label + "wk", [128, 8 * 128], BF16)
    wv_sl = p.sb(label + "wv", [128, 8 * 128], BF16)
    wo_sl = p.sb(label + "wo", [128, D], BF16)
    KVS = p.sb(label + "kvs", [128, 66], BF16)
    BQ = load_const_sb(p, label + "bq", wd["b_q"], [128, CT], F32)
    BK = load_const_sb(p, label + "bk", wd["b_k"], [128, CT], F32)
    BV = load_const_sb(p, label + "bv", wd["b_v"], [128, CT], F32)
    BO = load_const_sb(p, label + "bo", wd["b_o"], [128, CT], F32)
    phit = c.sqc[:, 0:512]
    for g in range(CT):
        load_weight_cols(p, wq_sl, wd["q"], D, g * 128, 128, name=label + f"q{g}")
        load_weight_cols(p, wk_sl, wd["k"], D, g * 128, 128, name=label + f"k{g}")
        load_weight_cols(p, wv_sl, wd["v"], D, g * 128, 128, name=label + f"v{g}")
        Wo = load_weight_rows(p, wo_sl, wd["o"], g * 128, 128, D, name=label + f"o{g}")
        Wq = Weights(wq_sl, D, 128)
        for b in range(NB):
            def cb_q(pb, co, cw, t0, tw):
                eact(p, f.qT[0:cw, t0:t0 + tw], pb[0:cw, 0:tw], AF.Identity,
                     bias=BQ[:, g:g + 1], reads=(pb, BQ), writes=(f.qT,),
                     label=label + "_qev")
            matmul_T(p, Wq, (lambda kb, t0, tw, b=b:
                             c.XN[:, kb * T + b * S + t0: kb * T + b * S + t0 + tw]),
                     cb_q, c.psums[0:2], Tlen=S, reads=(c.XN,), label=label + "_q")
            for ch in range(S // 512):
                emit_phi(p, c, f.qT, (slice(0, 128), slice(ch * 512, (ch + 1) * 512)),
                         phit, label + f"_phq{ch}")
            for which, wsl, bias, dstb in (("k", wk_sl, BK, f.att), ("v", wv_sl, BV, f.vx)):
                W_ = Weights(wsl, D, 128)
                for tt_i in range(16):
                    pb = c.psums[2 + (tt_i % 2)]
                    for kb in range(CT):
                        lhs = c.XN[:, kb * T + b * S + tt_i * 128:
                                   kb * T + b * S + (tt_i + 1) * 128]
                        et(p, (lambda tt, o=pb[0:128, 0:128], w=lhs,
                               r=W_.tile(kb, 0, 128),
                               st=(kb == 0), sp=(kb == CT - 1):
                               tt.matmul(o, w, r, start=st, stop=sp)),
                           reads=(c.XN, W_.buf), writes=(pb,), label=label + "_" + which)
                    eact(p, dstb[:, tt_i * 128:(tt_i + 1) * 128], pb[0:128, 0:128],
                         AF.Identity, bias=bias[:, g:g + 1],
                         reads=(pb, bias), writes=(dstb,), label=label + "_" + which + "ev")
                if which == "k":
                    for ch in range(4):
                        emit_phi(p, c, f.att,
                                 (slice(0, 128), slice(ch * 512, (ch + 1) * 512)),
                                 phit, label + f"_phk{ch}")
            for hh in range(2):
                pkv = c.psums[4]
                for tt_i in range(16):
                    et(p, (lambda tt, o=pkv[0:64, 0:64],
                           w=f.att[:, tt_i * 128 + hh * 64: tt_i * 128 + hh * 64 + 64],
                           r=f.vx[:, tt_i * 128 + hh * 64: tt_i * 128 + hh * 64 + 64],
                           st=(tt_i == 0), sp=False:
                           tt.matmul(o, w, r, start=st, stop=sp)),
                       reads=(f.att, f.vx), writes=(pkv,), label=label + "_kv")
                    et(p, (lambda tt, o=pkv[0:64, 64:65],
                           w=f.att[:, tt_i * 128 + hh * 64: tt_i * 128 + hh * 64 + 64],
                           r=c.ones_col_bf.ap,
                           st=False, sp=(tt_i == 15):
                           tt.matmul(o, w, r, start=st, stop=sp)),
                       reads=(f.att, c.ones_col_bf), writes=(pkv,), label=label + "_ks")
                eact(p, KVS[hh * 64:hh * 64 + 64, 0:65], pkv[0:64, 0:65], AF.Copy,
                     reads=(pkv,), writes=(KVS,), label=label + "_kvev")
            OC = f.pt[0]
            R0 = c.R[0]
            for ch in range(S // 512):
                for hh in range(2):
                    pr = c.prow[hh]
                    et(p, (lambda tt, o=pr[0:1, 0:512],
                           w=KVS[hh * 64:hh * 64 + 64, 64:65],
                           r=f.qT[hh * 64:hh * 64 + 64, ch * 512:(ch + 1) * 512]:
                           tt.matmul(o, w, r, start=True, stop=True)),
                       reads=(KVS, f.qT), writes=(pr,), label=label + "_dn")
                    eact(p, R0[0:1, 0:512], pr[0:1, 0:512], AF.Identity, bias=1e-6,
                         reads=(pr,), writes=(R0,), label=label + "_dne")
                    ev(p, (lambda v, o=R0[0:1, 0:512], i=R0[0:1, 0:512]:
                           v.reciprocal(o, i)),
                       reads=(R0,), writes=(R0,), label=label + "_dnr")
                    pb2 = c.psums[5]
                    et(p, (lambda tt, o=pb2[0:128, 0:512], w=c.ones_row_f.ap,
                           r=R0[0:1, 0:512]:
                           tt.matmul(o, w, r, start=True, stop=True)),
                       reads=(R0, c.ones_row_f), writes=(pb2,), label=label + "_dnb")
                    po = c.psums[2 + hh]
                    et(p, (lambda tt, o=po[0:64, 0:512],
                           w=KVS[hh * 64:hh * 64 + 64, 0:64],
                           r=f.qT[hh * 64:hh * 64 + 64, ch * 512:(ch + 1) * 512]:
                           tt.matmul(o, w, r, start=True, stop=True)),
                       reads=(KVS, f.qT), writes=(po,), label=label + "_o")
                    eact(p, f.ob[0:64, 0:512], po[0:64, 0:512], AF.Copy,
                         reads=(po,), writes=(f.ob,), label=label + "_oev")
                    ev(p, (lambda v, o=OC[hh * 64:hh * 64 + 64, 0:512],
                           a=f.ob[0:64, 0:512], bq=pb2[0:64, 0:512]:
                           v.tensor_tensor(o, a, bq, ALU.mult)),
                       reads=(f.ob, pb2), writes=(OC,), label=label + "_on")
                for co in range(CT):
                    pb = c.psums[0 + (co % 2)]
                    et(p, (lambda tt, o=pb[0:128, 0:512],
                           w=Wo.buf[:, co * 128:(co + 1) * 128], r=OC[:, 0:512]:
                           tt.matmul(o, w, r, start=True, stop=True)),
                       reads=(Wo.buf, OC), writes=(pb,), label=label + "_wo")
                    sl = slice(co * T + b * S + ch * 512, co * T + b * S + (ch + 1) * 512)
                    ev(p, (lambda v, o=c.XT[:, sl], x=c.XT[:, sl], a=pb[0:128, 0:512]:
                           v.tensor_tensor(o, x, a, ALU.add)),
                       reads=(c.XT, pb), writes=(c.XT,), label=label + "_woadd")
    for ct in range(CT):
        ev(p, (lambda v, o=c.XT[:, ct * T:(ct + 1) * T], x=c.XT[:, ct * T:(ct + 1) * T],
               bo=BO[:, ct:ct + 1]:
               v.tensor_scalar_add(o, x, bo)),
           reads=(c.XT, BO), writes=(c.XT,), label=label + "_bo")


def emit_l3_ffn(p, c, wd, label="l3f"):
    CW = p.sb(label + "cw", [128, 32 * 8], F32)
    CWR = p.sb(label + "cwr", [128, T], BF16)
    CWREP = p.sb(label + "cwrep", [128, T], BF16)
    SCB = p.sb(label + "scb", [128, 8], F32)
    RB = p.sb(label + "rb", [128, 8], F32)
    GS = p.sb(label + "gs", [128, 4 * 512], BF16)
    US = p.sb(label + "us", [128, 4 * 512], BF16)
    CVB = p.sb(label + "cvb", [128, S], BF16)
    identf = p.sb(label + "idf", [128, 128], F32)
    wr = load_weight(p, label + "wr", wd["Wr"], D, 8)
    wg_b = p.sb(label + "wg", [128, 8 * 512], BF16)
    wu_b = p.sb(label + "wu", [128, 8 * 512], BF16)
    wdn_b = p.sb(label + "wd", [128, 4 * 1024], BF16)
    CWC = load_const_sb(p, label + "cwc", wd["conv_w"], [128, 4 * CT], F32)
    CBC = load_const_sb(p, label + "cbc", wd["conv_b"], [128, CT], F32)
    rbr = nc_rb = wd["router_bias"]
    ev(p, lambda v: v.tensor_copy(identf.ap, c.ident_bf.ap),
       reads=(c.ident_bf,), writes=(identf,), label=label + "_idf")
    pb = c.psums[5]
    rbb = load_const_sb(p, label + "rbb", wd["router_bias"], [1, 8], F32)
    et(p, (lambda tt, o=pb[0:128, 0:8], w=c.ones_row_f.ap, r=rbb[0:1, 0:8]:
           tt.matmul(o, w, r, start=True, stop=True)),
       reads=(rbb, c.ones_row_f), writes=(pb,), label=label + "_rbb")
    eact(p, RB.ap, pb[0:128, 0:8], AF.Copy, reads=(pb,), writes=(RB,),
         label=label + "_rbe")
    for tt_i in range(32):
        pb = c.psums[0 + (tt_i % 2)]
        for kb in range(CT):
            lhs = c.XN[:, kb * T + tt_i * 128: kb * T + (tt_i + 1) * 128]
            et(p, (lambda tt, o=pb[0:128, 0:8], w=lhs, r=wr.tile(kb, 0, 8),
                   st=(kb == 0), sp=(kb == CT - 1):
                   tt.matmul(o, w, r, start=st, stop=sp)),
               reads=(c.XN, wr.buf), writes=(pb,), label=label + "_sc")
        s_ = CW[:, tt_i * 8:(tt_i + 1) * 8]
        eact(p, s_, pb[0:128, 0:8], AF.Sigmoid, reads=(pb,), writes=(CW,),
             label=label + "_sce")
        r_ = SCB[:, 0:8]
        ev(p, (lambda v, o=r_, a=s_, bq=RB.ap: v.tensor_tensor(o, a, bq, ALU.add)),
           reads=(CW, RB), writes=(SCB,), label=label + "_r")
        m1 = c.R[0][:, 0:1]
        ev(p, (lambda v, o=m1, i=r_: v.reduce_max(o, i, AX.X)),
           reads=(SCB,), writes=(c.R[0],), label=label + "_m1")
        mk = c.R[1][:, 0:8]
        ev2(p, (lambda v, o=mk, i=r_, s=m1: v.tensor_scalar(o, i, s, None, ALU.is_ge)),
            reads=(SCB,), sreads=(c.R[0],), writes=(c.R[1],), label=label + "_k1")
        ev(p, (lambda v, o=r_, i=mk, r2=r_:
               v.scalar_tensor_tensor(o, i, -1e30, r2, ALU.mult, ALU.add)),
           reads=(c.R[1], SCB), writes=(SCB,), label=label + "_r2")
        ev(p, (lambda v, o=m1, i=r_: v.reduce_max(o, i, AX.X)),
           reads=(SCB,), writes=(c.R[0],), label=label + "_m2")
        mk2 = c.R[2][:, 0:8]
        ev2(p, (lambda v, o=mk2, i=r_, s=m1: v.tensor_scalar(o, i, s, None, ALU.is_ge)),
            reads=(SCB,), sreads=(c.R[0],), writes=(c.R[2],), label=label + "_k2")
        ev(p, (lambda v, o=mk, a=mk, bq=mk2: v.tensor_tensor(o, a, bq, ALU.add)),
           reads=(c.R[1], c.R[2]), writes=(c.R[1],), label=label + "_ms")
        ev(p, (lambda v, o=s_ if False else SCB[:, 0:8], a=s_, bq=mk:
               v.tensor_tensor(o, a, bq, ALU.mult)),
           reads=(CW, c.R[1]), writes=(SCB,), label=label + "_ws")
        ev(p, (lambda v, o=m1, i=SCB[:, 0:8]: v.reduce_sum(o, i, AX.X)),
           reads=(SCB,), writes=(c.R[0],), label=label + "_dn")
        ev(p, (lambda v, o=m1, i=m1: v.tensor_scalar_add(o, i, 1e-9)),
           reads=(c.R[0],), writes=(c.R[0],), label=label + "_dne")
        ev(p, (lambda v, o=m1, i=m1: v.reciprocal(o, i)),
           reads=(c.R[0],), writes=(c.R[0],), label=label + "_dnr")
        ev2(p, (lambda v, o=s_, i=SCB[:, 0:8], s=m1:
                v.tensor_scalar_mul(o, i, s)),
            reads=(SCB,), sreads=(c.R[0],), writes=(CW,), label=label + "_cw")
    for e in range(4):
        Wg = load_weight(p, label + f"wg{e}", wd[f"Wg{e}"], D, 512, buf=wg_b)
        Wu = load_weight(p, label + f"wu{e}", wd[f"Wu{e}"], D, 512, buf=wu_b)
        Wd = load_weight(p, label + f"wd{e}", wd[f"Wd{e}"], 512, 1024, buf=wdn_b)
        for g0 in range(0, 32, 4):
            pr = c.prow[0]
            for j in range(4):
                tt_i = g0 + j
                et(p, (lambda tt, o=pr[0:1, j * 128:(j + 1) * 128],
                       w=CW[:, tt_i * 8 + e: tt_i * 8 + e + 1], r=identf.ap,
                       st=(j == 0), sp=(j == 3):
                       tt.matmul(o, w, r, start=st, stop=sp)),
                   reads=(CW, identf), writes=(pr,), label=label + "_cwr")
            eact(p, CWR[0:1, g0 * 128:(g0 + 4) * 128], pr[0:1, 0:512], AF.Copy,
                 reads=(pr,), writes=(CWR,), label=label + "_cwre")
        for ch in range(T // 512):
            pb = c.psums[5]
            et(p, (lambda tt, o=pb[0:128, 0:512], w=c.ones_row_bf.ap,
                   r=CWR[0:1, ch * 512:(ch + 1) * 512]:
                   tt.matmul(o, w, r, start=True, stop=True)),
               reads=(CWR, c.ones_row_bf), writes=(pb,), label=label + "_cwb")
            eact(p, CWREP[:, ch * 512:(ch + 1) * 512], pb[0:128, 0:512], AF.Copy,
                 reads=(pb,), writes=(CWREP,), label=label + "_cwbe")
        for tch in range(T // 512):
            t0 = tch * 512
            def rhs(kb, tq0, tw, t0=t0):
                return c.XN[:, kb * T + t0 + tq0: kb * T + t0 + tq0 + tw]
            def cb_g(pb_, co, cw_, tq0, tw):
                eact(p, GS[:, (co // 128) * 512 + tq0:(co // 128) * 512 + tq0 + tw],
                     pb_[0:128, 0:tw], AF.Silu, reads=(pb_,), writes=(GS,),
                     label=label + "_ge")
            matmul_T(p, Wg, rhs, cb_g, c.psums[0:2], Tlen=512, reads=(c.XN,),
                     label=label + "_g")
            def cb_u(pb_, co, cw_, tq0, tw, t0=t0):
                dst = US[:, (co // 128) * 512 + tq0:(co // 128) * 512 + tq0 + tw]
                ev(p, (lambda v, o=dst, a=pb_[0:128, 0:tw],
                       bq=CWREP[:, t0 + tq0: t0 + tq0 + tw]:
                       v.tensor_tensor(o, a, bq, ALU.mult)),
                   reads=(pb_, CWREP), writes=(US,), label=label + "_ue")
            matmul_T(p, Wu, rhs, cb_u, c.psums[2:4], Tlen=512, reads=(c.XN,),
                     label=label + "_u")
            ev(p, (lambda v, o=GS.ap, a=GS.ap, bq=US.ap:
                   v.tensor_tensor(o, a, bq, ALU.mult)),
               reads=(GS, US), writes=(GS,), label=label + "_h")
            for co in range(CT):
                pb_ = c.psums[4 + (co % 2)]
                for kb in range(4):
                    et(p, (lambda tt, o=pb_[0:128, 0:512],
                           w=Wd.tile(kb, co * 128, 128),
                           r=GS[:, kb * 512:(kb + 1) * 512],
                           st=(kb == 0), sp=(kb == 3):
                           tt.matmul(o, w, r, start=st, stop=sp)),
                       reads=(Wd.buf, GS), writes=(pb_,), label=label + "_d")
                sl = slice(co * T + t0, co * T + t0 + 512)
                ev(p, (lambda v, o=c.XT[:, sl], x=c.XT[:, sl], a=pb_[0:128, 0:512]:
                       v.tensor_tensor(o, x, a, ALU.add)),
                   reads=(c.XT, pb_), writes=(c.XT,), label=label + "_dadd")
    for which, exps in (("46", (4, 6)), ("7", (7,))):
        for g0 in range(0, 32, 4):
            pr = c.prow[0]
            first = True
            for e in exps:
                for j in range(4):
                    tt_i = g0 + j
                    et(p, (lambda tt, o=pr[0:1, j * 128:(j + 1) * 128],
                           w=CW[:, tt_i * 8 + e: tt_i * 8 + e + 1], r=identf.ap,
                           st=(first and j == 0), sp=(e == exps[-1] and j == 3):
                           tt.matmul(o, w, r, start=st, stop=sp)),
                       reads=(CW, identf), writes=(pr,), label=label + "_shr")
                first = False
            eact(p, CWR[0:1, g0 * 128:(g0 + 4) * 128], pr[0:1, 0:512], AF.Copy,
                 reads=(pr,), writes=(CWR,), label=label + "_shre")
        for ch in range(T // 512):
            pb = c.psums[5]
            et(p, (lambda tt, o=pb[0:128, 0:512], w=c.ones_row_bf.ap,
                   r=CWR[0:1, ch * 512:(ch + 1) * 512]:
                   tt.matmul(o, w, r, start=True, stop=True)),
               reads=(CWR, c.ones_row_bf), writes=(pb,), label=label + "_shb")
            eact(p, CWREP[:, ch * 512:(ch + 1) * 512], pb[0:128, 0:512], AF.Copy,
                 reads=(pb,), writes=(CWREP,), label=label + "_shbe")
        for ct in range(CT):
            for ch in range(T // 512):
                sl = slice(ct * T + ch * 512, ct * T + (ch + 1) * 512)
                if which == "46":
                    ev(p, (lambda v, o=c.sqc[:, 0:512], a=c.XN[:, sl],
                           bq=CWREP[:, ch * 512:(ch + 1) * 512]:
                           v.tensor_tensor(o, a, bq, ALU.mult)),
                       reads=(c.XN, CWREP), writes=(c.sqc,), label=label + "_t46")
                else:
                    ev(p, (lambda v, o=c.sqc[:, 0:512], a=c.XN[:, sl]:
                           v.tensor_relu(o, a)),
                       reads=(c.XN,), writes=(c.sqc,), label=label + "_rl")
                    ev(p, (lambda v, o=c.sqc[:, 0:512], a=c.sqc[:, 0:512],
                           bq=CWREP[:, ch * 512:(ch + 1) * 512]:
                           v.tensor_tensor(o, a, bq, ALU.mult)),
                       reads=(c.sqc, CWREP), writes=(c.sqc,), label=label + "_t7")
                ev(p, (lambda v, o=c.XT[:, sl], x=c.XT[:, sl], a=c.sqc[:, 0:512]:
                       v.tensor_tensor(o, x, a, ALU.add)),
                   reads=(c.XT, c.sqc), writes=(c.XT,), label=label + "_sha")
    for ct in range(CT):
        for b in range(NB):
            base = ct * T + b * S
            eact(p, CVB[:, 0:S], c.XN[:, base:base + S], AF.Identity,
                 scale=CWC[:, 3 * CT + ct: 3 * CT + ct + 1],
                 bias=CBC[:, ct:ct + 1],
                 reads=(c.XN, CWC, CBC), writes=(CVB,), label=label + "_cv3")
            for k in range(3):
                off = 3 - k
                ev(p, (lambda v, o=CVB[:, off:S], a=c.XN[:, base:base + S - off],
                       w=CWC[:, k * CT + ct: k * CT + ct + 1], g=CVB[:, off:S]:
                       v.scalar_tensor_tensor(o, a, w, g, ALU.mult, ALU.add)),
                   reads=(c.XN, CWC, CVB), writes=(CVB,), label=label + f"_cv{k}")
            ev(p, (lambda v, o=c.XT[:, base:base + S], x=c.XT[:, base:base + S],
                   a=CVB[:, 0:S]:
                   v.tensor_tensor(o, x, a, ALU.add)),
               reads=(c.XT, CVB), writes=(c.XT,), label=label + "_cva")


def emit_l0_mix(p, c, wd, s1rep, label="l0m", chw=512, stream_w=False, nc=None,
                s1_dram=None):
    names = ("Wx", "Wy", "Wi", "Wa", "Wo")
    N1 = load_const_sb(p, label + "n1", wd["norm1_pcol"], [128, CT], F32)
    if stream_w:
        wdram = {}
        pre = p.sb(label + "pre", [128, CT * D], BF16)
        for nm in names:
            wdram[nm] = Buf(nc.dram_tensor(label + "_s" + nm, [D, D], BF16,
                                           kind="Internal"), nm)
            if nm in ("Wx", "Wy"):
                ap_in = wd[nm].ap().rearrange("(kb q) n -> q kb n", q=128)
                ap_out = pre.ap.rearrange("q (kb n) -> q kb n", kb=CT)
                p.dma("gpsimd", ap_out, ap_in, reads=(pre,), writes=(pre,),
                      label=label + "_pl" + nm)
                for kb in range(CT):
                    ev2(p, (lambda v, o=pre[:, kb * D:(kb + 1) * D],
                            i=pre[:, kb * D:(kb + 1) * D], s=N1[:, kb:kb + 1]:
                            v.tensor_scalar_mul(o, i, s)),
                        reads=(pre,), sreads=(N1,), writes=(pre,),
                        label=label + "_pf")
                ap_d = wdram[nm].t.ap().rearrange("(kb q) n -> q kb n", q=128)
                p.dma("gpsimd", ap_d, ap_out, reads=(pre,), writes=(wdram[nm],),
                      label=label + "_ps" + nm)
            else:
                p.dma("gpsimd", wdram[nm].t.ap(), wd[nm].ap(),
                      writes=(wdram[nm],), label=label + "_pre" + nm)
        wslot = [p.sb(label + f"wsl{i}", [128, CT * 512], BF16) for i in range(2)]
        Wres = None
    else:
        Wres = {nm: load_weight(p, label + nm.lower(), wd[nm], D, D)
                for nm in names}
        for W_ in (Wres["Wx"], Wres["Wy"]):
            for kb in range(CT):
                ev2(p, (lambda v, o=W_.buf[:, kb * D:(kb + 1) * D],
                        i=W_.buf[:, kb * D:(kb + 1) * D], s=N1[:, kb:kb + 1]:
                        v.tensor_scalar_mul(o, i, s)),
                    reads=(W_.buf,), sreads=(N1,), writes=(W_.buf,), label=label + "_n1f")
    CWC = load_const_sb(p, label + "cwc", wd["conv_w_pcol"], [128, 4 * CT], F32)
    CBC = load_const_sb(p, label + "cbc", wd["conv_b_pcol"], [128, CT], F32)
    LAM = load_const_sb(p, label + "lam", wd["lam_pcol"], [128, CT], F32)
    LAMS = p.sb(label + "lams", [128, CT], F32)
    T1 = c.R[0][:, 0:CT]
    T2 = c.R[1][:, 0:CT]
    eact(p, T1, LAM.ap, AF.Abs, reads=(LAM,), writes=(c.R[0],),
         label=label + "_sp1")
    eact(p, T1, T1, AF.Exp, scale=-1.0, reads=(c.R[0],), writes=(c.R[0],),
         label=label + "_sp2")
    eact(p, T1, T1, AF.Ln, bias=1.0, reads=(c.R[0],), writes=(c.R[0],),
         label=label + "_sp3")
    ev(p, lambda v: v.tensor_relu(T2, LAM.ap), reads=(LAM,), writes=(c.R[1],),
       label=label + "_sp4")
    ev(p, lambda v: v.tensor_tensor(T1, T1, T2, ALU.add),
       reads=(c.R[0], c.R[1]), writes=(c.R[0],), label=label + "_sp5")
    ev(p, lambda v: v.tensor_scalar_mul(LAMS.ap, T1, -8.0),
       reads=(c.R[0],), writes=(LAMS,), label=label + "_sp6")

    UC = p.sb(label + "uc", [128, CT * chw], BF16)
    XB = p.sb(label + "xb", [128, CT * (chw + 4)], BF16)
    CV = p.sb(label + "cv", [128, CT * chw], BF16)
    RS = p.sb(label + "rs", [128, CT * chw], BF16)
    IS = p.sb(label + "is", [128, CT * chw], BF16)
    YB = p.sb(label + "yb", [128, CT * chw], BF16)
    HS = CV
    TMP = p.sb(label + "tmp", [128, chw], BF16)
    CAR = p.sb(label + "car", [128, CT], F32)
    S1C = p.sb(label + "s1c", [128, chw], BF16) if s1_dram is not None else None
    XBW = chw + 4
    nch = S // chw
    widx = [0]

    def mm_block(nm, rhsbuf, rhs_block, evac, Wfull):
        nsub = (chw + 511) // 512
        if stream_w:
            parts = []
            for h in range(2):
                sl = wslot[widx[0] % 2]
                widx[0] += 1
                ap_in = wdram[nm].t.ap()[:, h * 512:(h + 1) * 512].rearrange(
                    "(kb q) n -> q kb n", q=128)
                ap_out = sl.ap.rearrange("q (kb n) -> q kb n", kb=CT)
                p.dma("sync", ap_out, ap_in, reads=(wdram[nm],), writes=(sl,),
                      label=label + "_ldw" + nm)
                parts.append((Weights(sl, D, 512, stride=512), h * 4, 4))
        else:
            parts = [(Wfull, 0, CT)]
        for Wm, co0, nco in parts:
            for col in range(nco):
                co = co0 + col
                for si in range(nsub):
                    w0 = si * 512
                    ww = min(512, chw - w0)
                    pb = c.psums[(co * nsub + si) % 2]
                    for kb in range(CT):
                        et(p, (lambda tt, o=pb[0:128, 0:ww],
                               w=Wm.tile(kb, col * 128, 128),
                               r=rhs_block(kb, w0, ww), st=(kb == 0),
                               sp=(kb == CT - 1):
                               tt.matmul(o, w, r, start=st, stop=sp)),
                           reads=(Wm.buf, rhsbuf), writes=(pb,), label=label + "_mm")
                    evac(pb, co, w0, ww)

    for b in range(NB):
        ev(p, lambda v: v.memset(CAR.ap, 0.0), writes=(CAR,), label=label + "_car0")
        for ct in range(CT):
            ev(p, (lambda v, o=XB[:, ct * XBW:ct * XBW + 3]:
                   v.memset(o, 0.0)),
               writes=(XB,), label=label + "_hal0")
        for ch in range(nch):
            t0 = b * S + ch * chw
            if s1_dram is not None:
                p.dma("gpsimd", S1C.ap, s1_dram.ap()[:, t0:t0 + chw],
                      reads=(S1C,), writes=(S1C,), label=label + "_lds1")
                s1ap = lambda: S1C[:, 0:chw]
            else:
                s1ap = lambda: s1rep[:, t0:t0 + chw]
            s1b = S1C if s1_dram is not None else s1rep
            for ct in range(CT):
                ev(p, (lambda v, o=UC[:, ct * chw:(ct + 1) * chw],
                       a=c.XN[:, ct * T + t0: ct * T + t0 + chw],
                       s=s1ap():
                       v.tensor_tensor(o, a, s, ALU.mult)),
                   reads=(c.XN, s1b), writes=(UC,), label=label + "_uc")
            if ch > 0:
                for ct in range(CT):
                    ev(p, (lambda v, o=XB[:, ct * XBW:ct * XBW + 3],
                           i=XB[:, ct * XBW + chw:ct * XBW + chw + 3]:
                           v.tensor_copy(o, i)),
                       reads=(XB,), writes=(XB,), label=label + "_halc")
            mm_block("Wx", UC, lambda kb, w0, ww: UC[:, kb * chw + w0:kb * chw + w0 + ww],
                     lambda pb, co, w0, ww: eact(p, XB[:, co * XBW + 3 + w0:co * XBW + 3 + w0 + ww],
                                                 pb[0:128, 0:ww], AF.Copy,
                                                 reads=(pb,), writes=(XB,),
                                                 label=label + "_xbev"),
                     Wres["Wx"] if Wres else None)
            mm_block("Wy", UC, lambda kb, w0, ww: UC[:, kb * chw + w0:kb * chw + w0 + ww],
                     lambda pb, co, w0, ww: eact(p, YB[:, co * chw + w0:co * chw + w0 + ww],
                                                 pb[0:128, 0:ww], AF.Gelu_apprx_tanh,
                                                 reads=(pb,), writes=(YB,),
                                                 label=label + "_yev"),
                     Wres["Wy"] if Wres else None)
            for ct in range(CT):
                eact(p, CV[:, ct * chw:(ct + 1) * chw],
                     XB[:, ct * XBW + 3:ct * XBW + 3 + chw], AF.Identity,
                     scale=CWC[:, 3 * CT + ct:3 * CT + ct + 1],
                     bias=CBC[:, ct:ct + 1],
                     reads=(XB, CWC, CBC), writes=(CV,), label=label + "_cv3")
                for k in range(3):
                    ev2(p, (lambda v, o=CV[:, ct * chw:(ct + 1) * chw],
                            a=XB[:, ct * XBW + k:ct * XBW + k + chw],
                            w=CWC[:, k * CT + ct:k * CT + ct + 1],
                            g=CV[:, ct * chw:(ct + 1) * chw]:
                            v.scalar_tensor_tensor(o, a, w, g, ALU.mult, ALU.add)),
                        reads=(XB, CV), sreads=(CWC,), writes=(CV,),
                        label=label + f"_cv{k}")
            mm_block("Wa", CV, lambda kb, w0, ww: CV[:, kb * chw + w0:kb * chw + w0 + ww],
                     lambda pb, co, w0, ww: eact(p, RS[:, co * chw + w0:co * chw + w0 + ww],
                                                 pb[0:128, 0:ww], AF.Sigmoid,
                                                 reads=(pb,), writes=(RS,),
                                                 label=label + "_rev"),
                     Wres["Wa"] if Wres else None)
            mm_block("Wi", CV, lambda kb, w0, ww: CV[:, kb * chw + w0:kb * chw + w0 + ww],
                     lambda pb, co, w0, ww: eact(p, IS[:, co * chw + w0:co * chw + w0 + ww],
                                                 pb[0:128, 0:ww], AF.Sigmoid,
                                                 reads=(pb,), writes=(IS,),
                                                 label=label + "_iev"),
                     Wres["Wi"] if Wres else None)
            for ct in range(CT):
                sl = slice(ct * chw, (ct + 1) * chw)
                ev2(p, (lambda v, o=RS[:, sl], i=RS[:, sl], s=LAMS[:, ct:ct + 1]:
                        v.tensor_scalar_mul(o, i, s)),
                    reads=(RS,), sreads=(LAMS,), writes=(RS,), label=label + "_la")
                eact(p, RS[:, sl], RS[:, sl], AF.Exp, reads=(RS,), writes=(RS,),
                     label=label + "_a")
                ev(p, (lambda v, o=TMP.ap, a=RS[:, sl]:
                       v.tensor_tensor(o, a, a, ALU.mult)),
                   reads=(RS,), writes=(TMP,), label=label + "_a2")
                ev(p, (lambda v, o=TMP.ap, i=TMP.ap:
                       v.tensor_scalar(o, i, -1.0, 1.0, ALU.mult, ALU.add)),
                   reads=(TMP,), writes=(TMP,), label=label + "_om")
                ev(p, (lambda v, o=TMP.ap, i=TMP.ap:
                       v.tensor_scalar_max(o, i, 1e-12)),
                   reads=(TMP,), writes=(TMP,), label=label + "_mx")
                eact(p, TMP.ap, TMP.ap, AF.Sqrt, reads=(TMP,), writes=(TMP,),
                     label=label + "_sq")
                ev(p, (lambda v, o=IS[:, sl], a=IS[:, sl], bq=CV[:, sl]:
                       v.tensor_tensor(o, a, bq, ALU.mult)),
                   reads=(IS, CV), writes=(IS,), label=label + "_ix")
                ev(p, (lambda v, o=IS[:, sl], a=IS[:, sl], bq=TMP.ap:
                       v.tensor_tensor(o, a, bq, ALU.mult)),
                   reads=(IS, TMP), writes=(IS,), label=label + "_b")
                ev2(p, (lambda v, o=HS[:, sl], a=RS[:, sl], bq=IS[:, sl],
                        init=CAR[:, ct:ct + 1]:
                        v.tensor_tensor_scan(o, a, bq, init, ALU.mult, ALU.add)),
                    reads=(RS, IS), sreads=(CAR,), writes=(HS,), label=label + "_scan")
                ev(p, (lambda v, o=CAR[:, ct:ct + 1],
                       i=HS[:, ct * chw + chw - 1:ct * chw + chw]:
                       v.tensor_copy(o, i)),
                   reads=(HS,), writes=(CAR,), label=label + "_carc")
                ev(p, (lambda v, o=HS[:, sl], a=HS[:, sl], bq=YB[:, sl]:
                       v.tensor_tensor(o, a, bq, ALU.mult)),
                   reads=(HS, YB), writes=(HS,), label=label + "_hy")
            mm_block("Wo", HS, lambda kb, w0, ww: HS[:, kb * chw + w0:kb * chw + w0 + ww],
                     lambda pb, co, w0, ww: ev(p, (lambda v,
                                                   o=c.XN[:, co * T + t0 + w0: co * T + t0 + w0 + ww],
                                                   x=c.XN[:, co * T + t0 + w0: co * T + t0 + w0 + ww],
                                                   a=pb[0:128, 0:ww]:
                                                   v.tensor_tensor(o, x, a, ALU.add)),
                                               reads=(c.XN, pb), writes=(c.XN,),
                                               label=label + "_h1"),
                     Wres["Wo"] if Wres else None)


def emit_l0_ffn_int(p, c, wd, label="l0i"):
    S2 = p.sb(label + "s2", [128, T], BF16)
    emit_rms_rep(p, c, c.XN, S2, 1e-6, label=label + "_rms")
    for ct in range(CT):
        ev(p, (lambda v, o=c.XT[:, ct * T:(ct + 1) * T],
               x=c.XT[:, ct * T:(ct + 1) * T], a=c.XN[:, ct * T:(ct + 1) * T]:
               v.tensor_tensor(o, x, a, ALU.add)),
           reads=(c.XT, c.XN), writes=(c.XT,), label=label + "_xh1")
    N2 = load_const_sb(p, label + "n2", wd["norm2_pcol"], [128, CT], F32)
    GW = 512
    NCO = GW // 128
    wgb = p.sb(label + "wgb", [128, CT * GW], BF16)
    wub = p.sb(label + "wub", [128, CT * GW], BF16)
    wdb = p.sb(label + "wdb", [128, (GW // 128) * 1024], BF16)
    UC = p.sb(label + "uc", [128, CT * 512], BF16)
    GH = p.sb(label + "gh", [128, NCO * 512], BF16)
    for hidg in range(4096 // GW):
        Wg = load_weight_cols(p, wgb, wd["Wg"], D, hidg * GW, GW,
                              name=label + f"wg{hidg}")
        Wu = load_weight_cols(p, wub, wd["Wu"], D, hidg * GW, GW,
                              name=label + f"wu{hidg}")
        KB = GW // 128
        ap_in = wd["Wd"].ap()[hidg * GW:(hidg + 1) * GW, :].rearrange(
            "(kb q) n -> q kb n", q=128)
        ap_out = wdb.ap.rearrange("q (kb n) -> q kb n", kb=KB)
        p.dma("gpsimd", ap_out, ap_in, reads=(wdb,), writes=(wdb,),
              label=label + f"ldwd{hidg}")
        Wd = Weights(wdb, GW, 1024)
        for W_ in (Wg, Wu):
            for kb in range(CT):
                ev2(p, (lambda v, o=W_.buf[:, kb * GW:(kb + 1) * GW],
                        i=W_.buf[:, kb * GW:(kb + 1) * GW], s=N2[:, kb:kb + 1]:
                        v.tensor_scalar_mul(o, i, s)),
                    reads=(W_.buf,), sreads=(N2,), writes=(W_.buf,),
                    label=label + "_n2f")
        for ch in range(T // 512):
            t0 = ch * 512
            for ct in range(CT):
                ev(p, (lambda v, o=UC[:, ct * 512:(ct + 1) * 512],
                       a=c.XN[:, ct * T + t0: ct * T + t0 + 512],
                       s=S2[:, t0:t0 + 512]:
                       v.tensor_tensor(o, a, s, ALU.mult)),
                   reads=(c.XN, S2), writes=(UC,), label=label + "_uc")
            for co in range(NCO):
                pb = c.psums[co % 2]
                for kb in range(CT):
                    et(p, (lambda tt, o=pb[0:128, 0:512],
                           w=Wg.tile(kb, co * 128, 128),
                           r=UC[:, kb * 512:(kb + 1) * 512],
                           st=(kb == 0), sp=(kb == CT - 1):
                           tt.matmul(o, w, r, start=st, stop=sp)),
                       reads=(wgb, UC), writes=(pb,), label=label + "_g")
                eact(p, GH[:, co * 512:(co + 1) * 512], pb[0:128, 0:512],
                     AF.Gelu_apprx_tanh, reads=(pb,), writes=(GH,),
                     label=label + "_gev")
            for co in range(NCO):
                pb = c.psums[2 + co % 2]
                for kb in range(CT):
                    et(p, (lambda tt, o=pb[0:128, 0:512],
                           w=Wu.tile(kb, co * 128, 128),
                           r=UC[:, kb * 512:(kb + 1) * 512],
                           st=(kb == 0), sp=(kb == CT - 1):
                           tt.matmul(o, w, r, start=st, stop=sp)),
                       reads=(wub, UC), writes=(pb,), label=label + "_u")
                ev(p, (lambda v, o=GH[:, co * 512:(co + 1) * 512],
                       a=pb[0:128, 0:512], bq=GH[:, co * 512:(co + 1) * 512]:
                       v.tensor_tensor(o, a, bq, ALU.mult)),
                   reads=(pb, GH), writes=(GH,), label=label + "_uev")
            for co in range(CT):
                pb = c.psums[4 + co % 2]
                for kb in range(GW // 128):
                    et(p, (lambda tt, o=pb[0:128, 0:512],
                           w=Wd.tile(kb, co * 128, 128),
                           r=GH[:, kb * 512:(kb + 1) * 512],
                           st=(kb == 0), sp=(kb == GW // 128 - 1):
                           tt.matmul(o, w, r, start=st, stop=sp)),
                       reads=(wdb, GH), writes=(pb,), label=label + "_d")
                sl = slice(co * T + t0, co * T + t0 + 512)
                ev(p, (lambda v, o=c.XT[:, sl], x=c.XT[:, sl], a=pb[0:128, 0:512]:
                       v.tensor_tensor(o, x, a, ALU.add)),
                   reads=(c.XT, pb), writes=(c.XT,), label=label + "_dadd")


FB = S + 1
FCB = NB * FB


def fcol(ct, b, t):
    return ct * FCB + b * FB + 1 + t


def alloc_f(p):
    F = p.sb("Fstream", [128, CT * FCB], BF16)
    return F


def emit_load_f(p, F, dram, queue="gpsimd"):
    for b in range(NB):
        for ct in range(CT):
            p.dma(queue, F[:, fcol(ct, b, 0): fcol(ct, b, 0) + S],
                  dram.ap()[b, ct * 128:(ct + 1) * 128, :], writes=(F,),
                  label=f"ldf{b}{ct}")
    for b in range(NB):
        for ct in range(CT):
            ev(p, (lambda v, o=F[:, fcol(ct, b, 0) - 1: fcol(ct, b, 0)]:
                   v.memset(o, 0.0)),
               writes=(F,), label="fguard")


def emit_store_f(p, F, dram, queue="gpsimd"):
    for b in range(NB):
        for ct in range(CT):
            p.dma(queue, dram.ap()[b, ct * 128:(ct + 1) * 128, :],
                  F[:, fcol(ct, b, 0): fcol(ct, b, 0) + S], reads=(F,),
                  label=f"stf{b}{ct}")


def emit_l0f_attn(p, c, f, F, wd, masks, l, label="lfa"):
    wq_sl = p.sb_shared("lfa_wq", [128, CT * 128], BF16)
    wk_sl = p.sb_shared("lfa_wk", [128, CT * 128], BF16)
    wv_sl = p.sb_shared("lfa_wv", [128, CT * 128], BF16)
    wo_sl = p.sb_shared("lfa_wo", [128, D], BF16)
    QS = load_const_sb(p, label + "qs", wd[f"qscale{l}"], [128, 1], F32)
    KS = load_const_sb(p, label + "ks", wd[f"kscale{l}"], [128, 1], F32)
    SW = load_const_sb(p, "l0fsw", wd["swap64"], [128, 64], BF16, shared=True)
    QC = load_const_sb(p, f"l0ftqc{l}", wd["tab_qc"], [128, S], BF16, shared=True)
    QSi = load_const_sb(p, f"l0ftqs{l}", wd["tab_qs"], [128, S], BF16, shared=True)
    KC = load_const_sb(p, f"l0ftkc{l}", wd["tab_kc"], [128, S], BF16, shared=True)
    KSi = load_const_sb(p, f"l0ftks{l}", wd["tab_ks"], [128, S], BF16, shared=True)
    SQ = p.sb_shared("lfa_sqb", [128, S], BF16)
    QR2 = p.sb_shared("lfa_qr2", [128, S], BF16)
    wqkv = wd[f"wqkv{l}"]

    def rhs_sh(kb, b, t0, tw):
        off = 1 if kb >= 4 else 0
        return F[:, fcol(kb, b, t0) - off: fcol(kb, b, t0) - off + tw]

    def a_spec(qt):
        out = []
        k0 = max(0, qt * 128 - 512)
        kend = qt * 128 + 128
        while k0 < kend:
            kw = min(512, kend - k0)
            d0 = qt * 128 - k0
            out.append((k0, kw, (masks["l0A"], 512 - d0)))
            k0 += kw
        return out

    def c_spec(qc):
        out = []
        for kt in range(max(0, qc * 4 - 4), qc * 4 + 4):
            dc = qc * 512 - kt * 128
            out.append((kt * 128, (masks["l0C"], 512 + dc)))
        return out

    for g in range(4):
        Wq = load_weight_cols(p, wq_sl, wqkv, D, g * 128, 128, name=label + f"q{g}")
        Wk = load_weight_cols(p, wk_sl, wqkv, D, 512 + g * 128, 128, name=label + f"k{g}")
        Wv = load_weight_cols(p, wv_sl, wqkv, D, 1024 + g * 128, 128, name=label + f"v{g}")
        Wo = load_weight_rows(p, wo_sl, wd[f"wo{l}"], g * 128, 128, D,
                              name=label + f"o{g}")
        for b in range(NB):
            for dst, W_, lab in ((f.qT, Wq, "_q"), (f.kT, Wk, "_k")):
                for co in range(1):
                    for tg0 in range(0, 4, 2):
                        for kb in range(CT):
                            for ti in (tg0, tg0 + 1):
                                pb = c.psums[ti - tg0]
                                et(p, (lambda tt, o=pb[0:128, 0:512],
                                       w=W_.tile(kb, 0, 128),
                                       r=rhs_sh(kb, b, ti * 512, 512),
                                       st=(kb == 0), sp=(kb == CT - 1):
                                       tt.matmul(o, w, r, start=st, stop=sp)),
                                   reads=(W_.buf, F), writes=(pb,), label=label + lab)
                        for ti in (tg0, tg0 + 1):
                            pb = c.psums[ti - tg0]
                            eact(p, dst[:, ti * 512:(ti + 1) * 512], pb[0:128, 0:512],
                                 AF.Copy, reads=(pb,), writes=(dst,),
                                 label=label + lab + "ev")
            ev(p, lambda v: v.memset(f.vx.ap, 1.0), writes=(f.vx,), label=label + "_vo")
            for tt_i in range(16):
                pb = c.psums[2 + (tt_i % 2)]
                for kb in range(CT):
                    off = 1 if kb >= 4 else 0
                    lhs = F[:, fcol(kb, b, tt_i * 128) - off:
                            fcol(kb, b, tt_i * 128) - off + 128]
                    et(p, (lambda tt, o=pb[0:128, 0:128], w=lhs, r=Wv.tile(kb, 0, 128),
                           st=(kb == 0), sp=(kb == CT - 1):
                           tt.matmul(o, w, r, start=st, stop=sp)),
                       reads=(F, Wv.buf), writes=(pb,), label=label + "_v")
                for hh in range(2):
                    eact(p, f.vx[:, tt_i * 130 + hh * 65: tt_i * 130 + hh * 65 + 64],
                         pb[0:128, hh * 64:(hh + 1) * 64], AF.Copy,
                         reads=(pb,), writes=(f.vx,), label=label + "_vev")
            for buf, tabC, tabS, psc, extra in ((f.qT, QC, QSi, QS, 8.0), (f.kT, KC, KSi, KS, None)):
                eact(p, SQ.ap, buf.ap, AF.Square, reads=(buf,), writes=(SQ,),
                     label=label + "_sq")
                for hh in range(2):
                    for ch in range(4):
                        pr = c.prow[ch % 2]
                        et(p, (lambda tt, o=pr[0:1, 0:512],
                               w=c.ones_col_bf[hh * 64:hh * 64 + 64, 0:1],
                               r=SQ[hh * 64:hh * 64 + 64, ch * 512:(ch + 1) * 512]:
                               tt.matmul(o, w, r, start=True, stop=True)),
                           reads=(SQ, c.ones_col_bf), writes=(pr,), label=label + "_ss")
                        R0 = c.R[0]
                        eact(p, R0[0:1, 0:512], pr[0:1, 0:512], AF.Sqrt,
                             reads=(pr,), writes=(R0,), label=label + "_nrm")
                        ev(p, (lambda v, o=R0[0:1, 0:512], i=R0[0:1, 0:512]:
                               v.tensor_scalar_max(o, i, 1e-12)),
                           reads=(R0,), writes=(R0,), label=label + "_cl")
                        ev(p, (lambda v, o=R0[0:1, 0:512], i=R0[0:1, 0:512]:
                               v.reciprocal(o, i)),
                           reads=(R0,), writes=(R0,), label=label + "_rc")
                        if extra is not None:
                            ev(p, (lambda v, o=R0[0:1, 0:512], i=R0[0:1, 0:512],
                                   e_=extra: v.tensor_scalar_mul(o, i, e_)),
                               reads=(R0,), writes=(R0,), label=label + "_x8")
                        pbb = c.psums[4]
                        et(p, (lambda tt, o=pbb[0:128, 0:512], w=c.ones_row_f.ap,
                               r=R0[0:1, 0:512]:
                               tt.matmul(o, w, r, start=True, stop=True)),
                           reads=(R0, c.ones_row_f), writes=(pbb,), label=label + "_sb")
                        sl = slice(ch * 512, (ch + 1) * 512)
                        ev(p, (lambda v, o=buf[hh * 64:hh * 64 + 64, sl],
                               a=buf[hh * 64:hh * 64 + 64, sl],
                               bq=pbb[hh * 64:hh * 64 + 64, 0:512]:
                               v.tensor_tensor(o, a, bq, ALU.mult)),
                           reads=(buf, pbb), writes=(buf,), label=label + "_sm")
                ev2(p, (lambda v, o=buf.ap, i=buf.ap, s=psc.ap:
                        v.tensor_scalar_mul(o, i, s)),
                    reads=(buf,), sreads=(psc,), writes=(buf,), label=label + "_ds")
                ev(p, (lambda v, o=QR2.ap, a=buf.ap, bq=tabC.ap:
                       v.tensor_tensor(o, a, bq, ALU.mult)),
                   reads=(buf, tabC), writes=(QR2,), label=label + "_rc1")
                for hh in range(2):
                    for ch in range(4):
                        psw = c.psums[5]
                        et(p, (lambda tt, o=psw[hh * 64:hh * 64 + 64, 0:512],
                               w=SW[hh * 64:hh * 64 + 64, 0:64],
                               r=buf[hh * 64:hh * 64 + 64, ch * 512:(ch + 1) * 512]:
                               tt.matmul(o, w, r, start=True, stop=True)),
                           reads=(SW, buf), writes=(psw,), label=label + "_sw")
                        sl = slice(ch * 512, (ch + 1) * 512)
                        ev(p, (lambda v, o=SQ[hh * 64:hh * 64 + 64, sl],
                               a=psw[hh * 64:hh * 64 + 64, 0:512],
                               bq=tabS[hh * 64:hh * 64 + 64, sl]:
                               v.tensor_tensor(o, a, bq, ALU.mult)),
                           reads=(psw, tabS), writes=(SQ,), label=label + "_rs1")
                ev(p, (lambda v, o=buf.ap, a=QR2.ap, bq=SQ.ap:
                       v.tensor_tensor(o, a, bq, ALU.add)),
                   reads=(QR2, SQ), writes=(buf,), label=label + "_rfin")
            for hh in range(2):
                emit_flash_head(p, c, f, f.qT, f.kT, hh * 64, hh, 64,
                                f.att, hh * 64, a_spec, c_spec,
                                label=f"{label}f{b}{g}{hh}")
            for co in range(CT):
                for tch_i in range(4):
                    pb = c.psums[4 + (tch_i % 2)]
                    et(p, (lambda tt, o=pb[0:128, 0:512],
                           w=Wo.buf[:, co * 128:(co + 1) * 128],
                           r=f.att[:, tch_i * 512:(tch_i + 1) * 512]:
                           tt.matmul(o, w, r, start=True, stop=True)),
                       reads=(Wo.buf, f.att), writes=(pb,), label=label + "_ro")
                    dst = F[:, fcol(co, b, tch_i * 512): fcol(co, b, tch_i * 512) + 512]
                    ev(p, (lambda v, o=dst, x=dst, a=pb[0:128, 0:512]:
                           v.tensor_tensor(o, x, a, ALU.add)),
                       reads=(F, pb), writes=(F,), label=label + "_radd")


def emit_l0f_ffn(p, c, F, wd, l, label="lff"):
    LG = load_const_sb(p, label + "lg", wd[f"lng{l}"], [128, CT], F32)
    LB = load_const_sb(p, label + "lb", wd[f"lnb{l}"], [128, CT], F32)
    B2C = load_const_sb(p, label + "b2", wd[f"b2p{l}"], [128, CT], F32)
    B1A = load_const_sb(p, label + "b1a", wd[f"b1ap{l}"], [128, 22], F32)
    B1G = load_const_sb(p, label + "b1g", wd[f"b1gp{l}"], [128, 22], F32)
    SH = 1024
    LNH = p.sb_shared("lff_lnh", [128, CT * SH], BF16)
    AH = p.sb_shared("lff_ah", [128, 4 * 512], BF16)
    GH = p.sb_shared("lff_gh", [128, 4 * 512], BF16)
    w1a_b = p.sb_shared("lff_w1a", [128, CT * 512], BF16)
    w1g_b = p.sb_shared("lff_w1g", [128, CT * 512], BF16)
    w2_b = p.sb_shared("lff_w2", [128, 4 * D], BF16)
    HID = 2730
    BSH = p.sb(label + "bsh", [128, CT], BF16)
    for b, hf in [(b_, h_) for b_ in range(NB) for h_ in range(S // SH)]:
        if hf == 0 and S // SH > 1:
            for ct in range(CT):
                eact(p, BSH[:, ct:ct + 1],
                     F[:, fcol(ct, b, SH - 1): fcol(ct, b, SH - 1) + 1],
                     AF.Copy, reads=(F,), writes=(BSH,), label=label + "_bst")
        for ch in range(SH // 512):
            t0 = hf * SH + ch * 512
            lt0 = ch * 512
            for ct in range(CT):
                off = 1 if ct >= 4 else 0
                dstap = LNH[:, ct * SH + lt0: ct * SH + lt0 + 512]
                if off and hf > 0 and ch == 0:
                    eact(p, LNH[:, ct * SH + lt0: ct * SH + lt0 + 1],
                         BSH[:, ct:ct + 1], AF.Copy, reads=(BSH,),
                         writes=(LNH,), label=label + "_cpb")
                    eact(p, LNH[:, ct * SH + lt0 + 1: ct * SH + lt0 + 512],
                         F[:, fcol(ct, b, t0): fcol(ct, b, t0) + 511],
                         AF.Copy, reads=(F,), writes=(LNH,), label=label + "_cpr")
                else:
                    srcap = F[:, fcol(ct, b, t0) - off: fcol(ct, b, t0) - off + 512]
                    eact(p, dstap, srcap, AF.Copy, reads=(F,), writes=(LNH,),
                         label=label + "_cp")
            for sq_, ridx in ((False, 1), (True, 2)):
                pr = c.prow[0 if not sq_ else 1]
                for ct in range(CT):
                    ap = LNH[:, ct * SH + lt0: ct * SH + lt0 + 512]
                    if sq_:
                        sqsl = c.sqc[:, (ct % 2) * 512:(ct % 2) * 512 + 512]
                        eact(p, sqsl, ap, AF.Square, reads=(LNH,), writes=(c.sqc,),
                             label=label + "_sq")
                        ap2, rd = sqsl, (c.sqc, c.recip_bf)
                    else:
                        ap2, rd = ap, (LNH, c.recip_bf)
                    et(p, (lambda tt, o=pr[0:1, 0:512], w=c.recip_bf.ap, r=ap2,
                           st=(ct == 0), sp=(ct == CT - 1):
                           tt.matmul(o, w, r, start=st, stop=sp)),
                       reads=rd, writes=(pr,), label=label + "_st")
                eact(p, c.R[ridx][0:1, 0:512], pr[0:1, 0:512], AF.Copy,
                     reads=(pr,), writes=(c.R[ridx],), label=label + "_sev")
            R0, R1, R2 = c.R[0], c.R[1], c.R[2]
            ev(p, lambda v: v.tensor_tensor(R0[0:1, :], R1[0:1, :], R1[0:1, :], ALU.mult),
               reads=(R1,), writes=(R0,), label=label + "_m2")
            ev(p, lambda v: v.scalar_tensor_tensor(R0[0:1, :], R0[0:1, :], -1.0,
                                                   R2[0:1, :], ALU.mult, ALU.add),
               reads=(R0, R2), writes=(R0,), label=label + "_var")
            eact(p, R0[0:1, :], R0[0:1, :], AF.Ln, bias=1e-5, reads=(R0,), writes=(R0,),
                 label=label + "_l")
            eact(p, R0[0:1, :], R0[0:1, :], AF.Exp, scale=-0.5, reads=(R0,), writes=(R0,),
                 label=label + "_e")
            ev(p, lambda v: v.tensor_tensor(R1[0:1, :], R1[0:1, :], R0[0:1, :], ALU.mult),
               reads=(R1, R0), writes=(R1,), label=label + "_b")
            pA, pB = c.psums[0], c.psums[1]
            emit_bcast_chunk(p, c, R0, pA, label=label + "_bA")
            emit_bcast_chunk(p, c, R1, pB, label=label + "_bB")
            for ct in range(CT):
                dstap = LNH[:, ct * SH + lt0: ct * SH + lt0 + 512]

                def app(v, o=dstap, a=pA[:, 0:512], bb=pB[:, 0:512],
                        g=LG[:, ct:ct + 1], b_=LB[:, ct:ct + 1]):
                    v.tensor_tensor(o, o, a, ALU.mult)
                    v.tensor_tensor(o, o, bb, ALU.subtract)
                    return v.tensor_scalar(o, o, g, b_, ALU.mult, ALU.add)
                p.emit("vector", app, reads=(pA, pB, LNH), sreads=(LG, LB),
                       writes=(LNH,), label=label + "_app")
        for hidg in range(6):
            w0 = hidg * 512
            wdt = min(512, HID - w0)
            nct = (wdt + 127) // 128
            W1a = load_weight_cols(p, w1a_b, wd[f"w1_{l}"], D, w0, wdt,
                                   name=label + f"a{hidg}")
            W1g = load_weight_cols(p, w1g_b, wd[f"w1_{l}"], D, HID + w0, wdt,
                                   name=label + f"g{hidg}")
            for kb in range(nct):
                kh = min(128, wdt - kb * 128)
                p.dma("gpsimd", w2_b[0:kh, kb * D:kb * D + D],
                      wd[f"w2_{l}"].ap()[w0 + kb * 128: w0 + kb * 128 + kh, :],
                      reads=(w2_b,), writes=(w2_b,), label=label + f"ldw2{hidg}")
            for tch_i in range(SH // 512):
                t0 = hf * SH + tch_i * 512
                for sub, W_, dstb, b1c, func in (("a", W1a, AH, B1A, AF.Identity),
                                                 ("g", W1g, GH, B1G, AF.Gelu)):
                    for coi in range(nct):
                        cw_ = min(128, wdt - coi * 128)
                        pb = c.psums[2 + (coi % 2)]
                        for kb in range(CT):
                            et(p, (lambda tt, o=pb[0:cw_, 0:512],
                                   w=W_.tile(kb, coi * 128, cw_),
                                   r=LNH[:, kb * SH + tch_i * 512: kb * SH + tch_i * 512 + 512],
                                   st=(kb == 0), sp=(kb == CT - 1):
                                   tt.matmul(o, w, r, start=st, stop=sp)),
                               reads=(W_.buf, LNH), writes=(pb,), label=label + "_w1" + sub)
                        eact(p, dstb[0:cw_, coi * 512:coi * 512 + 512], pb[0:cw_, 0:512],
                             func, bias=b1c[0:cw_, hidg * 4 + coi:hidg * 4 + coi + 1],
                             reads=(pb, b1c), writes=(dstb,), label=label + "_ev" + sub)
                ev(p, (lambda v, o=AH[0:128, 0:nct * 512], a=AH[0:128, 0:nct * 512],
                       bq=GH[0:128, 0:nct * 512]:
                       v.tensor_tensor(o, a, bq, ALU.mult)),
                   reads=(AH, GH), writes=(AH,), label=label + "_hh")
                for co in range(CT):
                    pb = c.psums[4 + (co % 2)]
                    for coi in range(nct):
                        kw_ = min(128, wdt - coi * 128)
                        et(p, (lambda tt, o=pb[0:128, 0:512],
                               w=w2_b[0:kw_, coi * D + co * 128: coi * D + (co + 1) * 128],
                               r=AH[0:kw_, coi * 512:coi * 512 + 512],
                               st=(coi == 0), sp=(coi == nct - 1):
                               tt.matmul(o, w, r, start=st, stop=sp)),
                           reads=(w2_b, AH), writes=(pb,), label=label + "_w2")
                    dst = F[:, fcol(co, b, t0): fcol(co, b, t0) + 512]
                    ev(p, (lambda v, o=dst, x=dst, a=pb[0:128, 0:512]:
                           v.tensor_tensor(o, x, a, ALU.add)),
                       reads=(F, pb), writes=(F,), label=label + "_fadd")
        if hf == S // SH - 1:
            for ct in range(CT):
                dst = F[:, fcol(ct, b, 0): fcol(ct, b, 0) + S]
                ev2(p, (lambda v, o=dst, i=dst, s=B2C[:, ct:ct + 1]:
                        v.tensor_scalar_add(o, i, s)),
                    reads=(F,), sreads=(B2C,), writes=(F,), label=label + "_b2")


def l0f_masks_host():
    out = {}
    MA = np.zeros((128, 1024 + 128), np.float32)
    for qf in range(128):
        for v in range(1024 + 128):
            d = qf - (v - 512)
            if not (0 <= d <= 512):
                MA[qf, v] = -1e30
    out["l0A"] = MA
    MC = np.zeros((128, 1024 + 512), np.float32)
    for kp in range(128):
        for v in range(1024 + 512):
            d = (v - 512) - kp
            if not (0 <= d <= 512):
                MC[kp, v] = -1e30
    out["l0C"] = MC
    return out


def l0f_tabs_host():
    DH, W = 64, 512
    half = np.arange(0, DH, 2, dtype=np.float64)
    inv = 1.0 / 10000.0 ** (half / DH)
    t = np.arange(S, dtype=np.float64)
    fr = t[:, None] * inv[None, :]
    fr = np.concatenate([fr, fr], 1)
    base = (half + 0.4 * DH) / (1.4 * DH)
    power = (t - S // 2) / (W // 2)
    sc = base[None, :] ** power[:, None]
    sc = np.concatenate([sc, sc], 1)
    cos, sin = np.cos(fr), np.sin(fr)
    qc = (cos * sc).T
    qs = (sin * sc).T
    kc = (cos / sc).T
    ks = (sin / sc).T
    qs2 = qs.copy(); qs2[0:32] *= -1.0
    ks2 = ks.copy(); ks2[0:32] *= -1.0
    dup = lambda a: np.concatenate([a, a], 0).astype(np.float32)
    return {"tab_qc": dup(qc), "tab_qs": dup(qs2), "tab_kc": dup(kc),
            "tab_ks": dup(ks2)}


def l0f_swap_host():
    sw = np.zeros((64, 64), np.float32)
    for d_ in range(64):
        sw[(d_ + 32) % 64, d_] = 1.0
    return np.concatenate([sw, sw], 0).astype(np.float32)


def emit_l2_ffn(p, c, wd, nc, label="l2f"):
    cwT = load_weight(p, label + "cwt", wd["choice_wT"], D, 8)
    identf = p.sb(label + "idf", [128, 128], F32)
    p.emit("gpsimd", lambda g: g.iota(identf.ap, [[-1, 128]], base=0,
                                      channel_multiplier=1,
                                      allow_small_or_imprecise_dtypes=True),
           writes=(identf,), label=label + "_idio")
    ev(p, lambda v: v.tensor_scalar(identf.ap, identf.ap, 0.0, None, ALU.is_equal),
       reads=(identf,), writes=(identf,), label=label + "_ideq")
    PB = p.sb(label + "pbm", [128, 16 * 8], F32)
    P2 = p.sb(label + "p2", [64, S], F32)
    JK = p.sb(label + "jk", [64, S], BF16)
    LO = p.sb(label + "lo", [128, 6], F32)
    BIDX = p.sb(label + "bidx", [128, 16 * 8], F32)
    GIX = p.sb(label + "gix", [128, 16 * 8], F32)
    IDS = p.sb(label + "ids", [128, 4], F32)
    UID = p.sb(label + "uid", [128, 2], U32)
    XTOK = p.sb(label + "xtok", [128, 1024], BF16)
    TMPG = XTOK
    IOTC = p.sb(label + "iotc", [128, 1], F32)
    ACC = p.sb(label + "acc", [128, D], F32)
    YBB = p.sb(label + "ybb", [128, D], BF16)
    XE = p.sb(label + "xe", [128, CT * 512], BF16)
    ZB = XE
    HE = p.sb(label + "he", [128, 4 * 512], BF16)
    w1b = p.sb(label + "w1b", [128, CT * 512], BF16)
    w2b = p.sb(label + "w2b", [128, 4 * 1024], BF16)

    p.emit("gpsimd", lambda g: g.iota(IOTC.ap, [[0, 1]], base=0, channel_multiplier=1,
                                      allow_small_or_imprecise_dtypes=True),
           writes=(IOTC,), label=label + "_io")
    xin = [Buf(nc.dram_tensor(f"l2f_xin{b_}", [8 * 512 + 128, 1024], BF16), f"xin{b_}")
           for b_ in range(NB)]
    ydr = [Buf(nc.dram_tensor(f"l2f_y{b_}", [8 * 513, D], F32), f"y{b_}")
           for b_ in range(NB)]

    for b in range(NB):
        for tt_i in range(16):
            pb_ = c.psums[tt_i % 2]
            for kb in range(CT):
                lhs = c.XN[:, kb * T + b * S + tt_i * 128:
                           kb * T + b * S + (tt_i + 1) * 128]
                et(p, (lambda tt, o=pb_[0:128, 0:8], w=lhs, r=cwT.tile(kb, 0, 8),
                       st=(kb == 0), sp=(kb == CT - 1):
                       tt.matmul(o, w, r, start=st, stop=sp)),
                   reads=(c.XN, cwT.buf), writes=(pb_,), label=label + "_lg")
            sl = PB[:, tt_i * 8:(tt_i + 1) * 8]
            eact(p, sl, pb_[0:128, 0:8], AF.Copy, reads=(pb_,), writes=(PB,),
                 label=label + "_le")
            ev(p, (lambda v, o=LO[:, 3:4], i=sl:
                   v.reduce_max(o, i, AX.X, negate=True)),
               reads=(PB,), writes=(LO,), label=label + "_lm")
            p.emit("scalar", (lambda s_, o=sl, i=sl, bb=LO[:, 3:4]:
                              s_.activation(o, i, AF.Exp, bias=bb,
                                            accum_out=LO[:, 2:3])),
                   reads=(PB,), sreads=(LO,), writes=(PB, LO), label=label + "_lx")
            ev2(p, (lambda v, o=LO[:, 2:3], i=LO[:, 2:3]: v.reciprocal(o, i)),
                reads=(LO,), writes=(LO,), label=label + "_lr")
            ev2(p, (lambda v, o=sl, i=sl, s=LO[:, 2:3]: v.tensor_scalar_mul(o, i, s)),
                reads=(PB,), sreads=(LO,), writes=(PB,), label=label + "_lp")
            pt_ = c.psums[2]
            et(p, (lambda tt, o=pt_[0:8, 0:128], i=sl, idf=identf.ap:
                   tt.transpose(o, i, idf)),
               reads=(PB, identf), writes=(pt_,), label=label + "_lt")
            eact(p, P2[0:8, tt_i * 128:(tt_i + 1) * 128], pt_[0:8, 0:128], AF.Copy,
                 reads=(pt_,), writes=(P2,), label=label + "_lte")
        ev(p, lambda v: v.memset(LO[0:64, 0:1], 0.0), writes=(LO,), label=label + "_b0")
        ev(p, lambda v: v.memset(LO[0:64, 1:2], 1.0), writes=(LO,), label=label + "_b1")
        for it in range(40):
            ev(p, (lambda v: v.tensor_tensor(LO[0:64, 2:3], LO[0:64, 0:1], LO[0:64, 1:2],
                                             ALU.add)),
               reads=(LO,), writes=(LO,), label=label + "_bt")
            ev(p, (lambda v: v.tensor_scalar_mul(LO[0:64, 2:3], LO[0:64, 2:3], 0.5)),
               reads=(LO,), writes=(LO,), label=label + "_bh")
            ev2(p, (lambda v: v.tensor_scalar(JK.ap, P2.ap, LO[0:64, 2:3], 0.0,
                                              ALU.is_ge, ALU.add,
                                              accum_out=LO[0:64, 3:4])),
                reads=(P2,), sreads=(LO,), writes=(JK, LO), label=label + "_bc")
            ev(p, (lambda v: v.tensor_scalar(LO[0:64, 3:4], LO[0:64, 3:4], 512.0, None,
                                             ALU.is_ge)),
               reads=(LO,), writes=(LO,), label=label + "_bg")
            ev(p, (lambda v: v.tensor_tensor(LO[0:64, 4:5], LO[0:64, 2:3], LO[0:64, 0:1],
                                             ALU.subtract)),
               reads=(LO,), writes=(LO,), label=label + "_b4")
            ev(p, (lambda v: v.tensor_tensor(LO[0:64, 5:6], LO[0:64, 1:2], LO[0:64, 2:3],
                                             ALU.subtract)),
               reads=(LO,), writes=(LO,), label=label + "_b5")
            ev2(p, (lambda v: v.scalar_tensor_tensor(LO[0:64, 0:1], LO[0:64, 4:5],
                                                     LO[0:64, 3:4], LO[0:64, 0:1],
                                                     ALU.mult, ALU.add)),
                reads=(LO,), sreads=(LO,), writes=(LO,), label=label + "_bl")
            ev2(p, (lambda v: v.scalar_tensor_tensor(LO[0:64, 1:2], LO[0:64, 5:6],
                                                     LO[0:64, 3:4], LO[0:64, 2:3],
                                                     ALU.mult, ALU.add)),
                reads=(LO,), sreads=(LO,), writes=(LO,), label=label + "_bu")
        ev2(p, (lambda v: v.tensor_scalar(JK.ap, P2.ap, LO[0:64, 0:1], None, ALU.is_ge)),
            reads=(P2,), sreads=(LO,), writes=(JK,), label=label + "_mk")
        ev(p, (lambda v: v.memset(ZB[0:64, 0:S], 0.0)), writes=(ZB,),
           label=label + "_zb")
        for e_ in range(8):
            p.dma("gpsimd", ydr[b].t.ap()[e_ * 513 + 512: e_ * 513 + 513, :],
                  ZB[0:1, 0:D], reads=(ZB, ydr[b]), writes=(ydr[b],),
                  label=label + "_yz2")
        ev(p, (lambda v: v.tensor_tensor_scan(P2.ap, ZB[0:64, 0:S], JK.ap, 0.0,
                                              ALU.max, ALU.add)),
           reads=(ZB, JK), writes=(P2,), label=label + "_psc")
        ev(p, (lambda v: v.tensor_scalar_add(P2.ap, P2.ap, -1.0)),
           reads=(P2,), writes=(P2,), label=label + "_pm1")
        for tt_i in range(16):
            pt_ = c.psums[3]
            et(p, (lambda tt, o=pt_[0:128, 0:8],
                   i=P2[0:8, tt_i * 128:(tt_i + 1) * 128], idf=identf[0:8, 0:8]:
                   tt.transpose(o, i, idf)),
               reads=(P2, identf), writes=(pt_,), label=label + "_tp")
            eact(p, BIDX[:, tt_i * 8:(tt_i + 1) * 8], pt_[0:128, 0:8], AF.Copy,
                 reads=(pt_,), writes=(BIDX,), label=label + "_tpe")
        if not hasattr(p, "_l2stg"):
            p._l2stg = p.sb(label + "_stg", [128, 128], F32)
        STG = p._l2stg
        for tt_i in range(16):
            eact(p, STG[0:8, 0:128], JK[0:8, tt_i * 128:(tt_i + 1) * 128], AF.Copy,
                 reads=(JK,), writes=(STG,), label=label + "_tmc")
            pmk = c.psums[4]
            et(p, (lambda tt, o=pmk[0:128, 0:8], i=STG[0:8, 0:128], idf=identf[0:8, 0:8]:
                   tt.transpose(o, i, idf)),
               reads=(STG, identf), writes=(pmk,), label=label + "_tm")
            eact(p, GIX[:, tt_i * 8:(tt_i + 1) * 8], pmk[0:128, 0:8], AF.Copy,
                 reads=(pmk,), writes=(GIX,), label=label + "_tme")
        for tt_i in range(16):
            for kb in range(CT):
                lhs = c.XN[:, kb * T + b * S + tt_i * 128:
                           kb * T + b * S + (tt_i + 1) * 128]
                STG = p._l2stg
                eact(p, STG.ap, lhs, AF.Copy, reads=(c.XN,), writes=(STG,),
                     label=label + "_xtc")
                pmx = c.psums[5]
                et(p, (lambda tt, o=pmx[0:128, 0:128], i=STG.ap, idf=identf.ap:
                       tt.transpose(o, i, idf)),
                   reads=(STG, identf), writes=(pmx,), label=label + "_xt")
                eact(p, XTOK[:, kb * 128:(kb + 1) * 128], pmx[0:128, 0:128], AF.Copy,
                     reads=(pmx,), writes=(XTOK,), label=label + "_xte")
            for e in range(8):
                pc_ = slice(tt_i * 8 + e, tt_i * 8 + e + 1)
                ev(p, (lambda v, o=IDS[:, 0:1], a=BIDX[:, pc_], e_=float(e * 512):
                       v.tensor_scalar_add(o, a, e_)),
                   reads=(BIDX,), writes=(IDS,), label=label + "_u1")
                ev(p, (lambda v, o=IDS[:, 0:1], a=IDS[:, 0:1], bb=GIX[:, pc_]:
                       v.tensor_tensor(o, a, bb, ALU.mult)),
                   reads=(IDS, GIX), writes=(IDS,), label=label + "_u2")
                ev(p, (lambda v, o=IDS[:, 1:2], a=GIX[:, pc_]:
                       v.tensor_scalar(o, a, -4096.0, 4096.0, ALU.mult, ALU.add)),
                   reads=(GIX,), writes=(IDS,), label=label + "_u3")
                ev(p, (lambda v, o=IDS[:, 2:3], a=GIX[:, pc_]:
                       v.tensor_scalar(o, a, -1.0, 1.0, ALU.mult, ALU.add)),
                   reads=(GIX,), writes=(IDS,), label=label + "_u3b")
                ev2(p, (lambda v, o=IDS[:, 1:2], i=IDS[:, 2:3], s=IOTC.ap,
                        g=IDS[:, 1:2]: v.scalar_tensor_tensor(o, i, s, g,
                                                              ALU.mult, ALU.add)),
                    reads=(IDS,), sreads=(IOTC,), writes=(IDS,), label=label + "_u3c")
                ev(p, (lambda v, o=IDS[:, 0:1], a=IDS[:, 0:1], bb=IDS[:, 1:2]:
                       v.tensor_tensor(o, a, bb, ALU.add)),
                   reads=(IDS,), writes=(IDS,), label=label + "_u4")
                ev(p, (lambda v, o=UID[:, 0:1], i=IDS[:, 0:1]: v.tensor_copy(o, i)),
                   reads=(IDS,), writes=(UID,), label=label + "_u5")
                p.emit("gpsimd",
                       (lambda g, o=xin[b].t.ap(), io=UID[:, 0:1], i=XTOK.ap:
                        g.indirect_dma_start(o, IOA(ap=io, axis=0), i, None)),
                       reads=(XTOK, UID, xin[b]), writes=(xin[b],), dma=True,
                       label=label + "_sc")
        for e in range(8):
            for kb in range(CT):
                p.emit("sync",
                       (lambda s_, o=XE[:, kb * 512:(kb + 1) * 512],
                        i=xin[b].t.ap()[e * 512:(e + 1) * 512, kb * 128:(kb + 1) * 128]:
                        s_.dma_start_transpose(o, i)),
                       reads=(xin[b], XE), writes=(XE,), dma=True,
                       label=label + "_xet")
            for q4 in range(4):
                W1h = load_weight_cols(p, w1b, wd["w1"][e], D, q4 * 512, 512,
                                       name=label + f"w1{b}{e}{q4}", stride=512)
                ap_in = wd["w2"][e].ap()[q4 * 512:(q4 + 1) * 512, :] \
                    .rearrange("(kb q) n -> q kb n", q=128)
                ap_out = w2b.ap.rearrange("q (kb n) -> q kb n", kb=4)
                p.dma("gpsimd", ap_out, ap_in, reads=(w2b,), writes=(w2b,),
                      label=label + f"ldw2{b}{e}{q4}")
                W2h = Weights(w2b, 512, 1024)
                for co in range(4):
                    pb_ = c.psums[co % 2]
                    for kb in range(CT):
                        et(p, (lambda tt, o=pb_[0:128, 0:512],
                               w=W1h.tile(kb, co * 128, 128),
                               r=XE[:, kb * 512:(kb + 1) * 512],
                               st=(kb == 0), sp=(kb == CT - 1):
                               tt.matmul(o, w, r, start=st, stop=sp)),
                           reads=(w1b, XE), writes=(pb_,), label=label + "_h")
                    eact(p, HE[:, co * 512:(co + 1) * 512], pb_[0:128, 0:512],
                         AF.Silu, reads=(pb_,), writes=(HE,), label=label + "_hs")
                for ss in range(4):
                    for cc in range(2):
                        pb_ = c.psums[2 + (2 * ss + cc) % 4]
                        for kb in range(4):
                            et(p, (lambda tt, o=pb_[0:128, 0:512],
                                   w=HE[:, kb * 512 + ss * 128: kb * 512 + ss * 128 + 128],
                                   r=W2h.tile(kb, cc * 512, 512),
                                   st=(kb == 0), sp=(kb == 3):
                                   tt.matmul(o, w, r, start=st, stop=sp)),
                               reads=(HE, w2b), writes=(pb_,), label=label + "_y")
                        eact(p, YBB[:, cc * 512:(cc + 1) * 512], pb_[0:128, 0:512],
                             AF.Copy, reads=(pb_,), writes=(YBB,), label=label + "_ye")
                    kw = dict(accum_op=ALU.add) if q4 > 0 else {}
                    p.dma("gpsimd",
                          ydr[b].t.ap()[e * 513 + ss * 128: e * 513 + (ss + 1) * 128, :],
                          YBB.ap, reads=(YBB, ydr[b]), writes=(ydr[b],),
                          label=label + "_yst", **kw)
        for tt_i in range(16):
            ev(p, lambda v: v.memset(ACC.ap, 0.0), writes=(ACC,), label=label + "_az")
            for e in range(8):
                pc_ = slice(tt_i * 8 + e, tt_i * 8 + e + 1)
                ev(p, (lambda v, o=IDS[:, 0:1], a=BIDX[:, pc_], bb=GIX[:, pc_]:
                       v.tensor_tensor(o, a, bb, ALU.mult)),
                   reads=(BIDX, GIX), writes=(IDS,), label=label + "_g1")
                ev(p, (lambda v, o=IDS[:, 1:2], a=GIX[:, pc_]:
                       v.tensor_scalar(o, a, -512.0, 512.0, ALU.mult, ALU.add)),
                   reads=(GIX,), writes=(IDS,), label=label + "_g2")
                ev(p, (lambda v, o=IDS[:, 0:1], a=IDS[:, 0:1], bb=IDS[:, 1:2]:
                       v.tensor_tensor(o, a, bb, ALU.add)),
                   reads=(IDS,), writes=(IDS,), label=label + "_g3")
                ev(p, (lambda v, o=UID[:, 1:2], i=IDS[:, 0:1]: v.tensor_copy(o, i)),
                   reads=(IDS,), writes=(UID,), label=label + "_g4")
                p.emit("gpsimd",
                       (lambda g, o=TMPG.ap, io=UID[:, 1:2],
                        i=ydr[b].t.ap(), eo=e * 513 * D:
                        g.indirect_dma_start(o, None, i, IOA(ap=io, axis=0),
                                             element_offset=eo)),
                       reads=(ydr[b], UID, TMPG), writes=(TMPG,), dma=True,
                       label=label + "_ga")
                ev2(p, (lambda v, o=ACC.ap, i=TMPG.ap,
                        s=PB[:, tt_i * 8 + e: tt_i * 8 + e + 1], g=ACC.ap:
                        v.scalar_tensor_tensor(o, i, s, g, ALU.mult, ALU.add)),
                    reads=(TMPG, ACC), sreads=(PB,), writes=(ACC,),
                    label=label + "_gacc")
            for kb in range(CT):
                pt_ = c.psums[5]
                et(p, (lambda tt, o=pt_[0:128, 0:128],
                       i=ACC[:, kb * 128:(kb + 1) * 128], idf=identf.ap:
                       tt.transpose(o, i, idf)),
                   reads=(ACC, identf), writes=(pt_,), label=label + "_at")
                sl = slice(kb * T + b * S + tt_i * 128,
                           kb * T + b * S + (tt_i + 1) * 128)
                ev(p, (lambda v, o=c.XT[:, sl], x=c.XT[:, sl], a=pt_[0:128, 0:128]:
                       v.tensor_tensor(o, x, a, ALU.add)),
                   reads=(c.XT, pt_), writes=(c.XT,), label=label + "_aa")


def _pcol(v, ncols=None):
    v = np.asarray(v, dtype=np.float32).reshape(-1)
    n = v.shape[0]
    nb = (n + 127) // 128 if ncols is None else ncols
    out = np.zeros((128, nb), dtype=np.float32)
    for cb in range(min(nb, (n + 127) // 128)):
        seg = v[cb * 128:(cb + 1) * 128]
        out[: seg.shape[0], cb] = seg
    return out


def _ct(a):
    return np.ascontiguousarray(a, dtype=np.float32)


def _dram_ins(nc, spec):
    return {k: nc.dram_tensor(k, list(v.shape), F32, kind="ExternalInput")
            for k, v in spec.items()}


def _act_io(nc, names_in, names_out):
    ins = {n: nc.dram_tensor(n, [NB, D, S], F32, kind="ExternalInput")
           for n in names_in}
    outs = {n: nc.dram_tensor(n, [NB, D, S], F32, kind="ExternalOutput")
            for n in names_out}
    return ins, outs


def _run(nc, const_ins, act_ins_percore, core_ids=None):
    core_ids = core_ids if core_ids is not None else list(range(8))
    in_maps = []
    for ci in range(len(core_ids)):
        m = dict(const_ins)
        for k, arrs in act_ins_percore.items():
            m[k] = arrs[ci]
        in_maps.append(m)
    return run_bass_kernel_spmd(nc, in_maps, core_ids=core_ids).results


def emit_add_from_dram(p, c, dram, label="afd"):
    tmp = p.sb(label + "_t", [128, 2048], BF16)
    for b in range(NB):
        for ct in range(CT):
            p.dma("gpsimd", tmp.ap, dram.ap()[b, ct * 128:(ct + 1) * 128, :],
                  reads=(tmp,), writes=(tmp,), label=label + "_ld")
            sl = slice(ct * T + b * S, ct * T + (b + 1) * S)
            ev(p, (lambda v, o=c.XT[:, sl], x=c.XT[:, sl], a=tmp.ap:
                   v.tensor_tensor(o, x, a, ALU.add)),
               reads=(c.XT, tmp), writes=(c.XT,), label=label + "_add")


def _host_prep(inputs):
    NC = 8
    x = np.asarray(inputs["L0_attn_x"])
    xT = np.ascontiguousarray(x.transpose(0, 2, 1))
    xs = [np.ascontiguousarray(xT[2 * ci:2 * ci + 2]) for ci in range(NC)]

    W = {k: np.asarray(v) for k, v in inputs.items()}
    l0 = {
        "Wx": _ct(W["L0_attn_Wx"]), "Wy": _ct(W["L0_attn_Wy"]),
        "Wi": _ct(W["L0_attn_Wi"]), "Wa": _ct(W["L0_attn_Wa"]),
        "Wo": _ct(W["L0_attn_Wo"]),
        "n1p": _pcol(W["L0_attn_norm1_w"]),
        "cwp": np.concatenate([_pcol(W["L0_attn_conv_w"][k]) for k in range(4)], 1),
        "cbp": _pcol(W["L0_attn_conv_b"]),
        "lamp": _pcol(W["L0_attn_lam"]),
        "Wg": _ct(W["L0_attn_Wg"]), "Wu": _ct(W["L0_attn_Wu"]),
        "Wd": _ct(W["L0_attn_Wd"]), "n2p": _pcol(W["L0_attn_norm2_w"]),
    }
    l0f = {"swap64": l0f_swap_host()}
    l0f.update(l0f_tabs_host())
    l0f.update(l0f_masks_host())
    for l in range(2):
        l0f[f"wqkv{l}"] = _ct(W["L0_ffn_wqkv"][l])
        l0f[f"wo{l}"] = _ct(W["L0_ffn_wo"][l])
        l0f[f"qs{l}"] = _ct(np.concatenate([W["L0_ffn_q_scale"][l]] * 2))[:, None]
        l0f[f"ks{l}"] = _ct(np.concatenate([W["L0_ffn_k_scale"][l]] * 2))[:, None]
        l0f[f"lng{l}"] = _pcol(W["L0_ffn_ln_g"][l])
        l0f[f"lnb{l}"] = _pcol(W["L0_ffn_ln_b"][l])
        l0f[f"w1_{l}"] = _ct(W["L0_ffn_w1"][l])
        l0f[f"b1a{l}"] = _pcol(W["L0_ffn_b1"][l][:2730], 22)
        l0f[f"b1g{l}"] = _pcol(W["L0_ffn_b1"][l][2730:], 22)
        l0f[f"w2_{l}"] = _ct(W["L0_ffn_w2"][l])
        l0f[f"b2{l}"] = _pcol(W["L0_ffn_b2"][l])
    l1 = {
        "wq": _ct(W["L1_attn_Wq"].T), "wk": _ct(W["L1_attn_Wk"].T),
        "wv": _ct(W["L1_attn_Wv"].T), "wo": _ct(W["L1_attn_Wo"].T),
        "win": _ct(W["L1_ffn_Win"]), "bin": _ct(W["L1_ffn_bin_"])[:, None],
        "w1": _ct(W["L1_ffn_W1"]), "b1": _ct(W["L1_ffn_b1"]),
        "w2": _ct(W["L1_ffn_W2"]), "b2": _ct(W["L1_ffn_b2"])[:, None],
        "wout": _ct(W["L1_ffn_Wout"]), "bout": _pcol(W["L1_ffn_bout"]),
    }
    l1.update(l1_masks_host())
    l2 = {
        "wq2": _ct(W["L2_attn_W_q"]), "wk2": _ct(W["L2_attn_W_k"]),
        "wv2": _ct(W["L2_attn_W_v"]), "wo2": _ct(W["L2_attn_W_o"]),
        "bq2": _pcol(W["L2_attn_b_q"]), "bk2": _pcol(W["L2_attn_b_k"]),
        "bv2": _pcol(W["L2_attn_b_v"]), "bo2": _pcol(W["L2_attn_b_o"]),
        "cwt2": _ct(W["L2_ffn_choice_w"].T),
    }
    for e in range(8):
        l2[f"w1_{e}"] = _ct(W["L2_ffn_w1"][e])
        l2[f"w2_{e}"] = _ct(W["L2_ffn_w2"][e])
    l3 = {
        "kvd": _ct(W["L3_attn_W_kv_down"]), "kup": _ct(W["L3_attn_W_k_up"]),
        "vup": _ct(W["L3_attn_W_v_up"]), "wq3": _ct(W["L3_attn_W_q"]),
        "wqr": _ct(W["L3_attn_W_q_rope"]), "wkr": _ct(W["L3_attn_W_k_rope"]),
        "wo3": _ct(W["L3_attn_W_o"]), "rope_t": l3_rope_host(),
        "wr": _ct(W["L3_ffn_Wr"]), "rb": _ct(W["L3_ffn_router_bias"]).reshape(1, 8),
        "cw3": np.concatenate([_pcol(W["L3_ffn_conv_w"][k, 0]) for k in range(4)], 1),
        "cb3": _pcol(W["L3_ffn_conv_b"]),
    }
    l3.update(l3_masks_host())
    for e in range(4):
        l3[f"wg{e}"] = _ct(W["L3_ffn_Wg"][e])
        l3[f"wu{e}"] = _ct(W["L3_ffn_Wu"][e])
        l3[f"wd{e}"] = _ct(W["L3_ffn_Wd"][e])
    return xs, l0, l0f, l1, l2, l3


def kernel_unfused(**inputs):
    xs, l0, l0f, l1, l2, l3 = _host_prep(inputs)

    nc = bass.Bass("TRN2", target_bir_lowering=False)
    ai, ao = _act_io(nc, ["x"], ["xn"])
    ao["s1"] = nc.dram_tensor("s1o", [128, T], F32, kind="ExternalOutput")
    with ExitStack() as stack:
        p = Prog(nc, stack)
        c = alloc_common(p, need_ident=False)
        S1 = p.sb("s1rep", [128, T], BF16)
        emit_load_xt(p, c.XT, ai["x"], queue="gpsimd")
        emit_ln(p, c, c.XT, c.XN, eps=1e-5, s1rep=S1, s1eps=1e-6)
        emit_store_xt(p, c.XN, ao["xn"], queue="gpsimd")
        p.dma("gpsimd", ao["s1"].ap(), S1.ap, reads=(S1,), label="sts1")
        p.finalize()
    r = _run(nc, {}, {"x": xs})
    xn_s = [np.asarray(rr["xn"]) for rr in r]
    s1_s = [np.asarray(rr["s1o"]) for rr in r]

    nc = bass.Bass("TRN2", target_bir_lowering=False)
    ai, ao = _act_io(nc, ["xn"], ["h1"])
    s1h = nc.dram_tensor("s1", [128, T], F32, kind="ExternalInput")
    wd = {"Wx": nc.dram_tensor("Wx", [D, D], F32, kind="ExternalInput"),
          "Wy": nc.dram_tensor("Wy", [D, D], F32, kind="ExternalInput"),
          "Wi": nc.dram_tensor("Wi", [D, D], F32, kind="ExternalInput"),
          "Wa": nc.dram_tensor("Wa", [D, D], F32, kind="ExternalInput"),
          "Wo": nc.dram_tensor("Wo", [D, D], F32, kind="ExternalInput"),
          "norm1_pcol": nc.dram_tensor("n1p", [128, CT], F32, kind="ExternalInput"),
          "conv_w_pcol": nc.dram_tensor("cwp", [128, 4 * CT], F32, kind="ExternalInput"),
          "conv_b_pcol": nc.dram_tensor("cbp", [128, CT], F32, kind="ExternalInput"),
          "lam_pcol": nc.dram_tensor("lamp", [128, CT], F32, kind="ExternalInput")}
    with ExitStack() as stack:
        p = Prog(nc, stack)
        p.init_consts()
        c = Ctx(); c.p = p
        c.XN = p.sb("XN", [128, CT * T], BF16)
        c.R = [p.sb(f"R{i}", [128, TCH], F32) for i in range(3)]
        c.psums = [p.ps(f"pp{i}", [128, 512], F32) for i in range(6)]
        c.prow = [p.ps(f"pr{i}", [1, 512], F32) for i in range(2)]
        S1 = p.sb("s1rep", [128, T], BF16)
        emit_load_xt(p, c.XN, ai["xn"], queue="gpsimd")
        p.dma("gpsimd", S1.ap, s1h.ap(), writes=(S1,), label="lds1")
        emit_l0_mix(p, c, wd, S1)
        emit_store_xt(p, c.XN, ao["h1"], queue="gpsimd")
        p.finalize()
    cst = {"Wx": l0["Wx"], "Wy": l0["Wy"], "Wi": l0["Wi"], "Wa": l0["Wa"],
           "Wo": l0["Wo"], "n1p": l0["n1p"], "cwp": l0["cwp"], "cbp": l0["cbp"],
           "lamp": l0["lamp"]}
    r = _run(nc, cst, {"xn": xn_s, "s1": s1_s})
    h1_s = [np.asarray(rr["h1"]) for rr in r]

    nc = bass.Bass("TRN2", target_bir_lowering=False)
    ai, ao = _act_io(nc, ["x", "h1"], ["xp0"])
    wd = {"Wg": nc.dram_tensor("Wg", [D, 4096], F32, kind="ExternalInput"),
          "Wu": nc.dram_tensor("Wu", [D, 4096], F32, kind="ExternalInput"),
          "Wd": nc.dram_tensor("Wd", [4096, D], F32, kind="ExternalInput"),
          "norm2_pcol": nc.dram_tensor("n2p", [128, CT], F32, kind="ExternalInput")}
    with ExitStack() as stack:
        p = Prog(nc, stack)
        c = alloc_common(p, nR=2, need_ident=False)
        emit_load_xt(p, c.XT, ai["x"], queue="gpsimd")
        emit_load_xt(p, c.XN, ai["h1"], queue="gpsimd")
        emit_l0_ffn_int(p, c, wd)
        emit_store_xt(p, c.XT, ao["xp0"], queue="gpsimd")
        p.finalize()
    r = _run(nc, {"Wg": l0["Wg"], "Wu": l0["Wu"], "Wd": l0["Wd"],
                  "n2p": l0["n2p"]}, {"x": xs, "h1": h1_s})
    xp0_s = [np.asarray(rr["xp0"]) for rr in r]

    nc = bass.Bass("TRN2", target_bir_lowering=False)
    ai, ao = _act_io(nc, ["xn"], ["f"])
    wd = {}
    for l in range(2):
        wd[f"wqkv{l}"] = nc.dram_tensor(f"wqkv{l}", [D, 1536], F32, kind="ExternalInput")
        wd[f"wo{l}"] = nc.dram_tensor(f"wo{l}", [512, D], F32, kind="ExternalInput")
        wd[f"qscale{l}"] = nc.dram_tensor(f"qs{l}", [128, 1], F32, kind="ExternalInput")
        wd[f"kscale{l}"] = nc.dram_tensor(f"ks{l}", [128, 1], F32, kind="ExternalInput")
        wd[f"lng{l}"] = nc.dram_tensor(f"lng{l}", [128, CT], F32, kind="ExternalInput")
        wd[f"lnb{l}"] = nc.dram_tensor(f"lnb{l}", [128, CT], F32, kind="ExternalInput")
        wd[f"w1_{l}"] = nc.dram_tensor(f"w1_{l}", [D, 5460], F32, kind="ExternalInput")
        wd[f"b1ap{l}"] = nc.dram_tensor(f"b1a{l}", [128, 22], F32, kind="ExternalInput")
        wd[f"b1gp{l}"] = nc.dram_tensor(f"b1g{l}", [128, 22], F32, kind="ExternalInput")
        wd[f"w2_{l}"] = nc.dram_tensor(f"w2_{l}", [2730, D], F32, kind="ExternalInput")
        wd[f"b2p{l}"] = nc.dram_tensor(f"b2{l}", [128, CT], F32, kind="ExternalInput")
    wd["swap64"] = nc.dram_tensor("swap64", [128, 64], F32, kind="ExternalInput")
    for nm in ("tab_qc", "tab_qs", "tab_kc", "tab_ks"):
        wd[nm] = nc.dram_tensor(nm, [128, S], F32, kind="ExternalInput")
    mh = {nm: nc.dram_tensor(nm, list(l0f[nm].shape), F32, kind="ExternalInput")
          for nm in ("l0A", "l0C")}
    with ExitStack() as stack:
        p = Prog(nc, stack)
        p.init_consts()
        c = Ctx(); c.p = p
        c.R = [p.sb(f"R{i}", [128, TCH], F32) for i in range(3)]
        c.sqc = p.sb("sqc", [128, 2 * TCH], BF16)
        c.psums = [p.ps(f"pp{i}", [128, 512], F32) for i in range(6)]
        c.prow = [p.ps(f"pr{i}", [1, 512], F32) for i in range(2)]
        c.ones_col_bf = p.sb("ones_col_bf", [128, 1], BF16)
        c.ones_row_f = p.sb("ones_row_f", [1, 128], F32)
        c.ones_row_bf = p.sb("ones_row_bf", [1, 128], BF16)
        c.recip_bf = p.sb("recip_bf", [128, 1], BF16)
        c.ident_bf = p.sb("ident_bf", [128, 128], BF16)
        g = "gpsimd"
        p.emit(g, lambda h: h.memset(c.ones_col_bf.ap, 1.0), writes=(c.ones_col_bf,), label="c1")
        p.emit(g, lambda h: h.memset(c.ones_row_f.ap, 1.0), writes=(c.ones_row_f,), label="c2")
        p.emit(g, lambda h: h.memset(c.ones_row_bf.ap, 1.0), writes=(c.ones_row_bf,), label="c2b")
        p.emit(g, lambda h: h.memset(c.recip_bf.ap, 1.0 / D), writes=(c.recip_bf,), label="c3")
        p.emit(g, lambda h: h.iota(c.ident_bf.ap, [[-1, 128]], base=0, channel_multiplier=1,
                                   allow_small_or_imprecise_dtypes=True),
               writes=(c.ident_bf,), label="idio")
        ev(p, lambda v: v.tensor_scalar(c.ident_bf.ap, c.ident_bf.ap, 0.0, None,
                                        ALU.is_equal),
           reads=(c.ident_bf,), writes=(c.ident_bf,), label="ideq")
        f = alloc_flash(p, c)
        F = alloc_f(p)
        masks = {nm: load_const_sb(p, nm, mh[nm], list(l0f[nm].shape))
                 for nm in ("l0A", "l0C")}
        emit_load_f(p, F, ai["xn"])
        for l in range(2):
            emit_l0f_attn(p, c, f, F, wd, masks, l, label=f"a{l}")
            emit_l0f_ffn(p, c, F, wd, l, label=f"f{l}")
        emit_store_f(p, F, ao["f"])
        p.finalize()
    cst = {k: l0f[k] for k in l0f}
    cst2 = {}
    for l in range(2):
        cst2[f"wqkv{l}"] = l0f[f"wqkv{l}"]; cst2[f"wo{l}"] = l0f[f"wo{l}"]
        cst2[f"qs{l}"] = l0f[f"qs{l}"]; cst2[f"ks{l}"] = l0f[f"ks{l}"]
        cst2[f"lng{l}"] = l0f[f"lng{l}"]; cst2[f"lnb{l}"] = l0f[f"lnb{l}"]
        cst2[f"w1_{l}"] = l0f[f"w1_{l}"]; cst2[f"b1a{l}"] = l0f[f"b1a{l}"]
        cst2[f"b1g{l}"] = l0f[f"b1g{l}"]; cst2[f"w2_{l}"] = l0f[f"w2_{l}"]
        cst2[f"b2{l}"] = l0f[f"b2{l}"]
    for nm in ("swap64", "tab_qc", "tab_qs", "tab_kc", "tab_ks", "l0A", "l0C"):
        cst2[nm] = l0f[nm]
    r = _run(nc, cst2, {"xn": xn_s})
    f_s = [np.asarray(rr["f"]) for rr in r]

    nc = bass.Bass("TRN2", target_bir_lowering=False)
    ai, ao = _act_io(nc, ["xp0", "f"], ["x2"])
    wdl1 = {nm: nc.dram_tensor(nm, [D, D], F32, kind="ExternalInput")
            for nm in ("wq", "wk", "wv", "wo")}
    fsh = {"win": [D, 64], "bin": [64, 1], "w1": [64, 16], "b1": [64, 16],
           "w2": [64, 16], "b2": [64, 1], "wout": [64, D], "bout": [128, 8]}
    fd = {nm: nc.dram_tensor("f_" + nm, sh, F32, kind="ExternalInput")
          for nm, sh in fsh.items()}
    mh = {nm: nc.dram_tensor("m_" + nm, list(l1[nm].shape), F32, kind="ExternalInput")
          for nm in ("a0", "a1", "cm")}
    with ExitStack() as stack:
        p = Prog(nc, stack)
        c = alloc_common(p)
        fl = alloc_flash(p, c)
        masks = {nm: load_const_sb(p, nm, mh[nm], list(l1[nm].shape))
                 for nm in ("a0", "a1", "cm")}
        emit_load_xt(p, c.XT, ai["xp0"], queue="gpsimd")
        emit_add_from_dram(p, c, ai["f"])
        emit_prenorm(p, c)
        emit_l1_attn(p, c, fl, {"q": wdl1["wq"], "k": wdl1["wk"], "v": wdl1["wv"], "o": wdl1["wo"]}, masks)
        emit_l1_ffn(p, c, {nm: fd[nm] for nm in fsh})
        emit_store_xt(p, c.XT, ao["x2"], queue="gpsimd")
        p.finalize()
    cst = {"wq": l1["wq"], "wk": l1["wk"], "wv": l1["wv"], "wo": l1["wo"],
           "m_a0": l1["a0"], "m_a1": l1["a1"], "m_cm": l1["cm"]}
    for nm in fsh:
        cst["f_" + nm] = l1[nm]
    r = _run(nc, cst, {"xp0": xp0_s, "f": f_s})
    x2_s = [np.asarray(rr["x2"]) for rr in r]

    nc = bass.Bass("TRN2", target_bir_lowering=False)
    ai, ao = _act_io(nc, ["x2"], ["xp2", "xn2"])
    wdl2 = {"q": nc.dram_tensor("wq2", [D, D], F32, kind="ExternalInput"),
            "k": nc.dram_tensor("wk2", [D, D], F32, kind="ExternalInput"),
            "v": nc.dram_tensor("wv2", [D, D], F32, kind="ExternalInput"),
            "o": nc.dram_tensor("wo2", [D, D], F32, kind="ExternalInput"),
            "b_q": nc.dram_tensor("bq2", [128, CT], F32, kind="ExternalInput"),
            "b_k": nc.dram_tensor("bk2", [128, CT], F32, kind="ExternalInput"),
            "b_v": nc.dram_tensor("bv2", [128, CT], F32, kind="ExternalInput"),
            "b_o": nc.dram_tensor("bo2", [128, CT], F32, kind="ExternalInput")}
    with ExitStack() as stack:
        p = Prog(nc, stack)
        c = alloc_common(p)
        fl = alloc_flash(p, c)
        emit_load_xt(p, c.XT, ai["x2"], queue="gpsimd")
        emit_prenorm(p, c)
        emit_l2_attn(p, c, fl, wdl2)
        emit_store_xt(p, c.XT, ao["xp2"], queue="gpsimd")
        emit_store_xt(p, c.XN, ao["xn2"], queue="gpsimd")
        p.finalize()
    cst = {"wq2": l2["wq2"], "wk2": l2["wk2"], "wv2": l2["wv2"], "wo2": l2["wo2"],
           "bq2": l2["bq2"], "bk2": l2["bk2"], "bv2": l2["bv2"], "bo2": l2["bo2"]}
    r = _run(nc, cst, {"x2": x2_s})
    xp2_s = [np.asarray(rr["xp2"]) for rr in r]
    xn2_s = [np.asarray(rr["xn2"]) for rr in r]

    nc = bass.Bass("TRN2", target_bir_lowering=False)
    ai, ao = _act_io(nc, ["xp2", "xn2"], ["x3"])
    wd = {"choice_wT": nc.dram_tensor("cwt2", [D, 8], F32, kind="ExternalInput"),
          "w1": [nc.dram_tensor(f"w1_{e}", [D, 2048], F32, kind="ExternalInput")
                 for e in range(8)],
          "w2": [nc.dram_tensor(f"w2_{e}", [2048, D], F32, kind="ExternalInput")
                 for e in range(8)]}
    with ExitStack() as stack:
        p = Prog(nc, stack)
        c = alloc_common(p, need_ident=False)
        emit_load_xt(p, c.XT, ai["xp2"], queue="gpsimd")
        emit_load_xt(p, c.XN, ai["xn2"], queue="gpsimd")
        emit_l2_ffn(p, c, wd, nc)
        emit_store_xt(p, c.XT, ao["x3"], queue="gpsimd")
        p.finalize()
    cst = {"cwt2": l2["cwt2"]}
    for e in range(8):
        cst[f"w1_{e}"] = l2[f"w1_{e}"]
        cst[f"w2_{e}"] = l2[f"w2_{e}"]
    r = _run(nc, cst, {"xp2": xp2_s, "xn2": xn2_s})
    x3_s = [np.asarray(rr["x3"]) for rr in r]

    nc = bass.Bass("TRN2", target_bir_lowering=False)
    ai, ao = _act_io(nc, ["x3"], ["xp3", "xn3"])
    wd = {"kv_down": nc.dram_tensor("kvd", [D, 128], F32, kind="ExternalInput"),
          "k_up": nc.dram_tensor("kup", [128, D], F32, kind="ExternalInput"),
          "v_up": nc.dram_tensor("vup", [128, D], F32, kind="ExternalInput"),
          "q": nc.dram_tensor("wq3", [D, D], F32, kind="ExternalInput"),
          "q_rope": nc.dram_tensor("wqr", [D, 32], F32, kind="ExternalInput"),
          "k_rope": nc.dram_tensor("wkr", [128, 32], F32, kind="ExternalInput"),
          "o": nc.dram_tensor("wo3", [D, D], F32, kind="ExternalInput"),
          "rope_t": nc.dram_tensor("rope_t", [128, 2 * S], F32, kind="ExternalInput")}
    mh = {nm: nc.dram_tensor(nm, [128, 896], F32, kind="ExternalInput")
          for nm in ("camA", "camC")}
    with ExitStack() as stack:
        p = Prog(nc, stack)
        c = alloc_common(p)
        fl = alloc_flash(p, c)
        masks = {nm: load_const_sb(p, nm, mh[nm], [128, 896]) for nm in mh}
        emit_load_xt(p, c.XT, ai["x3"], queue="gpsimd")
        emit_prenorm(p, c)
        emit_l3_attn(p, c, fl, wd, masks)
        emit_store_xt(p, c.XT, ao["xp3"], queue="gpsimd")
        emit_store_xt(p, c.XN, ao["xn3"], queue="gpsimd")
        p.finalize()
    cst = {"kvd": l3["kvd"], "kup": l3["kup"], "vup": l3["vup"], "wq3": l3["wq3"],
           "wqr": l3["wqr"], "wkr": l3["wkr"], "wo3": l3["wo3"],
           "rope_t": l3["rope_t"], "camA": l3["camA"], "camC": l3["camC"]}
    r = _run(nc, cst, {"x3": x3_s})
    xp3_s = [np.asarray(rr["xp3"]) for rr in r]
    xn3_s = [np.asarray(rr["xn3"]) for rr in r]

    nc = bass.Bass("TRN2", target_bir_lowering=False)
    ai, ao = _act_io(nc, ["xp3", "xn3"], ["y"])
    wd = {"Wr": nc.dram_tensor("wr", [D, 8], F32, kind="ExternalInput"),
          "router_bias": nc.dram_tensor("rb", [1, 8], F32, kind="ExternalInput"),
          "conv_w": nc.dram_tensor("cw3", [128, 32], F32, kind="ExternalInput"),
          "conv_b": nc.dram_tensor("cb3", [128, CT], F32, kind="ExternalInput")}
    for e in range(4):
        wd[f"Wg{e}"] = nc.dram_tensor(f"wg{e}", [D, 512], F32, kind="ExternalInput")
        wd[f"Wu{e}"] = nc.dram_tensor(f"wu{e}", [D, 512], F32, kind="ExternalInput")
        wd[f"Wd{e}"] = nc.dram_tensor(f"wd{e}", [512, D], F32, kind="ExternalInput")
    with ExitStack() as stack:
        p = Prog(nc, stack)
        c = alloc_common(p)
        emit_load_xt(p, c.XT, ai["xp3"], queue="gpsimd")
        emit_load_xt(p, c.XN, ai["xn3"], queue="gpsimd")
        emit_l3_ffn(p, c, wd)
        emit_store_xt(p, c.XT, ao["y"], queue="gpsimd")
        p.finalize()
    cst = {"wr": l3["wr"], "rb": l3["rb"], "cw3": l3["cw3"], "cb3": l3["cb3"]}
    for e in range(4):
        cst[f"wg{e}"] = l3[f"wg{e}"]
        cst[f"wu{e}"] = l3[f"wu{e}"]
        cst[f"wd{e}"] = l3[f"wd{e}"]
    r = _run(nc, cst, {"xp3": xp3_s, "xn3": xn3_s})
    y_s = [np.asarray(rr["y"]) for rr in r]

    out = np.concatenate([y.transpose(0, 2, 1) for y in y_s], axis=0)
    return np.ascontiguousarray(out.astype(np.float32))


def _mk_alias():
    A = {}

    def a(name, pool, c0, w):
        A[name] = (pool, c0, w)

    a("XN", "PXN", 0, 32768)
    a("Fstream", "PXN", 0, 32784)
    a("fl_qT", "PFL", 0, 2048); a("fl_kT", "PFL", 2048, 2048)
    a("fl_vx", "PFL", 4096, 2080); a("fl_ncol", "PFL", 6176, 16)
    a("fl_ob", "PFL", 6192, 512); a("fl_pt0", "PFL", 6704, 512)
    a("fl_pt1", "PFL", 7216, 512)
    a("lfa_sqb", "PFL", 7728, 2048); a("lfa_qr2", "PFL", 9776, 2048)
    a("fl_att", "PFL", 11824, 2048); a("fl_nrow", "PFL", 13872, 2048)
    a("s1rep", "PFL", 0, 4096)
    a("l0muc", "PFL", 0, 2048); a("l0mxb", "PFL", 2048, 2080)
    a("l0mcv", "PFL", 4128, 2048); a("l0mrs", "PFL", 6176, 2048)
    a("l0mis", "PFL", 8224, 2048); a("l0myb", "PFL", 10272, 2048)
    a("l0mtmp", "PFL", 12320, 256)
    a("l0iuc", "PFL", 0, 4096); a("l0igh", "PFL", 4096, 2048)
    a("l0is2", "PFL", 8192, 4096)
    a("lff_lnh", "PFL", 0, 8192)
    a("lff_ah", "PFL", 8192, 2048); a("lff_gh", "PFL", 10240, 2048)
    a("l1fss", "PFL", 0, 2048); a("l1fh", "PFL", 2048, 2048)
    a("l1fg", "PFL", 4096, 2048); a("l1fob", "PFL", 6144, 512)
    a("cst_a0", "PFL", 7728, 128); a("cst_a1", "PFL", 7856, 256)
    a("cst_cm", "PFL", 8112, 1024)
    a("l2fxe", "PFL", 0, 4096); a("l2fhe", "PFL", 4096, 2048)
    a("l2fxtok", "PFL", 6144, 1024); a("l2fybb", "PFL", 7168, 1024)
    a("l2fjk", "PFL", 8192, 2048)
    a("l3aqcat", "PFL", 7728, 2048); a("l3akcat", "PFL", 9776, 2048)
    a("l3fcwrep", "PFL", 0, 4096); a("l3fgs", "PFL", 4096, 2048)
    a("l3fus", "PFL", 6144, 2048); a("l3fcvb", "PFL", 8192, 2048)
    a("l3fcwr", "PFL", 10240, 4096)
    a("l0mpre", "PW", 0, 8192)
    a("l0mwsl0", "PW", 0, 4096); a("l0mwsl1", "PW", 4096, 4096)
    a("l0iwgb", "PW", 0, 4096); a("l0iwub", "PW", 4096, 4096)
    a("l0iwdb", "PW", 8192, 4096)
    for l in range(2):
        a(f"cst_l0ftqc{l}", "PW", 0, 2048)
        a(f"cst_l0ftqs{l}", "PW", 2048, 2048)
        a(f"cst_l0ftkc{l}", "PW", 4096, 2048)
        a(f"cst_l0ftks{l}", "PW", 6144, 2048)
        a(f"cst_l0A_{l}", "PW", 8192, 1152)
        a(f"cst_l0C_{l}", "PW", 9344, 1536)
    a("lfa_wq", "PW", 10880, 1024); a("lfa_wk", "PW", 11904, 1024)
    a("lfa_wv", "PW", 12928, 1024); a("lfa_wo", "PW", 13952, 1024)
    a("lff_w1a", "PW", 0, 4096); a("lff_w1g", "PW", 4096, 4096)
    a("lff_w2", "PW", 8192, 4096)
    for i in range(2):
        a(f"l1wq{i}", "PW", 0 + 1024 * i, 1024)
        a(f"l1wk{i}", "PW", 2048 + 1024 * i, 1024)
        a(f"l1wv{i}", "PW", 4096 + 1024 * i, 1024)
        a(f"l1wo{i}", "PW", 6144 + 1024 * i, 1024)
    a("w_l1fwin", "PW", 8192, 512); a("l1fwout", "PW", 8704, 1024)
    a("l2awq", "PW", 0, 1024); a("l2awk", "PW", 1024, 1024)
    a("l2awv", "PW", 2048, 1024); a("l2awo", "PW", 3072, 1024)
    a("l2akvs", "PW", 4096, 66)
    a("l2fw1b", "PW", 0, 4096); a("l2fw2b", "PW", 4096, 4096)
    a("w_l3akvd", "PW", 0, 1024); a("l3awq0", "PW", 1024, 1024)
    a("l3awku0", "PW", 2048, 128); a("l3awvu0", "PW", 2176, 128)
    a("l3awo0", "PW", 2304, 1024); a("w_l3awqr", "PW", 3328, 256)
    a("l3awkr", "PW", 3584, 32)
    a("cst_camA", "PW", 3616, 896); a("cst_camC", "PW", 4512, 896)
    a("l3aqrot", "PW", 5408, 4096); a("l3akrot", "PW", 9504, 4096)
    a("l3ackv", "PW", 13600, 2048)
    a("l3fwg", "PW", 0, 4096); a("l3fwu", "PW", 4096, 4096)
    a("l3fwd", "PW", 8192, 4096)
    a("R3", "PF32", 0, 512); a("fl_ms", "PF32", 0, 512)
    a("fl_mcol", "PF32", 512, 4)
    a("R0", "PF32", 516, 512); a("R1", "PF32", 1028, 512)
    a("R2", "PF32", 1540, 512)
    a("l2fp2", "PF32", 0, 2048); a("l2facc", "PF32", 0, 1024)
    a("l2fpbm", "PF32", 2052, 128); a("l2fbidx", "PF32", 2180, 128)
    a("l2fgix", "PF32", 2308, 128); a("l2fids", "PF32", 2436, 4)
    a("l2fiotc", "PF32", 2440, 1); a("l2flo", "PF32", 2441, 6)
    a("l2f_stg", "PF32", 2447, 128)
    a("l2fidf", "PF32", 2575, 128); a("l3fidf", "PF32", 2575, 128)
    a("l3fcw", "PF32", 0, 256); a("l3fscb", "PF32", 2052, 8)
    a("l3frb", "PF32", 2060, 8)
    return A


def kernel(**inputs):
    xs, l0, l0f, l1, l2, l3 = _host_prep(inputs)

    nc = bass.Bass("TRN2", target_bir_lowering=False)
    ei = lambda n, sh: nc.dram_tensor(n, sh, F32, kind="ExternalInput")
    x_in = ei("x", [NB, D, S])
    y_out = nc.dram_tensor("y", [NB, D, S], F32, kind="ExternalOutput")
    xn_dram = nc.dram_tensor("xnb", [NB, D, S], BF16, kind="Internal")
    s1_dram = nc.dram_tensor("s1b", [128, T], BF16, kind="Internal")

    wd_l0m = {nm: ei(nm, [D, D]) for nm in ("Wx", "Wy", "Wi", "Wa", "Wo")}
    wd_l0m["norm1_pcol"] = ei("n1p", [128, CT])
    wd_l0m["conv_w_pcol"] = ei("cwp", [128, 4 * CT])
    wd_l0m["conv_b_pcol"] = ei("cbp", [128, CT])
    wd_l0m["lam_pcol"] = ei("lamp", [128, CT])
    wd_l0i = {"Wg": ei("Wg", [D, 4096]), "Wu": ei("Wu", [D, 4096]),
              "Wd": ei("Wd", [4096, D]), "norm2_pcol": ei("n2p", [128, CT])}
    wd_l0f = {}
    for l in range(2):
        wd_l0f[f"wqkv{l}"] = ei(f"wqkv{l}", [D, 1536])
        wd_l0f[f"wo{l}"] = ei(f"wo{l}", [512, D])
        wd_l0f[f"qscale{l}"] = ei(f"qs{l}", [128, 1])
        wd_l0f[f"kscale{l}"] = ei(f"ks{l}", [128, 1])
        wd_l0f[f"lng{l}"] = ei(f"lng{l}", [128, CT])
        wd_l0f[f"lnb{l}"] = ei(f"lnb{l}", [128, CT])
        wd_l0f[f"w1_{l}"] = ei(f"w1_{l}", [D, 5460])
        wd_l0f[f"b1ap{l}"] = ei(f"b1a{l}", [128, 22])
        wd_l0f[f"b1gp{l}"] = ei(f"b1g{l}", [128, 22])
        wd_l0f[f"w2_{l}"] = ei(f"w2_{l}", [2730, D])
        wd_l0f[f"b2p{l}"] = ei(f"b2{l}", [128, CT])
    wd_l0f["swap64"] = ei("swap64", [128, 64])
    for nm in ("tab_qc", "tab_qs", "tab_kc", "tab_ks"):
        wd_l0f[nm] = ei(nm, [128, S])
    mh_l0f = {nm: ei(nm, list(l0f[nm].shape)) for nm in ("l0A", "l0C")}
    wd_l1a = {"q": ei("wq", [D, D]), "k": ei("wk", [D, D]),
              "v": ei("wv", [D, D]), "o": ei("wo", [D, D])}
    fsh = {"win": [D, 64], "bin": [64, 1], "w1": [64, 16], "b1": [64, 16],
           "w2": [64, 16], "b2": [64, 1], "wout": [64, D], "bout": [128, 8]}
    fd_l1f = {nm: ei("f_" + nm, sh) for nm, sh in fsh.items()}
    mh_l1 = {nm: ei("m_" + nm, list(l1[nm].shape)) for nm in ("a0", "a1", "cm")}
    wd_l2a = {"q": ei("wq2", [D, D]), "k": ei("wk2", [D, D]),
              "v": ei("wv2", [D, D]), "o": ei("wo2", [D, D]),
              "b_q": ei("bq2", [128, CT]), "b_k": ei("bk2", [128, CT]),
              "b_v": ei("bv2", [128, CT]), "b_o": ei("bo2", [128, CT])}
    wd_l2f = {"choice_wT": ei("cwt2", [D, 8]),
              "w1": [ei(f"e1_{e}", [D, 2048]) for e in range(8)],
              "w2": [ei(f"e2_{e}", [2048, D]) for e in range(8)]}
    wd_l3a = {"kv_down": ei("kvd", [D, 128]), "k_up": ei("kup", [128, D]),
              "v_up": ei("vup", [128, D]), "q": ei("wq3", [D, D]),
              "q_rope": ei("wqr", [D, 32]), "k_rope": ei("wkr", [128, 32]),
              "o": ei("wo3", [D, D]), "rope_t": ei("rope_t", [128, 2 * S])}
    mh_l3 = {nm: ei(nm, [128, 896]) for nm in ("camA", "camC")}
    wd_l3f = {"Wr": ei("wr", [D, 8]), "router_bias": ei("rb", [1, 8]),
              "conv_w": ei("cw3", [128, 32]), "conv_b": ei("cb3", [128, CT])}
    for e in range(4):
        wd_l3f[f"Wg{e}"] = ei(f"wg{e}", [D, 512])
        wd_l3f[f"Wu{e}"] = ei(f"wu{e}", [D, 512])
        wd_l3f[f"Wd{e}"] = ei(f"wd{e}", [512, D])

    with ExitStack() as stack:
        p = Prog(nc, stack)
        p.mkpool("PXN", 32784, BF16)
        p.mkpool("PFL", 15920, BF16)
        p.mkpool("PW", 15648, BF16)
        p.mkpool("PF32", 2703, F32)
        p.alias = _mk_alias()
        c = alloc_common(p)
        fl = alloc_flash(p, c)
        F = alloc_f(p)
        S1 = p.sb("s1rep", [128, T], BF16)
        emit_load_xt(p, c.XT, x_in, queue="gpsimd")
        emit_ln(p, c, c.XT, c.XN, eps=1e-5, s1rep=S1, s1eps=1e-6)
        emit_store_xt(p, c.XN, xn_dram, queue="sync")
        p.dma("sync", s1_dram.ap(), S1.ap, reads=(S1,), label="sts1")
        emit_l0_mix(p, c, wd_l0m, None, chw=256, stream_w=True, nc=nc,
                    s1_dram=s1_dram)
        emit_l0_ffn_int(p, c, wd_l0i)
        emit_load_f(p, F, xn_dram)
        for l in range(2):
            masks = {nm: load_const_sb(p, f"{nm}_{l}", mh_l0f[nm],
                                       list(l0f[nm].shape))
                     for nm in ("l0A", "l0C")}
            emit_l0f_attn(p, c, fl, F, wd_l0f, masks, l, label=f"a{l}")
            emit_l0f_ffn(p, c, F, wd_l0f, l, label=f"f{l}")
        for b in range(NB):
            for ct in range(CT):
                sl = slice(ct * T + b * S, ct * T + (b + 1) * S)
                ev(p, (lambda v, o=c.XT[:, sl], x2=c.XT[:, sl],
                       aa=F[:, fcol(ct, b, 0): fcol(ct, b, 0) + S]:
                       v.tensor_tensor(o, x2, aa, ALU.add)),
                   reads=(c.XT, F), writes=(c.XT,), label="addF")
        emit_prenorm(p, c)
        masks1 = {nm: load_const_sb(p, nm, mh_l1[nm], list(l1[nm].shape))
                  for nm in ("a0", "a1", "cm")}
        emit_l1_attn(p, c, fl, wd_l1a, masks1)
        emit_l1_ffn(p, c, fd_l1f)
        emit_prenorm(p, c)
        emit_l2_attn(p, c, fl, wd_l2a)
        emit_l2_ffn(p, c, wd_l2f, nc)
        emit_prenorm(p, c)
        masks3 = {nm: load_const_sb(p, nm, mh_l3[nm], [128, 896])
                  for nm in mh_l3}
        emit_l3_attn(p, c, fl, wd_l3a, masks3)
        emit_l3_ffn(p, c, wd_l3f)
        emit_store_xt(p, c.XT, y_out, queue="gpsimd")
        p.finalize()

    cst = {"Wx": l0["Wx"], "Wy": l0["Wy"], "Wi": l0["Wi"], "Wa": l0["Wa"],
           "Wo": l0["Wo"], "n1p": l0["n1p"], "cwp": l0["cwp"],
           "cbp": l0["cbp"], "lamp": l0["lamp"],
           "Wg": l0["Wg"], "Wu": l0["Wu"], "Wd": l0["Wd"], "n2p": l0["n2p"]}
    for l in range(2):
        for k in ("wqkv", "wo", "qs", "ks", "lng", "lnb", "b2"):
            cst[f"{k}{l}"] = l0f[f"{k}{l}"]
        for k in ("w1_", "w2_"):
            cst[f"{k}{l}"] = l0f[f"{k}{l}"]
        cst[f"b1a{l}"] = l0f[f"b1a{l}"]
        cst[f"b1g{l}"] = l0f[f"b1g{l}"]
    for nm in ("swap64", "tab_qc", "tab_qs", "tab_kc", "tab_ks", "l0A", "l0C"):
        cst[nm] = l0f[nm]
    cst.update({"wq": l1["wq"], "wk": l1["wk"], "wv": l1["wv"], "wo": l1["wo"],
                "m_a0": l1["a0"], "m_a1": l1["a1"], "m_cm": l1["cm"]})
    for nm in fsh:
        cst["f_" + nm] = l1[nm]
    cst.update({"wq2": l2["wq2"], "wk2": l2["wk2"], "wv2": l2["wv2"],
                "wo2": l2["wo2"], "bq2": l2["bq2"], "bk2": l2["bk2"],
                "bv2": l2["bv2"], "bo2": l2["bo2"], "cwt2": l2["cwt2"]})
    for e in range(8):
        cst[f"e1_{e}"] = l2[f"w1_{e}"]
        cst[f"e2_{e}"] = l2[f"w2_{e}"]
    cst.update({"kvd": l3["kvd"], "kup": l3["kup"], "vup": l3["vup"],
                "wq3": l3["wq3"], "wqr": l3["wqr"], "wkr": l3["wkr"],
                "wo3": l3["wo3"], "rope_t": l3["rope_t"],
                "camA": l3["camA"], "camC": l3["camC"],
                "wr": l3["wr"], "rb": l3["rb"], "cw3": l3["cw3"],
                "cb3": l3["cb3"]})
    for e in range(4):
        cst[f"wg{e}"] = l3[f"wg{e}"]
        cst[f"wu{e}"] = l3[f"wu{e}"]
        cst[f"wd{e}"] = l3[f"wd{e}"]

    r = _run(nc, cst, {"x": xs})
    y_s = [np.asarray(rr["y"]) for rr in r]
    out = np.concatenate([yy.transpose(0, 2, 1) for yy in y_s], axis=0)
    return np.ascontiguousarray(out.astype(np.float32))


# version 06
# speedup vs baseline: 1.1334x; 1.1334x over previous
import numpy as np
import ml_dtypes
import concourse.bass as bass
import concourse.mybir as mybir
from concourse.bass_utils import run_bass_kernel_spmd
from contextlib import ExitStack

F32 = mybir.dt.float32
BF16 = mybir.dt.bfloat16
U32 = mybir.dt.uint32
AF = mybir.ActivationFunctionType
ALU = mybir.AluOpType
AX = mybir.AxisListType
IOA = bass.IndirectOffsetOnAxis

D = 1024
S = 2048
NB = 2
T = NB * S
CT = D // 128
TCH = 512

ENGINES = ("tensor", "vector", "scalar", "gpsimd", "sync")
SAFE_SAME = {"tensor"}


class Op:
    __slots__ = ("eng", "fn", "is_dma", "deps", "inc", "cum", "dmacum", "label")

    def __init__(self, eng, fn, is_dma, label):
        self.eng, self.fn, self.is_dma, self.label = eng, fn, is_dma, label
        self.deps = []
        self.inc = False
        self.cum = 0
        self.dmacum = 0


class Buf:
    __slots__ = ("t", "name", "last_w", "readers")

    def __init__(self, t, name):
        self.t, self.name = t, name
        self.last_w = None
        self.readers = []

    def __getitem__(self, sl):
        return self.t[sl]

    @property
    def ap(self):
        return self.t[:]


class SubBuf:
    __slots__ = ("parent", "c0", "width", "pp", "name", "last_w", "readers",
                 "overlaps")

    def __init__(self, parent, c0, width, pp, name):
        self.parent, self.c0, self.width, self.pp = parent, c0, width, pp
        self.name = name
        self.last_w = None
        self.readers = []
        self.overlaps = []

    def __getitem__(self, sl):
        if not isinstance(sl, tuple):
            return self.parent.t[sl, self.c0:self.c0 + self.width]
        ps, fs = sl
        f0 = 0 if fs.start is None else fs.start
        f1 = self.width if fs.stop is None else fs.stop
        return self.parent.t[ps, slice(f0 + self.c0, f1 + self.c0)]

    @property
    def ap(self):
        return self.parent.t[0:self.pp, self.c0:self.c0 + self.width]


class Prog:
    def __init__(self, nc, stack):
        self.nc, self.stack = nc, stack
        self.ops = {e: [] for e in ENGINES}
        self.alias = {}
        self.pools = {}

    def mkpool(self, name, cols, dt=BF16):
        self.pools[name] = Buf(self.stack.enter_context(
            self.nc.sbuf_tensor(name, [128, cols], dt)), name)
        return self.pools[name]

    def sb(self, name, shape, dt=F32):
        if name in self.alias:
            if not hasattr(self, "_aviews"):
                self._aviews = {}
            if name in self._aviews:
                v = self._aviews[name]
                assert shape[1] <= v.width, f"alias {name} resized"
                return v
            pool, c0, w = self.alias[name]
            b = self.pools[pool]
            assert c0 + shape[1] <= b.t.shape[1], \
                f"alias {name}: [{c0},{c0 + shape[1]}) exceeds pool {pool}"
            assert shape[1] <= w, f"alias {name} too small: {shape[1]} > {w}"
            assert b.t.dtype == dt, f"alias {name}: dtype mismatch vs pool {pool}"
            v = SubBuf(b, c0, shape[1], shape[0], name)
            for o in self._aviews.values():
                if o.parent is b and o.c0 < c0 + shape[1] and c0 < o.c0 + o.width:
                    o.overlaps.append(v)
                    v.overlaps.append(o)
            self._aviews[name] = v
            return v
        return Buf(self.stack.enter_context(self.nc.sbuf_tensor(name, list(shape), dt)), name)

    def init_consts(self, ncols=12):
        self.cpool = self.sb("cpool", [128, ncols], F32)
        self.cvals = {}

    def constf(self, val):
        val = float(val)
        if val not in self.cvals:
            j = len(self.cvals)
            self.cvals[val] = j
            ap = self.cpool[:, j:j + 1]
            self.nc.const_aps.register(F32, val, ap)
            self.emit("gpsimd", lambda h, a=ap, v=val: h.memset(a, v),
                      writes=(self.cpool,), label=f"const_{val}")
        return self.cpool

    def sb_shared(self, name, shape, dt=F32):
        if not hasattr(self, "_shared"):
            self._shared = {}
        if name not in self._shared:
            self._shared[name] = self.sb(name, shape, dt)
        return self._shared[name]

    def ps(self, name, shape, dt=F32):
        return Buf(self.stack.enter_context(self.nc.psum_tensor(name, list(shape), dt)), name)

    def emit(self, eng, fn, reads=(), writes=(), dma=False, label="", sreads=()):
        op = Op(eng, fn, dma, label)
        self.ops[eng].append(op)

        def dep(p, force=False):
            if p is None or p is op:
                return
            if (not force) and (not p.is_dma) and (not op.is_dma)                     and p.eng == eng and eng in SAFE_SAME:
                return
            if p not in op.deps:
                op.deps.append(p)

        for r in sreads:
            if r is None:
                continue
            dep(r.last_w, force=True)
            for ov in getattr(r, "overlaps", ()):
                dep(ov.last_w, force=True)
            r.readers.append(op)
        for r in reads:
            if r is None:
                continue
            dep(r.last_w)
            for ov in getattr(r, "overlaps", ()):
                dep(ov.last_w)
            r.readers.append(op)
        for w in writes:
            if w is None:
                continue
            dep(w.last_w)
            for rd in w.readers:
                dep(rd)
            for ov in getattr(w, "overlaps", ()):
                dep(ov.last_w)
                for rd in ov.readers:
                    dep(rd)
            w.last_w = op
            w.readers = []
        return op

    def dma(self, queue, out, in_, reads=(), writes=(), label="dma", **kw):
        return self.emit(queue, lambda e: e.dma_start(out, in_, **kw),
                         reads=reads, writes=writes, dma=True, label=label)

    def finalize(self):
        nc = self.nc
        for e in ENGINES:
            for o in self.ops[e]:
                if o.is_dma:
                    o.inc = True
                for pdep in o.deps:
                    pdep.inc = True
        for e in ENGINES:
            c = d = 0
            for o in self.ops[e]:
                if o.is_dma:
                    d += 16
                    o.dmacum = d
                elif o.inc:
                    c += 1
                o.cum = c
        esem = {e: self.stack.enter_context(nc.semaphore(f"p_{e}"))
                for e in ENGINES if any((not o.is_dma) and o.inc for o in self.ops[e])}
        dsem = {e: self.stack.enter_context(nc.semaphore(f"d_{e}"))
                for e in ENGINES if any(o.is_dma for o in self.ops[e])}
        block = self.stack.enter_context(nc.Block())
        handles = {"tensor": nc.tensor, "vector": nc.vector, "scalar": nc.scalar,
                   "gpsimd": nc.gpsimd, "sync": nc.sync}

        dma_final = {e: 16 * sum(1 for o in self.ops[e] if o.is_dma)
                     for e in dsem}
        term_eng = next((e for e in ("sync", "gpsimd", "vector", "scalar",
                                     "tensor") if self.ops[e]), None)

        def make_replay(eng):
            def replay(_h):
                h = handles[eng]
                waited = {}
                for o in self.ops[eng]:
                    for pd in o.deps:
                        if pd.is_dma:
                            key, val, s = ("d", pd.eng), pd.dmacum, dsem[pd.eng]
                        else:
                            key, val, s = ("e", pd.eng), pd.cum, esem[pd.eng]
                        if waited.get(key, 0) >= val:
                            continue
                        h.wait_ge(s, val)
                        waited[key] = val
                    inst = o.fn(h)
                    if o.is_dma:
                        inst.then_inc(dsem[eng], 16)
                    elif o.inc:
                        if inst is None:
                            raise RuntimeError(f"op '{o.label}' on {eng} needs inc but returned None")
                        inst.then_inc(esem[eng], 1)
                if eng == term_eng:
                    for q, fin in dma_final.items():
                        if fin > waited.get(("d", q), 0):
                            h.wait_ge(dsem[q], fin)
            return replay

        for eng in ENGINES:
            if self.ops[eng]:
                getattr(block, eng)(make_replay(eng))


def eact(p, out_ap, in_ap, func, bias=None, scale=None, accum=None,
         reads=(), writes=(), label="act"):
    kw = {}
    reads = list(reads)
    if bias is not None:
        kw["bias"] = bias
        if isinstance(bias, (int, float)) and float(bias) not in (0.0,):
            reads.append(p.constf(bias))
    if scale is not None:
        kw["scale"] = scale
        if isinstance(scale, (int, float)) and float(scale) not in (0.0, 1.0):
            reads.append(p.constf(scale))
    if accum is not None:
        kw["accum_out"] = accum
    return p.emit("scalar", lambda s: s.activation(out_ap, in_ap, func, **kw),
                  reads=reads, writes=writes, label=label)


def ev(p, fn, reads=(), writes=(), label="vec"):
    return p.emit("vector", fn, reads=reads, writes=writes, label=label)


def ev2(p, fn, reads=(), sreads=(), writes=(), label="vec"):
    return p.emit("vector", fn, reads=reads, sreads=sreads, writes=writes, label=label)


def et(p, fn, reads=(), writes=(), label="mm"):
    return p.emit("tensor", fn, reads=reads, writes=writes, label=label)


class Weights:

    def __init__(self, buf, K, N, stride=None):
        self.buf, self.K, self.N = buf, K, N
        self.stride = stride if stride is not None else N
        self.KB = (K + 127) // 128

    def tile(self, kb, n0, nw):
        return self.buf[:, kb * self.stride + n0: kb * self.stride + n0 + nw]


def load_weight(p, name, dram_t, K, N, queue="gpsimd", buf=None):
    KB = (K + 127) // 128
    if buf is None:
        buf = p.sb(f"w_{name}", [128, KB * N], BF16)
    if K % 128 != 0:
        raise ValueError(f"K={K} not multiple of 128 for {name}")
    ap_in = dram_t.ap().rearrange("(kb q) n -> q kb n", q=128)
    ap_out = buf.ap.rearrange("q (kb n) -> q kb n", kb=KB)
    p.dma(queue, ap_out, ap_in, writes=(buf,), label=f"ldw_{name}")
    return Weights(buf, K, N)


def load_weight_cols(p, wbuf, dram_t, K, n0, nw, queue="gpsimd", name="w",
                     stride=None):
    KB = K // 128
    ap_in = dram_t.ap()[:, n0:n0 + nw].rearrange("(kb q) n -> q kb n", q=128)
    ap_out = wbuf.ap.rearrange("q (kb n) -> q kb n", kb=KB)[:, :, 0:nw]
    p.dma(queue, ap_out, ap_in, reads=(wbuf,), writes=(wbuf,), label=f"ldw_{name}")
    if stride is None:
        wcols = wbuf.width if isinstance(wbuf, SubBuf) else wbuf.t.shape[1]
        stride = wcols // KB
    return Weights(wbuf, K, nw, stride=stride)


def load_weight_rows(p, wbuf, dram_t, k0, kh, N, queue="gpsimd", name="w"):
    ap_in = dram_t.ap()[k0:k0 + kh, :]
    p.dma(queue, wbuf[0:kh, 0:N], ap_in, reads=(wbuf,), writes=(wbuf,),
          label=f"ldw_{name}")
    return Weights(wbuf, kh, N)


def matmul_T(p, W, rhs_fn, out_cb, psums, K=None, N=None, Tlen=T, tch=TCH,
             n0=0, reads=(), label="proj"):
    K = K if K is not None else W.K
    N = N if N is not None else W.N
    nt = (Tlen + tch - 1) // tch
    KB = (K + 127) // 128
    npb = len(psums)
    co = 0
    while co < N:
        cw = min(128, N - co)
        for tg0 in range(0, nt, npb):
            tis = range(tg0, min(tg0 + npb, nt))
            for kb in range(KB):
                wtile = W.tile(kb, n0 + co, cw)
                for ti in tis:
                    t0 = ti * tch
                    tw = min(tch, Tlen - t0)
                    pb = psums[ti - tg0]
                    rhs_ap = rhs_fn(kb, t0, tw)
                    et(p, (lambda tt, o=pb[0:cw, 0:tw], w=wtile, r=rhs_ap,
                           st=(kb == 0), sp=(kb == KB - 1):
                           tt.matmul(o, w, r, start=st, stop=sp)),
                       reads=(W.buf,) + tuple(reads), writes=(pb,), label=label)
            for ti in tis:
                t0 = ti * tch
                tw = min(tch, Tlen - t0)
                out_cb(psums[ti - tg0], co, cw, t0, tw)
        co += cw


class Ctx:
    pass


def alloc_common(p, need_xn=True, nR=3, need_ident=True):
    c = Ctx()
    c.p = p
    p.init_consts()
    c.XT = p.sb("XT", [128, CT * T], BF16)
    c.XN = p.sb("XN", [128, CT * T], BF16) if need_xn else None
    c.ones_col_bf = p.sb("ones_col_bf", [128, 1], BF16)
    c.ones_row_f = p.sb("ones_row_f", [1, 128], F32)
    c.ones_row_bf = p.sb("ones_row_bf", [1, 128], BF16)
    c.recip_bf = p.sb("recip_bf", [128, 1], BF16)
    c.R = [p.sb(f"R{i}", [128, TCH], F32) for i in range(nR)]
    c.sqc = p.sb("sqc", [128, 2 * TCH], BF16)
    c.psums = [p.ps(f"pp{i}", [128, 512], F32) for i in range(6)]
    c.prow = [p.ps(f"pr{i}", [1, 512], F32) for i in range(2)]
    g = "gpsimd"
    if need_ident:
        c.ident_bf = p.sb("ident_bf", [128, 128], BF16)
        p.emit(g, lambda h: h.iota(c.ident_bf.ap, [[-1, 128]], base=0,
                                   channel_multiplier=1,
                                   allow_small_or_imprecise_dtypes=True),
               writes=(c.ident_bf,), label="ident_iota")
        ev(p, lambda v: v.tensor_scalar(c.ident_bf.ap, c.ident_bf.ap, 0.0, None,
                                        ALU.is_equal),
           reads=(c.ident_bf,), writes=(c.ident_bf,), label="ident_eq")
    p.emit(g, lambda h: h.memset(c.ones_col_bf.ap, 1.0), writes=(c.ones_col_bf,), label="c1")
    p.emit(g, lambda h: h.memset(c.ones_row_f.ap, 1.0), writes=(c.ones_row_f,), label="c2")
    p.emit(g, lambda h: h.memset(c.ones_row_bf.ap, 1.0), writes=(c.ones_row_bf,), label="c2b")
    p.emit(g, lambda h: h.memset(c.recip_bf.ap, 1.0 / D), writes=(c.recip_bf,), label="c3")
    return c


def load_const_sb(p, name, dram_t, shape, dt=BF16, queue="gpsimd", shared=False):
    if shared:
        if not hasattr(p, "_cshared"):
            p._cshared = {}
        if name in p._cshared:
            return p._cshared[name]
    buf = p.sb("cst_" + name, shape, dt)
    p.dma(queue, buf.ap, dram_t.ap(), writes=(buf,), label="ldc_" + name)
    if shared:
        p._cshared[name] = buf
    return buf


def emit_load_xt(p, buf, x_dram, queue="sync"):
    for b in range(NB):
        for ct in range(CT):
            dst = buf[:, ct * T + b * S: ct * T + (b + 1) * S]
            src = x_dram.ap()[b, ct * 128:(ct + 1) * 128, :]
            p.dma(queue, dst, src, writes=(buf,), label=f"ldx{b}_{ct}")


def emit_store_xt(p, buf, y_dram, queue="sync"):
    for b in range(NB):
        for ct in range(CT):
            src = buf[:, ct * T + b * S: ct * T + (b + 1) * S]
            dst = y_dram.ap()[b, ct * 128:(ct + 1) * 128, :]
            p.dma(queue, dst, src, reads=(buf,), label=f"stx{b}_{ct}")


def emit_rowsums(p, c, src, ti, dstrow, scale_col, square=False, ctiles=CT,
                 prow_i=0, srcT=T, label="rs"):
    pb = c.prow[prow_i]
    tw = min(TCH, srcT - ti * TCH)
    for ct in range(ctiles):
        sl = slice(ct * srcT + ti * TCH, ct * srcT + ti * TCH + tw)
        if square:
            sqsl = c.sqc[:, (ct % 2) * TCH: (ct % 2) * TCH + tw]
            eact(p, sqsl, src[:, sl], AF.Square, reads=(src,), writes=(c.sqc,),
                 label=label + "_sq")
            ap, rd = sqsl, (c.sqc, scale_col)
        else:
            ap, rd = src[:, sl], (src, scale_col)
        et(p, (lambda tt, o=pb[0:1, 0:tw], w=scale_col.ap, r=ap,
               st=(ct == 0), sp=(ct == ctiles - 1):
               tt.matmul(o, w, r, start=st, stop=sp)),
           reads=rd, writes=(pb,), label=label)
    eact(p, dstrow[0:1, 0:tw], pb[0:1, 0:tw], AF.Copy,
         reads=(pb,), writes=(dstrow,), label=label + "_ev")


def emit_bcast_chunk(p, c, row_buf, pb, tw=TCH, label="bc"):
    et(p, (lambda tt, o=pb[:, 0:tw], w=c.ones_row_f.ap, r=row_buf[0:1, 0:tw]:
           tt.matmul(o, w, r, start=True, stop=True)),
       reads=(row_buf, c.ones_row_f), writes=(pb,), label=label)


def emit_ln(p, c, src, dst, eps=1e-5, gcol=None, bcol=None,
            s1rep=None, s1eps=None, label="ln"):
    R0, R1, R2 = c.R
    if s1rep is not None and not hasattr(c, "R3"):
        c.R3 = p.sb("R3", [128, TCH], F32)
    R3 = getattr(c, "R3", None)
    for ti in range(T // TCH):
        emit_rowsums(p, c, src, ti, R0, c.recip_bf, prow_i=0, label=label + "_m")
        emit_rowsums(p, c, src, ti, R1, c.recip_bf, square=True, prow_i=1,
                     label=label + "_e")
        ev(p, lambda v: v.tensor_tensor(R2.ap, R0.ap, R0.ap, ALU.mult),
           reads=(R0,), writes=(R2,), label=label + "_m2")
        ev(p, lambda v: v.scalar_tensor_tensor(R1.ap, R2.ap, -1.0, R1.ap, ALU.mult, ALU.add),
           reads=(R2, R1), writes=(R1,), label=label + "_var")
        eact(p, R2.ap, R1.ap, AF.Ln, bias=float(eps), reads=(R1,), writes=(R2,),
             label=label + "_lnA")
        eact(p, R2.ap, R2.ap, AF.Exp, scale=-0.5, reads=(R2,), writes=(R2,),
             label=label + "_expA")
        ev(p, lambda v: v.tensor_tensor(R0.ap, R0.ap, R2.ap, ALU.mult),
           reads=(R0, R2), writes=(R0,), label=label + "_B")
        if s1rep is not None:
            ev(p, lambda v: v.tensor_tensor(R3.ap, R1.ap, R2.ap, ALU.mult),
               reads=(R1, R2), writes=(R3,), label=label + "_s1a")
            ev(p, lambda v: v.tensor_tensor(R3.ap, R3.ap, R2.ap, ALU.mult),
               reads=(R3, R2), writes=(R3,), label=label + "_s1b")
            eact(p, R3.ap, R3.ap, AF.Ln, bias=float(s1eps), reads=(R3,), writes=(R3,),
                 label=label + "_s1l")
            eact(p, R3.ap, R3.ap, AF.Exp, scale=-0.5, reads=(R3,), writes=(R3,),
                 label=label + "_s1e")
            pS = c.psums[2]
            emit_bcast_chunk(p, c, R3, pS, label=label + "_s1bc")
            eact(p, s1rep[:, ti * TCH:(ti + 1) * TCH], pS[:, :], AF.Copy,
                 reads=(pS,), writes=(s1rep,), label=label + "_s1ev")
        pA, pB = c.psums[0], c.psums[1]
        emit_bcast_chunk(p, c, R2, pA, label=label + "_bcA")
        emit_bcast_chunk(p, c, R0, pB, label=label + "_bcB")
        if dst is not None:
            for ct in range(CT):
                sl = slice(ct * T + ti * TCH, ct * T + (ti + 1) * TCH)
                def app(v, o=dst[:, sl], x=src[:, sl], a=pA[:, :], b=pB[:, :],
                        g=(gcol[:, ct:ct + 1] if gcol is not None else None),
                        bb=(bcol[:, ct:ct + 1] if bcol is not None else None)):
                    v.tensor_tensor(o, x, a, ALU.mult)
                    i2 = v.tensor_tensor(o, o, b, ALU.subtract)
                    if g is not None:
                        i2 = v.tensor_scalar(o, o, g, bb, ALU.mult, ALU.add)
                    return i2
                rd = [src, pA, pB] + ([gcol] if gcol is not None else [])                     + ([bcol] if bcol is not None else [])
                ev(p, app, reads=rd, writes=(dst,), label=label + "_app")


def emit_rms_rep(p, c, src, rep, eps, wcol=None, label="rms"):
    R1 = c.R[1]
    for ti in range(T // TCH):
        emit_rowsums(p, c, src, ti, R1, c.recip_bf, square=True, prow_i=0,
                     label=label + "_e")
        eact(p, R1.ap, R1.ap, AF.Ln, bias=float(eps), reads=(R1,), writes=(R1,),
             label=label + "_l")
        eact(p, R1.ap, R1.ap, AF.Exp, scale=-0.5, reads=(R1,), writes=(R1,),
             label=label + "_x")
        pS = c.psums[3]
        emit_bcast_chunk(p, c, R1, pS, label=label + "_bc")
        eact(p, rep[:, ti * TCH:(ti + 1) * TCH], pS[:, :], AF.Copy,
             reads=(pS,), writes=(rep,), label=label + "_ev")


def emit_prenorm(p, c, s1rep=None, s1eps=None):
    emit_ln(p, c, c.XT, c.XN, eps=1e-5, s1rep=s1rep, s1eps=s1eps, label="pn")


def alloc_flash(p, c, dh_max=96, nkt_max=16):
    f = Ctx()
    f.qT = p.sb("fl_qT", [128, S], BF16)
    f.kT = p.sb("fl_kT", [128, S], BF16)
    f.vx = p.sb("fl_vx", [128, 16 * 130], BF16)
    f.att = p.sb("fl_att", [128, S], BF16)
    f.ncol = p.sb("fl_ncol", [128, 16], BF16)
    f.mcol = p.sb("fl_mcol", [128, 4], F32)
    f.nrow = p.sb("fl_nrow", [128, S], BF16)
    f.ms = p.sb("fl_ms", [128, 512], F32)
    f.ms2 = p.sb("fl_ms2", [128, 512], F32)
    f.st = f.ms
    f.rl = f.ms
    f.pt = [p.sb(f"fl_pt{i}", [128, 512], BF16) for i in range(2)]
    f.ob = p.sb("fl_ob", [128, 512], BF16)
    return f


def emit_flash_head(p, c, f, qT, kT, qbase, h_in_g, dh, att_out, atbase,
                    a_spec, c_spec, label="fl"):
    NQT = S // 128
    msl = [f.ms, f.ms2]
    cnt = 0
    for qt in range(NQT):
        chunks = a_spec(qt)
        for j, (k0, kw, mk) in enumerate(chunks):
            rr = cnt % 2
            cnt += 1
            pb = c.psums[4 + rr]
            et(p, (lambda tt, o=pb[0:128, 0:kw], w=qT[qbase:qbase + dh, qt * 128:(qt + 1) * 128],
                   r=kT[qbase:qbase + dh, k0:k0 + kw]:
                   tt.matmul(o, w, r, start=True, stop=True)),
               reads=(qT, kT), writes=(pb,), label=label + "_sa")
            if mk is not None:
                mbuf = msl[rr]
                ev(p, (lambda v, o=mbuf[:, 0:kw], a=pb[0:128, 0:kw], m=mk[0][0:128, mk[1]:mk[1] + kw]:
                       v.tensor_tensor(o, a, m, ALU.add)),
                   reads=(pb, mk[0]), writes=(mbuf,), label=label + "_samask")
                srcb, srcap = mbuf, mbuf[:, 0:kw]
            else:
                srcb, srcap = pb, pb[0:128, 0:kw]
            ev(p, (lambda v, o=f.mcol[:, j:j + 1], i=srcap:
                   v.reduce_max(o, i, AX.X)),
               reads=(srcb,), writes=(f.mcol,), label=label + "_sared")
        nj = len(chunks)
        ev(p, (lambda v, o=f.ncol[:, qt:qt + 1], i=f.mcol[:, 0:nj]:
               v.reduce_max(o, i, AX.X, negate=True)),
           reads=(f.mcol,), writes=(f.ncol,), label=label + "_saneg")
    for g0 in range(0, NQT, 4):
        pb = c.prow[0]
        for j in range(4):
            qt = g0 + j
            et(p, (lambda tt, o=pb[0:1, j * 128:(j + 1) * 128],
                   w=f.ncol[:, qt:qt + 1], r=c.ident_bf.ap,
                   st=(j == 0), sp=(j == 3):
                   tt.matmul(o, w, r, start=st, stop=sp)),
               reads=(f.ncol, c.ident_bf), writes=(pb,), label=label + "_sbrow")
        eact(p, f.nrow[0:1, g0 * 128:(g0 + 4) * 128], pb[0:1, 0:512], AF.Copy,
             reads=(pb,), writes=(f.nrow,), label=label + "_sbev")
    for qc in range(S // 512):
        kts = c_spec(qc)
        po = c.psums[2 + 2 * (qc % 2)]
        pb2 = c.psums[3 + 2 * (qc % 2)]

        def front(ki, kt0):
            ps_ = c.psums[ki % 2]
            et(p, (lambda tt, o=ps_[0:128, 0:512],
                   w=kT[qbase:qbase + dh, kt0:kt0 + 128],
                   r=qT[qbase:qbase + dh, qc * 512:(qc + 1) * 512]:
                   tt.matmul(o, w, r, start=True, stop=False)),
               reads=(kT, qT), writes=(ps_,), label=label + "_st")
            et(p, (lambda tt, o=ps_[0:128, 0:512], w=c.ones_row_bf.ap,
                   r=f.nrow[0:1, qc * 512:(qc + 1) * 512]:
                   tt.matmul(o, w, r, start=False, stop=True)),
               reads=(f.nrow, c.ones_row_bf), writes=(ps_,), label=label + "_stm")

        def back(ki, kt0, mkT):
            ps_ = c.psums[ki % 2]
            ptb = f.pt[ki % 2]
            if mkT is not None:
                mb, mo = mkT
                mbuf = msl[ki % 2]
                ev(p, (lambda v, o=mbuf.ap, a=ps_[0:128, 0:512], m=mb[0:128, mo:mo + 512]:
                       v.tensor_tensor(o, a, m, ALU.add)),
                   reads=(ps_, mb), writes=(mbuf,), label=label + "_cmask")
                eact(p, ptb.ap, mbuf.ap, AF.Exp, reads=(mbuf,), writes=(ptb,),
                     label=label + "_exp")
            else:
                eact(p, ptb.ap, ps_[0:128, 0:512], AF.Exp, reads=(ps_,), writes=(ptb,),
                     label=label + "_exp")
            tt_idx = kt0 // 128
            vsl = f.vx[:, tt_idx * 130 + h_in_g * 65: tt_idx * 130 + h_in_g * 65 + 65]
            et(p, (lambda tt, o=po[0:65, 0:512], w=vsl, r=ptb.ap,
                   st=(ki == 0), sp=(ki == len(kts) - 1):
                   tt.matmul(o, w, r, start=st, stop=sp)),
               reads=(f.vx, ptb), writes=(po,), label=label + "_pv")

        for ki, (kt0, mkT) in enumerate(kts):
            front(ki, kt0)
            if ki > 0:
                back(ki - 1, kts[ki - 1][0], kts[ki - 1][1])
        back(len(kts) - 1, kts[-1][0], kts[-1][1])
        rl = c.R[0]
        ev(p, (lambda v, o=rl[0:1, 0:512], i=po[64:65, 0:512]:
               v.reciprocal(o, i)),
           reads=(po,), writes=(rl,), label=label + "_rl")
        et(p, (lambda tt, o=pb2[0:64, 0:512], w=c.ones_row_f[0:1, 0:64],
               r=rl[0:1, 0:512]:
               tt.matmul(o, w, r, start=True, stop=True)),
           reads=(rl, c.ones_row_f), writes=(pb2,), label=label + "_rlbc")
        eact(p, f.ob[0:64, 0:512], po[0:65, 0:512][0:64, 0:512], AF.Copy,
             reads=(po,), writes=(f.ob,), label=label + "_oev")
        ev(p, (lambda v, o=att_out[atbase:atbase + 64, qc * 512:(qc + 1) * 512],
               a=f.ob[0:64, 0:512], bq=pb2[0:64, 0:512]:
               v.tensor_tensor(o, a, bq, ALU.mult)),
           reads=(f.ob, pb2), writes=(att_out,), label=label + "_onorm")


def emit_proj_group_T(p, c, W, dst, b, n0, nw=128, scale=None, func=AF.Copy,
                      src=None, label="projT"):
    src = src if src is not None else c.XN

    def rhs(kb, t0, tw):
        return src[:, kb * T + b * S + t0: kb * T + b * S + t0 + tw]

    def cb(pb, co, cw, t0, tw):
        eact(p, dst[co: co + cw, t0:t0 + tw], pb[0:cw, 0:tw], func, scale=scale,
             reads=(pb,), writes=(dst,), label=label + "_ev")

    matmul_T(p, W, rhs, cb, c.psums[0:2], K=W.K, N=nw, Tlen=S, n0=n0,
             reads=(src,), label=label)


def emit_proj_group_tok(p, c, W, f, b, n0, label="projV"):
    ev(p, lambda v: v.memset(f.vx.ap, 1.0), writes=(f.vx,), label=label + "_ones")
    KB = W.KB
    for tt_i in range(16):
        pb = c.psums[2 + (tt_i % 2)]
        for kb in range(KB):
            lhs = c.XN[:, kb * T + b * S + tt_i * 128: kb * T + b * S + (tt_i + 1) * 128]
            et(p, (lambda tt, o=pb[0:128, 0:128], w=lhs, r=W.tile(kb, n0, 128),
                   st=(kb == 0), sp=(kb == KB - 1):
                   tt.matmul(o, w, r, start=st, stop=sp)),
               reads=(c.XN, W.buf), writes=(pb,), label=label)
        for hh in range(2):
            eact(p, f.vx[:, tt_i * 130 + hh * 65: tt_i * 130 + hh * 65 + 64],
                 pb[0:128, hh * 64:(hh + 1) * 64], AF.Copy,
                 reads=(pb,), writes=(f.vx,), label=label + "_ev")


def emit_resid_from_att(p, c, Wo, f, b, g, label="wo"):
    for co in range(CT):
        for tch_i in range(S // 512):
            pb = c.psums[4 + (tch_i % 2)]
            et(p, (lambda tt, o=pb[0:128, 0:512], w=Wo.tile(g, co * 128, 128),
                   r=f.att[:, tch_i * 512:(tch_i + 1) * 512]:
                   tt.matmul(o, w, r, start=True, stop=True)),
               reads=(Wo.buf, f.att), writes=(pb,), label=label)
            sl = slice(co * T + b * S + tch_i * 512, co * T + b * S + (tch_i + 1) * 512)
            ev(p, (lambda v, o=c.XT[:, sl], x=c.XT[:, sl], a=pb[0:128, 0:512]:
                   v.tensor_tensor(o, x, a, ALU.add)),
               reads=(c.XT, pb), writes=(c.XT,), label=label + "_add")


def emit_l1_attn(p, c, f, wd, masks, label="l1a"):
    wq = [p.sb(f"l1wq{i}", [128, 8 * 128], BF16) for i in range(2)]
    wk = [p.sb(f"l1wk{i}", [128, 8 * 128], BF16) for i in range(2)]
    wv = [p.sb(f"l1wv{i}", [128, 8 * 128], BF16) for i in range(2)]
    wo = [p.sb(f"l1wo{i}", [128, D], BF16) for i in range(2)]
    for g in range(CT):
        i = g % 2
        Wq = load_weight_cols(p, wq[i], wd["q"], D, g * 128, 128, name=f"{label}q{g}")
        Wk = load_weight_cols(p, wk[i], wd["k"], D, g * 128, 128, name=f"{label}k{g}")
        Wv = load_weight_cols(p, wv[i], wd["v"], D, g * 128, 128, name=f"{label}v{g}")
        Wo = load_weight_rows(p, wo[i], wd["o"], g * 128, 128, D, name=f"{label}o{g}")
        for b in range(NB):
            emit_proj_group_T(p, c, Wq, f.qT, b, 0, scale=0.125, label=f"{label}q{b}{g}")
            emit_proj_group_T(p, c, Wk, f.kT, b, 0, label=f"{label}k{b}{g}")
            emit_proj_group_tok(p, c, Wv, f, b, 0, label=f"{label}v{b}{g}")

            def a_spec(qt):
                if qt == 0:
                    return [(0, 128, (masks["a0"], 0))]
                return [((qt - 1) * 128, 256, (masks["a1"], 0))]

            def c_spec(qc):
                out = []
                for j in range(-1, 4):
                    kt = qc * 4 + j
                    if kt < 0:
                        continue
                    out.append((kt * 128, (masks["cm"], 384 - 128 * j)))
                return out

            for hh in range(2):
                emit_flash_head(p, c, f, f.qT, f.kT, hh * 64, hh, 64,
                                f.att, hh * 64, a_spec, c_spec,
                                label=f"{label}h{b}{g}{hh}")
            emit_resid_from_att(p, c, Wo, f, b, 0, label=f"{label}o{b}{g}")


def emit_l1_ffn(p, c, wd, label="l1f"):
    Win = load_weight(p, label + "win", wd["win"], D, 64)
    Wout_b = p.sb(label + "wout", [128, D], BF16)
    Wout = load_weight_rows(p, Wout_b, wd["wout"], 0, 64, D, name=label + "wo")
    W1 = load_const_sb(p, label + "w1", wd["w1"], [64, 16], F32)
    B1 = load_const_sb(p, label + "b1", wd["b1"], [64, 16], F32)
    W2 = load_const_sb(p, label + "w2", wd["w2"], [64, 16], F32)
    B2 = load_const_sb(p, label + "b2", wd["b2"], [64, 1], F32)
    BIN = load_const_sb(p, label + "bin", wd["bin"], [64, 1], F32)
    BOUT = load_const_sb(p, label + "bout", wd["bout"], [128, CT], F32)
    TH = T // 2
    SS = p.sb(label + "ss", [128, TH], BF16)
    H = p.sb(label + "h", [128, TH], BF16)
    G = p.sb(label + "g", [128, TH], BF16)
    OB = p.sb(label + "ob", [128, 512], BF16)

    for half in range(2):
        toff = half * TH

        def rhs(kb, t0, tw, toff=toff):
            return c.XN[:, kb * T + toff + t0: kb * T + toff + t0 + tw]

        def cb(pb, co, cw, t0, tw):
            eact(p, SS[0:cw, t0:t0 + tw], pb[0:cw, 0:tw], AF.Identity, bias=BIN.ap,
                 reads=(pb, BIN), writes=(SS,), label=label + "_sev")

        matmul_T(p, Win, rhs, cb, c.psums[0:2], Tlen=TH, reads=(c.XN,),
                 label=label + "_s")
        eact(p, G[0:64, :], SS[0:64, :], AF.Identity, bias=B2.ap, scale=0.0,
             reads=(SS, B2), writes=(G,), label=label + "_g0")
        for i in range(16):
            eact(p, H[0:64, :], SS[0:64, :], AF.Relu, scale=W1[0:64, i:i + 1],
                 bias=B1[0:64, i:i + 1], reads=(SS, W1, B1), writes=(H,),
                 label=label + f"_h{i}")
            ev(p, (lambda v, o=G[0:64, :], hh=H[0:64, :], w2=W2[0:64, i:i + 1],
                   g=G[0:64, :]: v.scalar_tensor_tensor(o, hh, w2, g, ALU.mult, ALU.add)),
               reads=(H, W2, G), writes=(G,), label=label + f"_a{i}")
        for co in range(CT):
            for tch_i in range(TH // 512):
                pb = c.psums[2 + (tch_i % 2)]
                et(p, (lambda tt, o=pb[0:128, 0:512],
                       w=Wout.buf[0:64, co * 128:(co + 1) * 128],
                       r=G[0:64, tch_i * 512:(tch_i + 1) * 512]:
                       tt.matmul(o, w, r, start=True, stop=True)),
                   reads=(Wout.buf, G), writes=(pb,), label=label + "_om")
                eact(p, OB[0:128, 0:512], pb[0:128, 0:512], AF.Identity,
                     bias=BOUT[:, co:co + 1], reads=(pb, BOUT), writes=(OB,),
                     label=label + "_oev")
                sl = slice(co * T + toff + tch_i * 512, co * T + toff + (tch_i + 1) * 512)
                ev(p, (lambda v, o=c.XT[:, sl], x=c.XT[:, sl], a=OB[0:128, 0:512]:
                       v.tensor_tensor(o, x, a, ALU.add)),
                   reads=(c.XT, OB), writes=(c.XT,), label=label + "_oadd")


def l1_masks_host():
    out = {}
    m0 = np.zeros((128, 128), np.float32)
    for qf in range(128):
        for kp in range(128):
            d = qf - kp
            if not (0 <= d <= 63):
                m0[qf, kp] = -1e30
    out["a0"] = m0
    m1 = np.zeros((128, 256), np.float32)
    for qf in range(128):
        for kp in range(256):
            d = qf + 128 - kp
            if not (0 <= d <= 63):
                m1[qf, kp] = -1e30
    out["a1"] = m1
    M = np.zeros((128, 1024), np.float32)
    for kp in range(128):
        for vv in range(1024):
            d = (vv - 384) - kp
            if not (0 <= d <= 63):
                M[kp, vv] = -1e30
    out["cm"] = M
    return out


def emit_l3_attn(p, c, f, wd, masks, label="l3a"):
    CKV = p.sb(label + "ckv", [128, S], BF16)
    QROT = p.sb(label + "qrot", [128, 2 * S], BF16)
    KROT = p.sb(label + "krot", [128, 2 * S], BF16)
    QCAT = p.sb(label + "qcat", [128, S], BF16)
    KCAT = p.sb(label + "kcat", [128, S], BF16)
    wkvd = load_weight(p, label + "kvd", wd["kv_down"], D, 128)
    wq_sl = [p.sb(label + "wq0", [128, 8 * 128], BF16)] * 2
    wku_sl = [p.sb(label + "wku0", [128, 128], BF16)] * 2
    wvu_sl = [p.sb(label + "wvu0", [128, 128], BF16)] * 2
    wo_sl = [p.sb(label + "wo0", [128, D], BF16)] * 2
    wqr = load_weight(p, label + "wqr", wd["q_rope"], D, 32)
    wkr_b = p.sb(label + "wkr", [128, 32], BF16)
    wkr = load_weight_rows(p, wkr_b, wd["k_rope"], 0, 128, 32, name=label + "wkr")
    SC = 1.0 / float(np.sqrt(96.0))

    def rope_apply(src, dst, tabsign, bb, lab):
        cos = f.att[0:16, 0:S]
        sin = f.vx[0:16, 0:S]
        for ch in range(S // 512):
            u = slice(ch * 512, (ch + 1) * 512)
            s1 = QCAT[0:16, ch * 512:(ch + 1) * 512]
            s2 = KCAT[0:16, ch * 512:(ch + 1) * 512]
            d1 = dst[0:16, ch * 512:(ch + 1) * 512]
            d2 = dst[0:16, S + ch * 512: S + (ch + 1) * 512]
            co_ = cos[0:16, u] if False else f.att[0:16, u]
            si_ = f.vx[0:16, u]
            tm = f.ob[0:16, 0:512]
            ev(p, (lambda v, tm=tm, s2=s2, si_=si_: v.tensor_tensor(tm, s2, si_, ALU.mult)),
               reads=(QCAT, KCAT, f.vx), writes=(f.ob,), label=lab + "1")
            ev(p, (lambda v, d1=d1, s1=s1, co_=co_: v.tensor_tensor(d1, s1, co_, ALU.mult)),
               reads=(QCAT, KCAT, f.att), writes=(dst_b,), label=lab + "2")
            ev(p, (lambda v, d1=d1, tm=tm: v.tensor_tensor(d1, d1, tm, ALU.subtract)),
               reads=(dst_b, f.ob), writes=(dst_b,), label=lab + "3")
            ev(p, (lambda v, tm=tm, s1=s1, si_=si_: v.tensor_tensor(tm, s1, si_, ALU.mult)),
               reads=(QCAT, KCAT, f.vx), writes=(f.ob,), label=lab + "4")
            ev(p, (lambda v, d2=d2, s2=s2, co_=co_: v.tensor_tensor(d2, s2, co_, ALU.mult)),
               reads=(QCAT, KCAT, f.att), writes=(dst_b,), label=lab + "5")
            ev(p, (lambda v, d2=d2, tm=tm: v.tensor_tensor(d2, d2, tm, ALU.add)),
               reads=(dst_b, f.ob), writes=(dst_b,), label=lab + "6")

    def a_spec(qt):
        kend = (qt + 1) * 128
        out = []
        k0 = 0
        while k0 < kend:
            kw = min(512, kend - k0)
            if k0 + kw >= kend:
                d0 = qt * 128 - k0
                out.append((k0, kw, (masks["camA"], 384 - d0)))
            else:
                out.append((k0, kw, None))
            k0 += kw
        return out

    def c_spec(qc):
        out = []
        for kt in range(0, qc * 4 + 4):
            dd = kt - 4 * qc
            if dd >= 0:
                out.append((kt * 128, (masks["camC"], 384 - 128 * dd)))
            else:
                out.append((kt * 128, None))
        return out

    for b in range(NB):
        p.dma("gpsimd", f.att[0:16, 0:S], wd["rope_t"].ap()[0:16, 0:S],
              reads=(f.att,), writes=(f.att,), label=label + "_ldcos")
        p.dma("gpsimd", f.vx[0:16, 0:S], wd["rope_t"].ap()[0:16, S:2 * S],
              reads=(f.vx,), writes=(f.vx,), label=label + "_ldsin")
        def cb_ckv(pb, co, cw, t0, tw):
            eact(p, CKV[co:co + cw, t0:t0 + tw], pb[0:cw, 0:tw], AF.Copy,
                 reads=(pb,), writes=(CKV,), label=label + "_ckvev")

        matmul_T(p, wkvd, (lambda kb, t0, tw, b=b:
                           c.XN[:, kb * T + b * S + t0: kb * T + b * S + t0 + tw]),
                 cb_ckv, c.psums[0:2], Tlen=S, reads=(c.XN,), label=label + "_ckv")
        for hf in range(2):
            def cb_qr(pb, co, cw, t0, tw, hf=hf):
                dstb = QCAT if hf == 0 else KCAT
                eact(p, dstb[0:16, t0: t0 + tw], pb[0:16, 0:tw],
                     AF.Copy, scale=SC, reads=(pb,), writes=(dstb,),
                     label=label + "_qrev")
            matmul_T(p, wqr, (lambda kb, t0, tw, b=b:
                              c.XN[:, kb * T + b * S + t0: kb * T + b * S + t0 + tw]),
                     cb_qr, c.psums[2:4], N=16, n0=hf * 16, Tlen=S,
                     reads=(c.XN,), label=label + "_qr")
        dst_b = QROT
        rope_apply(None, QROT, +1, b, label + "_qrope")
        for hf in range(2):
            def cb_kr(pb, co, cw, t0, tw, hf=hf):
                dstb = QCAT if hf == 0 else KCAT
                eact(p, dstb[0:16, t0: t0 + tw], pb[0:16, 0:tw],
                     AF.Copy, reads=(pb,), writes=(dstb,), label=label + "_krev")
            matmul_T(p, wkr, (lambda kb, t0, tw:
                              CKV[:, t0: t0 + tw]),
                     cb_kr, c.psums[2:4], N=16, n0=hf * 16, Tlen=S,
                     reads=(CKV,), label=label + "_kr")
        dst_b = KROT
        rope_apply(None, KROT, +1, b, label + "_krope")
        ev(p, lambda v: v.memset(QCAT.ap, 0.0), writes=(QCAT,), label=label + "_qz")
        ev(p, lambda v: v.memset(KCAT.ap, 0.0), writes=(KCAT,), label=label + "_kz")
        ev(p, lambda v: v.tensor_copy(QCAT[0:16, :], QROT[0:16, 0:S]),
           reads=(QROT,), writes=(QCAT,), label=label + "_qc1")
        ev(p, lambda v: v.tensor_copy(QCAT[32:48, :], QROT[0:16, S:2 * S]),
           reads=(QROT,), writes=(QCAT,), label=label + "_qc2")
        ev(p, lambda v: v.tensor_copy(KCAT[0:16, :], KROT[0:16, 0:S]),
           reads=(KROT,), writes=(KCAT,), label=label + "_kc1")
        ev(p, lambda v: v.tensor_copy(KCAT[32:48, :], KROT[0:16, S:2 * S]),
           reads=(KROT,), writes=(KCAT,), label=label + "_kc2")
        for g in range(CT):
            i = g % 2
            Wq = load_weight_cols(p, wq_sl[i], wd["q"], D, g * 128, 128,
                                  name=label + f"q{g}")
            Wku = load_weight_cols(p, wku_sl[i], wd["k_up"], 128, g * 128, 128,
                                   name=label + f"ku{g}")
            Wvu = load_weight_cols(p, wvu_sl[i], wd["v_up"], 128, g * 128, 128,
                                   name=label + f"vu{g}")
            Wo = load_weight_rows(p, wo_sl[i], wd["o"], g * 128, 128, D,
                                  name=label + f"o{g}")
            emit_proj_group_T(p, c, Wq, f.qT, b, 0, scale=SC, label=f"{label}q{b}{g}")
            matmul_T(p, Wku, (lambda kb, t0, tw: CKV[:, t0:t0 + tw]),
                     (lambda pb, co, cw, t0, tw:
                      eact(p, f.kT[co:co + cw, t0:t0 + tw], pb[0:cw, 0:tw], AF.Copy,
                           reads=(pb,), writes=(f.kT,), label=label + "_kev")),
                     c.psums[0:2], Tlen=S, reads=(CKV,), label=label + "_k")
            ev(p, lambda v: v.memset(f.vx.ap, 1.0), writes=(f.vx,), label=label + "_vo")
            for tt_i in range(16):
                pb = c.psums[2 + (tt_i % 2)]
                et(p, (lambda tt, o=pb[0:128, 0:128],
                       w=CKV[:, tt_i * 128:(tt_i + 1) * 128],
                       r=Wvu.tile(0, 0, 128):
                       tt.matmul(o, w, r, start=True, stop=True)),
                   reads=(CKV, Wvu.buf), writes=(pb,), label=label + "_v")
                for hh in range(2):
                    eact(p, f.vx[:, tt_i * 130 + hh * 65: tt_i * 130 + hh * 65 + 64],
                         pb[0:128, hh * 64:(hh + 1) * 64], AF.Copy,
                         reads=(pb,), writes=(f.vx,), label=label + "_vev")
            for hh in range(2):
                ev(p, (lambda v, o=QCAT[64:128, :], i_=f.qT[hh * 64:hh * 64 + 64, :]:
                       v.tensor_copy(o, i_)),
                   reads=(f.qT,), writes=(QCAT,), label=label + "_qh")
                ev(p, (lambda v, o=KCAT[64:128, :], i_=f.kT[hh * 64:hh * 64 + 64, :]:
                       v.tensor_copy(o, i_)),
                   reads=(f.kT,), writes=(KCAT,), label=label + "_kh")
                emit_flash_head(p, c, f, QCAT, KCAT, 0, hh, 128,
                                f.att, hh * 64, a_spec, c_spec,
                                label=f"{label}f{b}{g}{hh}")
            emit_resid_from_att(p, c, Wo, f, b, 0, label=f"{label}r{b}{g}")


def l3_masks_host():
    camA = np.zeros((128, 896), np.float32)
    for qf in range(128):
        for v in range(896):
            if (v - 384) > qf:
                camA[qf, v] = -1e30
    camC = np.zeros((128, 896), np.float32)
    for kp in range(128):
        for v in range(896):
            if kp > (v - 384):
                camC[kp, v] = -1e30
    return {"camA": camA, "camC": camC}


def l3_rope_host():
    d_R = 32
    inv = 1.0 / 10000.0 ** (np.arange(0, d_R, 2) / d_R)
    t = np.arange(S)
    fr = np.outer(t, inv)
    out = np.zeros((128, 2 * S), np.float32)
    out[0:16, 0:S] = np.cos(fr).T
    out[0:16, S:2 * S] = np.sin(fr).T
    return out


def emit_phi(p, c, buf, sl, tmp, label):
    ev(p, (lambda v, t=tmp, x=buf[sl]: v.tensor_scalar_min(t, x, 0.0)),
       reads=(buf,), writes=(c.sqc,), label=label + "m")
    eact(p, tmp, tmp, AF.Exp, reads=(c.sqc,), writes=(c.sqc,), label=label + "e")
    ev(p, (lambda v, o=buf[sl], x=buf[sl], t=tmp:
           v.scalar_tensor_tensor(o, x, 1.0, t, ALU.add, ALU.max)),
       reads=(buf, c.sqc), writes=(buf,), label=label + "x")


def emit_l2_attn(p, c, f, wd, label="l2a"):
    wq_sl = p.sb(label + "wq", [128, 8 * 128], BF16)
    wk_sl = p.sb(label + "wk", [128, 8 * 128], BF16)
    wv_sl = p.sb(label + "wv", [128, 8 * 128], BF16)
    wo_sl = p.sb(label + "wo", [128, D], BF16)
    KVS = p.sb(label + "kvs", [128, 66], BF16)
    BQ = load_const_sb(p, label + "bq", wd["b_q"], [128, CT], F32)
    BK = load_const_sb(p, label + "bk", wd["b_k"], [128, CT], F32)
    BV = load_const_sb(p, label + "bv", wd["b_v"], [128, CT], F32)
    BO = load_const_sb(p, label + "bo", wd["b_o"], [128, CT], F32)
    phit = c.sqc[:, 0:512]
    for g in range(CT):
        load_weight_cols(p, wq_sl, wd["q"], D, g * 128, 128, name=label + f"q{g}")
        load_weight_cols(p, wk_sl, wd["k"], D, g * 128, 128, name=label + f"k{g}")
        load_weight_cols(p, wv_sl, wd["v"], D, g * 128, 128, name=label + f"v{g}")
        Wo = load_weight_rows(p, wo_sl, wd["o"], g * 128, 128, D, name=label + f"o{g}")
        Wq = Weights(wq_sl, D, 128)
        for b in range(NB):
            def cb_q(pb, co, cw, t0, tw):
                eact(p, f.qT[0:cw, t0:t0 + tw], pb[0:cw, 0:tw], AF.Identity,
                     bias=BQ[:, g:g + 1], reads=(pb, BQ), writes=(f.qT,),
                     label=label + "_qev")
            matmul_T(p, Wq, (lambda kb, t0, tw, b=b:
                             c.XN[:, kb * T + b * S + t0: kb * T + b * S + t0 + tw]),
                     cb_q, c.psums[0:2], Tlen=S, reads=(c.XN,), label=label + "_q")
            for ch in range(S // 512):
                emit_phi(p, c, f.qT, (slice(0, 128), slice(ch * 512, (ch + 1) * 512)),
                         phit, label + f"_phq{ch}")
            for which, wsl, bias, dstb in (("k", wk_sl, BK, f.att), ("v", wv_sl, BV, f.vx)):
                W_ = Weights(wsl, D, 128)
                for tt_i in range(16):
                    pb = c.psums[2 + (tt_i % 2)]
                    for kb in range(CT):
                        lhs = c.XN[:, kb * T + b * S + tt_i * 128:
                                   kb * T + b * S + (tt_i + 1) * 128]
                        et(p, (lambda tt, o=pb[0:128, 0:128], w=lhs,
                               r=W_.tile(kb, 0, 128),
                               st=(kb == 0), sp=(kb == CT - 1):
                               tt.matmul(o, w, r, start=st, stop=sp)),
                           reads=(c.XN, W_.buf), writes=(pb,), label=label + "_" + which)
                    eact(p, dstb[:, tt_i * 128:(tt_i + 1) * 128], pb[0:128, 0:128],
                         AF.Identity, bias=bias[:, g:g + 1],
                         reads=(pb, bias), writes=(dstb,), label=label + "_" + which + "ev")
                if which == "k":
                    for ch in range(4):
                        emit_phi(p, c, f.att,
                                 (slice(0, 128), slice(ch * 512, (ch + 1) * 512)),
                                 phit, label + f"_phk{ch}")
            for hh in range(2):
                pkv = c.psums[4]
                for tt_i in range(16):
                    et(p, (lambda tt, o=pkv[0:64, 0:64],
                           w=f.att[:, tt_i * 128 + hh * 64: tt_i * 128 + hh * 64 + 64],
                           r=f.vx[:, tt_i * 128 + hh * 64: tt_i * 128 + hh * 64 + 64],
                           st=(tt_i == 0), sp=False:
                           tt.matmul(o, w, r, start=st, stop=sp)),
                       reads=(f.att, f.vx), writes=(pkv,), label=label + "_kv")
                    et(p, (lambda tt, o=pkv[0:64, 64:65],
                           w=f.att[:, tt_i * 128 + hh * 64: tt_i * 128 + hh * 64 + 64],
                           r=c.ones_col_bf.ap,
                           st=False, sp=(tt_i == 15):
                           tt.matmul(o, w, r, start=st, stop=sp)),
                       reads=(f.att, c.ones_col_bf), writes=(pkv,), label=label + "_ks")
                eact(p, KVS[hh * 64:hh * 64 + 64, 0:65], pkv[0:64, 0:65], AF.Copy,
                     reads=(pkv,), writes=(KVS,), label=label + "_kvev")
            OC = f.pt[0]
            R0 = c.R[0]
            for ch in range(S // 512):
                for hh in range(2):
                    pr = c.prow[hh]
                    et(p, (lambda tt, o=pr[0:1, 0:512],
                           w=KVS[hh * 64:hh * 64 + 64, 64:65],
                           r=f.qT[hh * 64:hh * 64 + 64, ch * 512:(ch + 1) * 512]:
                           tt.matmul(o, w, r, start=True, stop=True)),
                       reads=(KVS, f.qT), writes=(pr,), label=label + "_dn")
                    eact(p, R0[0:1, 0:512], pr[0:1, 0:512], AF.Identity, bias=1e-6,
                         reads=(pr,), writes=(R0,), label=label + "_dne")
                    ev(p, (lambda v, o=R0[0:1, 0:512], i=R0[0:1, 0:512]:
                           v.reciprocal(o, i)),
                       reads=(R0,), writes=(R0,), label=label + "_dnr")
                    pb2 = c.psums[5]
                    et(p, (lambda tt, o=pb2[0:128, 0:512], w=c.ones_row_f.ap,
                           r=R0[0:1, 0:512]:
                           tt.matmul(o, w, r, start=True, stop=True)),
                       reads=(R0, c.ones_row_f), writes=(pb2,), label=label + "_dnb")
                    po = c.psums[2 + hh]
                    et(p, (lambda tt, o=po[0:64, 0:512],
                           w=KVS[hh * 64:hh * 64 + 64, 0:64],
                           r=f.qT[hh * 64:hh * 64 + 64, ch * 512:(ch + 1) * 512]:
                           tt.matmul(o, w, r, start=True, stop=True)),
                       reads=(KVS, f.qT), writes=(po,), label=label + "_o")
                    eact(p, f.ob[0:64, 0:512], po[0:64, 0:512], AF.Copy,
                         reads=(po,), writes=(f.ob,), label=label + "_oev")
                    ev(p, (lambda v, o=OC[hh * 64:hh * 64 + 64, 0:512],
                           a=f.ob[0:64, 0:512], bq=pb2[0:64, 0:512]:
                           v.tensor_tensor(o, a, bq, ALU.mult)),
                       reads=(f.ob, pb2), writes=(OC,), label=label + "_on")
                for co in range(CT):
                    pb = c.psums[0 + (co % 2)]
                    et(p, (lambda tt, o=pb[0:128, 0:512],
                           w=Wo.buf[:, co * 128:(co + 1) * 128], r=OC[:, 0:512]:
                           tt.matmul(o, w, r, start=True, stop=True)),
                       reads=(Wo.buf, OC), writes=(pb,), label=label + "_wo")
                    sl = slice(co * T + b * S + ch * 512, co * T + b * S + (ch + 1) * 512)
                    ev(p, (lambda v, o=c.XT[:, sl], x=c.XT[:, sl], a=pb[0:128, 0:512]:
                           v.tensor_tensor(o, x, a, ALU.add)),
                       reads=(c.XT, pb), writes=(c.XT,), label=label + "_woadd")
    for ct in range(CT):
        ev(p, (lambda v, o=c.XT[:, ct * T:(ct + 1) * T], x=c.XT[:, ct * T:(ct + 1) * T],
               bo=BO[:, ct:ct + 1]:
               v.tensor_scalar_add(o, x, bo)),
           reads=(c.XT, BO), writes=(c.XT,), label=label + "_bo")


def emit_l3_ffn(p, c, wd, label="l3f"):
    CW = p.sb(label + "cw", [128, 32 * 8], F32)
    CWR = p.sb(label + "cwr", [128, T], BF16)
    CWREP = p.sb(label + "cwrep", [128, T], BF16)
    SCB = p.sb(label + "scb", [128, 8], F32)
    RB = p.sb(label + "rb", [128, 8], F32)
    GS = p.sb(label + "gs", [128, 4 * 512], BF16)
    US = p.sb(label + "us", [128, 4 * 512], BF16)
    CVB = p.sb(label + "cvb", [128, S], BF16)
    identf = p.sb(label + "idf", [128, 128], F32)
    wr = load_weight(p, label + "wr", wd["Wr"], D, 8)
    wg_b = p.sb(label + "wg", [128, 8 * 512], BF16)
    wu_b = p.sb(label + "wu", [128, 8 * 512], BF16)
    wdn_b = p.sb(label + "wd", [128, 4 * 1024], BF16)
    CWC = load_const_sb(p, label + "cwc", wd["conv_w"], [128, 4 * CT], F32)
    CBC = load_const_sb(p, label + "cbc", wd["conv_b"], [128, CT], F32)
    rbr = nc_rb = wd["router_bias"]
    ev(p, lambda v: v.tensor_copy(identf.ap, c.ident_bf.ap),
       reads=(c.ident_bf,), writes=(identf,), label=label + "_idf")
    pb = c.psums[5]
    rbb = load_const_sb(p, label + "rbb", wd["router_bias"], [1, 8], F32)
    et(p, (lambda tt, o=pb[0:128, 0:8], w=c.ones_row_f.ap, r=rbb[0:1, 0:8]:
           tt.matmul(o, w, r, start=True, stop=True)),
       reads=(rbb, c.ones_row_f), writes=(pb,), label=label + "_rbb")
    eact(p, RB.ap, pb[0:128, 0:8], AF.Copy, reads=(pb,), writes=(RB,),
         label=label + "_rbe")
    for tt_i in range(32):
        pb = c.psums[0 + (tt_i % 2)]
        for kb in range(CT):
            lhs = c.XN[:, kb * T + tt_i * 128: kb * T + (tt_i + 1) * 128]
            et(p, (lambda tt, o=pb[0:128, 0:8], w=lhs, r=wr.tile(kb, 0, 8),
                   st=(kb == 0), sp=(kb == CT - 1):
                   tt.matmul(o, w, r, start=st, stop=sp)),
               reads=(c.XN, wr.buf), writes=(pb,), label=label + "_sc")
        s_ = CW[:, tt_i * 8:(tt_i + 1) * 8]
        eact(p, s_, pb[0:128, 0:8], AF.Sigmoid, reads=(pb,), writes=(CW,),
             label=label + "_sce")
        r_ = SCB[:, 0:8]
        ev(p, (lambda v, o=r_, a=s_, bq=RB.ap: v.tensor_tensor(o, a, bq, ALU.add)),
           reads=(CW, RB), writes=(SCB,), label=label + "_r")
        m1 = c.R[0][:, 0:1]
        ev(p, (lambda v, o=m1, i=r_: v.reduce_max(o, i, AX.X)),
           reads=(SCB,), writes=(c.R[0],), label=label + "_m1")
        mk = c.R[1][:, 0:8]
        ev2(p, (lambda v, o=mk, i=r_, s=m1: v.tensor_scalar(o, i, s, None, ALU.is_ge)),
            reads=(SCB,), sreads=(c.R[0],), writes=(c.R[1],), label=label + "_k1")
        ev(p, (lambda v, o=r_, i=mk, r2=r_:
               v.scalar_tensor_tensor(o, i, -1e30, r2, ALU.mult, ALU.add)),
           reads=(c.R[1], SCB), writes=(SCB,), label=label + "_r2")
        ev(p, (lambda v, o=m1, i=r_: v.reduce_max(o, i, AX.X)),
           reads=(SCB,), writes=(c.R[0],), label=label + "_m2")
        mk2 = c.R[2][:, 0:8]
        ev2(p, (lambda v, o=mk2, i=r_, s=m1: v.tensor_scalar(o, i, s, None, ALU.is_ge)),
            reads=(SCB,), sreads=(c.R[0],), writes=(c.R[2],), label=label + "_k2")
        ev(p, (lambda v, o=mk, a=mk, bq=mk2: v.tensor_tensor(o, a, bq, ALU.add)),
           reads=(c.R[1], c.R[2]), writes=(c.R[1],), label=label + "_ms")
        ev(p, (lambda v, o=s_ if False else SCB[:, 0:8], a=s_, bq=mk:
               v.tensor_tensor(o, a, bq, ALU.mult)),
           reads=(CW, c.R[1]), writes=(SCB,), label=label + "_ws")
        ev(p, (lambda v, o=m1, i=SCB[:, 0:8]: v.reduce_sum(o, i, AX.X)),
           reads=(SCB,), writes=(c.R[0],), label=label + "_dn")
        ev(p, (lambda v, o=m1, i=m1: v.tensor_scalar_add(o, i, 1e-9)),
           reads=(c.R[0],), writes=(c.R[0],), label=label + "_dne")
        ev(p, (lambda v, o=m1, i=m1: v.reciprocal(o, i)),
           reads=(c.R[0],), writes=(c.R[0],), label=label + "_dnr")
        ev2(p, (lambda v, o=s_, i=SCB[:, 0:8], s=m1:
                v.tensor_scalar_mul(o, i, s)),
            reads=(SCB,), sreads=(c.R[0],), writes=(CW,), label=label + "_cw")
    for e in range(4):
        Wg = load_weight(p, label + f"wg{e}", wd[f"Wg{e}"], D, 512, buf=wg_b)
        Wu = load_weight(p, label + f"wu{e}", wd[f"Wu{e}"], D, 512, buf=wu_b)
        Wd = load_weight(p, label + f"wd{e}", wd[f"Wd{e}"], 512, 1024, buf=wdn_b)
        for g0 in range(0, 32, 4):
            pr = c.prow[0]
            for j in range(4):
                tt_i = g0 + j
                et(p, (lambda tt, o=pr[0:1, j * 128:(j + 1) * 128],
                       w=CW[:, tt_i * 8 + e: tt_i * 8 + e + 1], r=identf.ap,
                       st=(j == 0), sp=(j == 3):
                       tt.matmul(o, w, r, start=st, stop=sp)),
                   reads=(CW, identf), writes=(pr,), label=label + "_cwr")
            eact(p, CWR[0:1, g0 * 128:(g0 + 4) * 128], pr[0:1, 0:512], AF.Copy,
                 reads=(pr,), writes=(CWR,), label=label + "_cwre")
        for ch in range(T // 512):
            pb = c.psums[5]
            et(p, (lambda tt, o=pb[0:128, 0:512], w=c.ones_row_bf.ap,
                   r=CWR[0:1, ch * 512:(ch + 1) * 512]:
                   tt.matmul(o, w, r, start=True, stop=True)),
               reads=(CWR, c.ones_row_bf), writes=(pb,), label=label + "_cwb")
            eact(p, CWREP[:, ch * 512:(ch + 1) * 512], pb[0:128, 0:512], AF.Copy,
                 reads=(pb,), writes=(CWREP,), label=label + "_cwbe")
        for tch in range(T // 512):
            t0 = tch * 512
            def rhs(kb, tq0, tw, t0=t0):
                return c.XN[:, kb * T + t0 + tq0: kb * T + t0 + tq0 + tw]
            def cb_g(pb_, co, cw_, tq0, tw):
                eact(p, GS[:, (co // 128) * 512 + tq0:(co // 128) * 512 + tq0 + tw],
                     pb_[0:128, 0:tw], AF.Silu, reads=(pb_,), writes=(GS,),
                     label=label + "_ge")
            matmul_T(p, Wg, rhs, cb_g, c.psums[0:2], Tlen=512, reads=(c.XN,),
                     label=label + "_g")
            def cb_u(pb_, co, cw_, tq0, tw, t0=t0):
                dst = US[:, (co // 128) * 512 + tq0:(co // 128) * 512 + tq0 + tw]
                ev(p, (lambda v, o=dst, a=pb_[0:128, 0:tw],
                       bq=CWREP[:, t0 + tq0: t0 + tq0 + tw]:
                       v.tensor_tensor(o, a, bq, ALU.mult)),
                   reads=(pb_, CWREP), writes=(US,), label=label + "_ue")
            matmul_T(p, Wu, rhs, cb_u, c.psums[2:4], Tlen=512, reads=(c.XN,),
                     label=label + "_u")
            ev(p, (lambda v, o=GS.ap, a=GS.ap, bq=US.ap:
                   v.tensor_tensor(o, a, bq, ALU.mult)),
               reads=(GS, US), writes=(GS,), label=label + "_h")
            for co in range(CT):
                pb_ = c.psums[4 + (co % 2)]
                for kb in range(4):
                    et(p, (lambda tt, o=pb_[0:128, 0:512],
                           w=Wd.tile(kb, co * 128, 128),
                           r=GS[:, kb * 512:(kb + 1) * 512],
                           st=(kb == 0), sp=(kb == 3):
                           tt.matmul(o, w, r, start=st, stop=sp)),
                       reads=(Wd.buf, GS), writes=(pb_,), label=label + "_d")
                sl = slice(co * T + t0, co * T + t0 + 512)
                ev(p, (lambda v, o=c.XT[:, sl], x=c.XT[:, sl], a=pb_[0:128, 0:512]:
                       v.tensor_tensor(o, x, a, ALU.add)),
                   reads=(c.XT, pb_), writes=(c.XT,), label=label + "_dadd")
    for which, exps in (("46", (4, 6)), ("7", (7,))):
        for g0 in range(0, 32, 4):
            pr = c.prow[0]
            first = True
            for e in exps:
                for j in range(4):
                    tt_i = g0 + j
                    et(p, (lambda tt, o=pr[0:1, j * 128:(j + 1) * 128],
                           w=CW[:, tt_i * 8 + e: tt_i * 8 + e + 1], r=identf.ap,
                           st=(first and j == 0), sp=(e == exps[-1] and j == 3):
                           tt.matmul(o, w, r, start=st, stop=sp)),
                       reads=(CW, identf), writes=(pr,), label=label + "_shr")
                first = False
            eact(p, CWR[0:1, g0 * 128:(g0 + 4) * 128], pr[0:1, 0:512], AF.Copy,
                 reads=(pr,), writes=(CWR,), label=label + "_shre")
        for ch in range(T // 512):
            pb = c.psums[5]
            et(p, (lambda tt, o=pb[0:128, 0:512], w=c.ones_row_bf.ap,
                   r=CWR[0:1, ch * 512:(ch + 1) * 512]:
                   tt.matmul(o, w, r, start=True, stop=True)),
               reads=(CWR, c.ones_row_bf), writes=(pb,), label=label + "_shb")
            eact(p, CWREP[:, ch * 512:(ch + 1) * 512], pb[0:128, 0:512], AF.Copy,
                 reads=(pb,), writes=(CWREP,), label=label + "_shbe")
        for ct in range(CT):
            for ch in range(T // 512):
                sl = slice(ct * T + ch * 512, ct * T + (ch + 1) * 512)
                if which == "46":
                    ev(p, (lambda v, o=c.sqc[:, 0:512], a=c.XN[:, sl],
                           bq=CWREP[:, ch * 512:(ch + 1) * 512]:
                           v.tensor_tensor(o, a, bq, ALU.mult)),
                       reads=(c.XN, CWREP), writes=(c.sqc,), label=label + "_t46")
                else:
                    ev(p, (lambda v, o=c.sqc[:, 0:512], a=c.XN[:, sl]:
                           v.tensor_relu(o, a)),
                       reads=(c.XN,), writes=(c.sqc,), label=label + "_rl")
                    ev(p, (lambda v, o=c.sqc[:, 0:512], a=c.sqc[:, 0:512],
                           bq=CWREP[:, ch * 512:(ch + 1) * 512]:
                           v.tensor_tensor(o, a, bq, ALU.mult)),
                       reads=(c.sqc, CWREP), writes=(c.sqc,), label=label + "_t7")
                ev(p, (lambda v, o=c.XT[:, sl], x=c.XT[:, sl], a=c.sqc[:, 0:512]:
                       v.tensor_tensor(o, x, a, ALU.add)),
                   reads=(c.XT, c.sqc), writes=(c.XT,), label=label + "_sha")
    for ct in range(CT):
        for b in range(NB):
            base = ct * T + b * S
            eact(p, CVB[:, 0:S], c.XN[:, base:base + S], AF.Identity,
                 scale=CWC[:, 3 * CT + ct: 3 * CT + ct + 1],
                 bias=CBC[:, ct:ct + 1],
                 reads=(c.XN, CWC, CBC), writes=(CVB,), label=label + "_cv3")
            for k in range(3):
                off = 3 - k
                ev(p, (lambda v, o=CVB[:, off:S], a=c.XN[:, base:base + S - off],
                       w=CWC[:, k * CT + ct: k * CT + ct + 1], g=CVB[:, off:S]:
                       v.scalar_tensor_tensor(o, a, w, g, ALU.mult, ALU.add)),
                   reads=(c.XN, CWC, CVB), writes=(CVB,), label=label + f"_cv{k}")
            ev(p, (lambda v, o=c.XT[:, base:base + S], x=c.XT[:, base:base + S],
                   a=CVB[:, 0:S]:
                   v.tensor_tensor(o, x, a, ALU.add)),
               reads=(c.XT, CVB), writes=(c.XT,), label=label + "_cva")


def emit_l0_mix(p, c, wd, s1rep, label="l0m", chw=512, stream_w=False, nc=None,
                s1_dram=None):
    names = ("Wx", "Wy", "Wi", "Wa", "Wo")
    N1 = load_const_sb(p, label + "n1", wd["norm1_pcol"], [128, CT], F32)
    if stream_w:
        wdram = {}
        pre = p.sb(label + "pre", [128, CT * D], BF16)
        for nm in names:
            wdram[nm] = Buf(nc.dram_tensor(label + "_s" + nm, [D, D], BF16,
                                           kind="Internal"), nm)
            if nm in ("Wx", "Wy"):
                ap_in = wd[nm].ap().rearrange("(kb q) n -> q kb n", q=128)
                ap_out = pre.ap.rearrange("q (kb n) -> q kb n", kb=CT)
                p.dma("gpsimd", ap_out, ap_in, reads=(pre,), writes=(pre,),
                      label=label + "_pl" + nm)
                for kb in range(CT):
                    ev2(p, (lambda v, o=pre[:, kb * D:(kb + 1) * D],
                            i=pre[:, kb * D:(kb + 1) * D], s=N1[:, kb:kb + 1]:
                            v.tensor_scalar_mul(o, i, s)),
                        reads=(pre,), sreads=(N1,), writes=(pre,),
                        label=label + "_pf")
                ap_d = wdram[nm].t.ap().rearrange("(kb q) n -> q kb n", q=128)
                p.dma("gpsimd", ap_d, ap_out, reads=(pre,), writes=(wdram[nm],),
                      label=label + "_ps" + nm)
            else:
                p.dma("gpsimd", wdram[nm].t.ap(), wd[nm].ap(),
                      writes=(wdram[nm],), label=label + "_pre" + nm)
        wslot = [p.sb(label + f"wsl{i}", [128, CT * 512], BF16) for i in range(2)]
        Wres = None
    else:
        Wres = {nm: load_weight(p, label + nm.lower(), wd[nm], D, D)
                for nm in names}
        for W_ in (Wres["Wx"], Wres["Wy"]):
            for kb in range(CT):
                ev2(p, (lambda v, o=W_.buf[:, kb * D:(kb + 1) * D],
                        i=W_.buf[:, kb * D:(kb + 1) * D], s=N1[:, kb:kb + 1]:
                        v.tensor_scalar_mul(o, i, s)),
                    reads=(W_.buf,), sreads=(N1,), writes=(W_.buf,), label=label + "_n1f")
    CWC = load_const_sb(p, label + "cwc", wd["conv_w_pcol"], [128, 4 * CT], F32)
    CBC = load_const_sb(p, label + "cbc", wd["conv_b_pcol"], [128, CT], F32)
    LAM = load_const_sb(p, label + "lam", wd["lam_pcol"], [128, CT], F32)
    LAMS = p.sb(label + "lams", [128, CT], F32)
    T1 = c.R[0][:, 0:CT]
    T2 = c.R[1][:, 0:CT]
    eact(p, T1, LAM.ap, AF.Abs, reads=(LAM,), writes=(c.R[0],),
         label=label + "_sp1")
    eact(p, T1, T1, AF.Exp, scale=-1.0, reads=(c.R[0],), writes=(c.R[0],),
         label=label + "_sp2")
    eact(p, T1, T1, AF.Ln, bias=1.0, reads=(c.R[0],), writes=(c.R[0],),
         label=label + "_sp3")
    ev(p, lambda v: v.tensor_relu(T2, LAM.ap), reads=(LAM,), writes=(c.R[1],),
       label=label + "_sp4")
    ev(p, lambda v: v.tensor_tensor(T1, T1, T2, ALU.add),
       reads=(c.R[0], c.R[1]), writes=(c.R[0],), label=label + "_sp5")
    ev(p, lambda v: v.tensor_scalar_mul(LAMS.ap, T1, -8.0),
       reads=(c.R[0],), writes=(LAMS,), label=label + "_sp6")

    UC = p.sb(label + "uc", [128, CT * chw], BF16)
    XB = p.sb(label + "xb", [128, CT * (chw + 4)], BF16)
    CV = p.sb(label + "cv", [128, CT * chw], BF16)
    RS = p.sb(label + "rs", [128, CT * chw], BF16)
    IS = p.sb(label + "is", [128, CT * chw], BF16)
    YB = p.sb(label + "yb", [128, CT * chw], BF16)
    HS = CV
    TMP = p.sb(label + "tmp", [128, chw], BF16)
    CAR = p.sb(label + "car", [128, CT], F32)
    S1C = p.sb(label + "s1c", [128, chw], BF16) if s1_dram is not None else None
    XBW = chw + 4
    nch = S // chw
    widx = [0]

    def mm_block(nm, rhsbuf, rhs_block, evac, Wfull):
        nsub = (chw + 511) // 512
        if stream_w:
            parts = []
            for h in range(2):
                sl = wslot[widx[0] % 2]
                widx[0] += 1
                ap_in = wdram[nm].t.ap()[:, h * 512:(h + 1) * 512].rearrange(
                    "(kb q) n -> q kb n", q=128)
                ap_out = sl.ap.rearrange("q (kb n) -> q kb n", kb=CT)
                p.dma("sync", ap_out, ap_in, reads=(wdram[nm],), writes=(sl,),
                      label=label + "_ldw" + nm)
                parts.append((Weights(sl, D, 512, stride=512), h * 4, 4))
        else:
            parts = [(Wfull, 0, CT)]
        for Wm, co0, nco in parts:
            for col in range(nco):
                co = co0 + col
                for si in range(nsub):
                    w0 = si * 512
                    ww = min(512, chw - w0)
                    pb = c.psums[(co * nsub + si) % 2]
                    for kb in range(CT):
                        et(p, (lambda tt, o=pb[0:128, 0:ww],
                               w=Wm.tile(kb, col * 128, 128),
                               r=rhs_block(kb, w0, ww), st=(kb == 0),
                               sp=(kb == CT - 1):
                               tt.matmul(o, w, r, start=st, stop=sp)),
                           reads=(Wm.buf, rhsbuf), writes=(pb,), label=label + "_mm")
                    evac(pb, co, w0, ww)

    for b in range(NB):
        ev(p, lambda v: v.memset(CAR.ap, 0.0), writes=(CAR,), label=label + "_car0")
        for ct in range(CT):
            ev(p, (lambda v, o=XB[:, ct * XBW:ct * XBW + 3]:
                   v.memset(o, 0.0)),
               writes=(XB,), label=label + "_hal0")
        for ch in range(nch):
            t0 = b * S + ch * chw
            if s1_dram is not None:
                p.dma("gpsimd", S1C.ap, s1_dram.ap()[:, t0:t0 + chw],
                      reads=(S1C,), writes=(S1C,), label=label + "_lds1")
                s1ap = lambda: S1C[:, 0:chw]
            else:
                s1ap = lambda: s1rep[:, t0:t0 + chw]
            s1b = S1C if s1_dram is not None else s1rep
            for ct in range(CT):
                ev(p, (lambda v, o=UC[:, ct * chw:(ct + 1) * chw],
                       a=c.XN[:, ct * T + t0: ct * T + t0 + chw],
                       s=s1ap():
                       v.tensor_tensor(o, a, s, ALU.mult)),
                   reads=(c.XN, s1b), writes=(UC,), label=label + "_uc")
            if ch > 0:
                for ct in range(CT):
                    ev(p, (lambda v, o=XB[:, ct * XBW:ct * XBW + 3],
                           i=XB[:, ct * XBW + chw:ct * XBW + chw + 3]:
                           v.tensor_copy(o, i)),
                       reads=(XB,), writes=(XB,), label=label + "_halc")
            mm_block("Wx", UC, lambda kb, w0, ww: UC[:, kb * chw + w0:kb * chw + w0 + ww],
                     lambda pb, co, w0, ww: eact(p, XB[:, co * XBW + 3 + w0:co * XBW + 3 + w0 + ww],
                                                 pb[0:128, 0:ww], AF.Copy,
                                                 reads=(pb,), writes=(XB,),
                                                 label=label + "_xbev"),
                     Wres["Wx"] if Wres else None)
            mm_block("Wy", UC, lambda kb, w0, ww: UC[:, kb * chw + w0:kb * chw + w0 + ww],
                     lambda pb, co, w0, ww: eact(p, YB[:, co * chw + w0:co * chw + w0 + ww],
                                                 pb[0:128, 0:ww], AF.Gelu_apprx_tanh,
                                                 reads=(pb,), writes=(YB,),
                                                 label=label + "_yev"),
                     Wres["Wy"] if Wres else None)
            for ct in range(CT):
                eact(p, CV[:, ct * chw:(ct + 1) * chw],
                     XB[:, ct * XBW + 3:ct * XBW + 3 + chw], AF.Identity,
                     scale=CWC[:, 3 * CT + ct:3 * CT + ct + 1],
                     bias=CBC[:, ct:ct + 1],
                     reads=(XB, CWC, CBC), writes=(CV,), label=label + "_cv3")
                for k in range(3):
                    ev2(p, (lambda v, o=CV[:, ct * chw:(ct + 1) * chw],
                            a=XB[:, ct * XBW + k:ct * XBW + k + chw],
                            w=CWC[:, k * CT + ct:k * CT + ct + 1],
                            g=CV[:, ct * chw:(ct + 1) * chw]:
                            v.scalar_tensor_tensor(o, a, w, g, ALU.mult, ALU.add)),
                        reads=(XB, CV), sreads=(CWC,), writes=(CV,),
                        label=label + f"_cv{k}")
            mm_block("Wa", CV, lambda kb, w0, ww: CV[:, kb * chw + w0:kb * chw + w0 + ww],
                     lambda pb, co, w0, ww: eact(p, RS[:, co * chw + w0:co * chw + w0 + ww],
                                                 pb[0:128, 0:ww], AF.Sigmoid,
                                                 reads=(pb,), writes=(RS,),
                                                 label=label + "_rev"),
                     Wres["Wa"] if Wres else None)
            mm_block("Wi", CV, lambda kb, w0, ww: CV[:, kb * chw + w0:kb * chw + w0 + ww],
                     lambda pb, co, w0, ww: eact(p, IS[:, co * chw + w0:co * chw + w0 + ww],
                                                 pb[0:128, 0:ww], AF.Sigmoid,
                                                 reads=(pb,), writes=(IS,),
                                                 label=label + "_iev"),
                     Wres["Wi"] if Wres else None)
            for ct in range(CT):
                sl = slice(ct * chw, (ct + 1) * chw)
                ev2(p, (lambda v, o=RS[:, sl], i=RS[:, sl], s=LAMS[:, ct:ct + 1]:
                        v.tensor_scalar_mul(o, i, s)),
                    reads=(RS,), sreads=(LAMS,), writes=(RS,), label=label + "_la")
                eact(p, RS[:, sl], RS[:, sl], AF.Exp, reads=(RS,), writes=(RS,),
                     label=label + "_a")
                ev(p, (lambda v, o=TMP.ap, a=RS[:, sl]:
                       v.tensor_tensor(o, a, a, ALU.mult)),
                   reads=(RS,), writes=(TMP,), label=label + "_a2")
                ev(p, (lambda v, o=TMP.ap, i=TMP.ap:
                       v.tensor_scalar(o, i, -1.0, 1.0, ALU.mult, ALU.add)),
                   reads=(TMP,), writes=(TMP,), label=label + "_om")
                ev(p, (lambda v, o=TMP.ap, i=TMP.ap:
                       v.tensor_scalar_max(o, i, 1e-12)),
                   reads=(TMP,), writes=(TMP,), label=label + "_mx")
                eact(p, TMP.ap, TMP.ap, AF.Sqrt, reads=(TMP,), writes=(TMP,),
                     label=label + "_sq")
                ev(p, (lambda v, o=IS[:, sl], a=IS[:, sl], bq=CV[:, sl]:
                       v.tensor_tensor(o, a, bq, ALU.mult)),
                   reads=(IS, CV), writes=(IS,), label=label + "_ix")
                ev(p, (lambda v, o=IS[:, sl], a=IS[:, sl], bq=TMP.ap:
                       v.tensor_tensor(o, a, bq, ALU.mult)),
                   reads=(IS, TMP), writes=(IS,), label=label + "_b")
                ev2(p, (lambda v, o=HS[:, sl], a=RS[:, sl], bq=IS[:, sl],
                        init=CAR[:, ct:ct + 1]:
                        v.tensor_tensor_scan(o, a, bq, init, ALU.mult, ALU.add)),
                    reads=(RS, IS), sreads=(CAR,), writes=(HS,), label=label + "_scan")
                ev(p, (lambda v, o=CAR[:, ct:ct + 1],
                       i=HS[:, ct * chw + chw - 1:ct * chw + chw]:
                       v.tensor_copy(o, i)),
                   reads=(HS,), writes=(CAR,), label=label + "_carc")
                ev(p, (lambda v, o=HS[:, sl], a=HS[:, sl], bq=YB[:, sl]:
                       v.tensor_tensor(o, a, bq, ALU.mult)),
                   reads=(HS, YB), writes=(HS,), label=label + "_hy")
            mm_block("Wo", HS, lambda kb, w0, ww: HS[:, kb * chw + w0:kb * chw + w0 + ww],
                     lambda pb, co, w0, ww: ev(p, (lambda v,
                                                   o=c.XN[:, co * T + t0 + w0: co * T + t0 + w0 + ww],
                                                   x=c.XN[:, co * T + t0 + w0: co * T + t0 + w0 + ww],
                                                   a=pb[0:128, 0:ww]:
                                                   v.tensor_tensor(o, x, a, ALU.add)),
                                               reads=(c.XN, pb), writes=(c.XN,),
                                               label=label + "_h1"),
                     Wres["Wo"] if Wres else None)


def emit_l0_ffn_int(p, c, wd, label="l0i"):
    S2 = p.sb(label + "s2", [128, T], BF16)
    emit_rms_rep(p, c, c.XN, S2, 1e-6, label=label + "_rms")
    for ct in range(CT):
        ev(p, (lambda v, o=c.XT[:, ct * T:(ct + 1) * T],
               x=c.XT[:, ct * T:(ct + 1) * T], a=c.XN[:, ct * T:(ct + 1) * T]:
               v.tensor_tensor(o, x, a, ALU.add)),
           reads=(c.XT, c.XN), writes=(c.XT,), label=label + "_xh1")
    N2 = load_const_sb(p, label + "n2", wd["norm2_pcol"], [128, CT], F32)
    GW = 512
    NCO = GW // 128
    wgb = p.sb(label + "wgb", [128, CT * GW], BF16)
    wub = p.sb(label + "wub", [128, CT * GW], BF16)
    wdb = p.sb(label + "wdb", [128, (GW // 128) * 1024], BF16)
    UC = p.sb(label + "uc", [128, CT * 512], BF16)
    GH = p.sb(label + "gh", [128, NCO * 512], BF16)
    for hidg in range(4096 // GW):
        Wg = load_weight_cols(p, wgb, wd["Wg"], D, hidg * GW, GW,
                              name=label + f"wg{hidg}")
        Wu = load_weight_cols(p, wub, wd["Wu"], D, hidg * GW, GW,
                              name=label + f"wu{hidg}")
        KB = GW // 128
        ap_in = wd["Wd"].ap()[hidg * GW:(hidg + 1) * GW, :].rearrange(
            "(kb q) n -> q kb n", q=128)
        ap_out = wdb.ap.rearrange("q (kb n) -> q kb n", kb=KB)
        p.dma("gpsimd", ap_out, ap_in, reads=(wdb,), writes=(wdb,),
              label=label + f"ldwd{hidg}")
        Wd = Weights(wdb, GW, 1024)
        for W_ in (Wg, Wu):
            for kb in range(CT):
                ev2(p, (lambda v, o=W_.buf[:, kb * GW:(kb + 1) * GW],
                        i=W_.buf[:, kb * GW:(kb + 1) * GW], s=N2[:, kb:kb + 1]:
                        v.tensor_scalar_mul(o, i, s)),
                    reads=(W_.buf,), sreads=(N2,), writes=(W_.buf,),
                    label=label + "_n2f")
        for ch in range(T // 512):
            t0 = ch * 512
            for ct in range(CT):
                ev(p, (lambda v, o=UC[:, ct * 512:(ct + 1) * 512],
                       a=c.XN[:, ct * T + t0: ct * T + t0 + 512],
                       s=S2[:, t0:t0 + 512]:
                       v.tensor_tensor(o, a, s, ALU.mult)),
                   reads=(c.XN, S2), writes=(UC,), label=label + "_uc")
            for co in range(NCO):
                pb = c.psums[co % 2]
                for kb in range(CT):
                    et(p, (lambda tt, o=pb[0:128, 0:512],
                           w=Wg.tile(kb, co * 128, 128),
                           r=UC[:, kb * 512:(kb + 1) * 512],
                           st=(kb == 0), sp=(kb == CT - 1):
                           tt.matmul(o, w, r, start=st, stop=sp)),
                       reads=(wgb, UC), writes=(pb,), label=label + "_g")
                eact(p, GH[:, co * 512:(co + 1) * 512], pb[0:128, 0:512],
                     AF.Gelu_apprx_tanh, reads=(pb,), writes=(GH,),
                     label=label + "_gev")
            for co in range(NCO):
                pb = c.psums[2 + co % 2]
                for kb in range(CT):
                    et(p, (lambda tt, o=pb[0:128, 0:512],
                           w=Wu.tile(kb, co * 128, 128),
                           r=UC[:, kb * 512:(kb + 1) * 512],
                           st=(kb == 0), sp=(kb == CT - 1):
                           tt.matmul(o, w, r, start=st, stop=sp)),
                       reads=(wub, UC), writes=(pb,), label=label + "_u")
                ev(p, (lambda v, o=GH[:, co * 512:(co + 1) * 512],
                       a=pb[0:128, 0:512], bq=GH[:, co * 512:(co + 1) * 512]:
                       v.tensor_tensor(o, a, bq, ALU.mult)),
                   reads=(pb, GH), writes=(GH,), label=label + "_uev")
            for co in range(CT):
                pb = c.psums[4 + co % 2]
                for kb in range(GW // 128):
                    et(p, (lambda tt, o=pb[0:128, 0:512],
                           w=Wd.tile(kb, co * 128, 128),
                           r=GH[:, kb * 512:(kb + 1) * 512],
                           st=(kb == 0), sp=(kb == GW // 128 - 1):
                           tt.matmul(o, w, r, start=st, stop=sp)),
                       reads=(wdb, GH), writes=(pb,), label=label + "_d")
                sl = slice(co * T + t0, co * T + t0 + 512)
                ev(p, (lambda v, o=c.XT[:, sl], x=c.XT[:, sl], a=pb[0:128, 0:512]:
                       v.tensor_tensor(o, x, a, ALU.add)),
                   reads=(c.XT, pb), writes=(c.XT,), label=label + "_dadd")


FB = S + 1
FCB = NB * FB


def fcol(ct, b, t):
    return ct * FCB + b * FB + 1 + t


def alloc_f(p):
    F = p.sb("Fstream", [128, CT * FCB], BF16)
    return F


def emit_load_f(p, F, dram, queue="gpsimd"):
    for b in range(NB):
        for ct in range(CT):
            p.dma(queue, F[:, fcol(ct, b, 0): fcol(ct, b, 0) + S],
                  dram.ap()[b, ct * 128:(ct + 1) * 128, :], writes=(F,),
                  label=f"ldf{b}{ct}")
    for b in range(NB):
        for ct in range(CT):
            ev(p, (lambda v, o=F[:, fcol(ct, b, 0) - 1: fcol(ct, b, 0)]:
                   v.memset(o, 0.0)),
               writes=(F,), label="fguard")


def emit_store_f(p, F, dram, queue="gpsimd"):
    for b in range(NB):
        for ct in range(CT):
            p.dma(queue, dram.ap()[b, ct * 128:(ct + 1) * 128, :],
                  F[:, fcol(ct, b, 0): fcol(ct, b, 0) + S], reads=(F,),
                  label=f"stf{b}{ct}")


def emit_l0f_attn(p, c, f, F, wd, masks, l, label="lfa"):
    wq_sl = p.sb_shared("lfa_wq", [128, CT * 128], BF16)
    wk_sl = p.sb_shared("lfa_wk", [128, CT * 128], BF16)
    wv_sl = p.sb_shared("lfa_wv", [128, CT * 128], BF16)
    wo_sl = p.sb_shared("lfa_wo", [128, D], BF16)
    QS = load_const_sb(p, label + "qs", wd[f"qscale{l}"], [128, 1], F32)
    KS = load_const_sb(p, label + "ks", wd[f"kscale{l}"], [128, 1], F32)
    SW = load_const_sb(p, "l0fsw", wd["swap64"], [128, 64], BF16, shared=True)
    QC = load_const_sb(p, f"l0ftqc{l}", wd["tab_qc"], [128, S], BF16, shared=True)
    QSi = load_const_sb(p, f"l0ftqs{l}", wd["tab_qs"], [128, S], BF16, shared=True)
    KC = load_const_sb(p, f"l0ftkc{l}", wd["tab_kc"], [128, S], BF16, shared=True)
    KSi = load_const_sb(p, f"l0ftks{l}", wd["tab_ks"], [128, S], BF16, shared=True)
    SQ = p.sb_shared("lfa_sqb", [128, S], BF16)
    QR2 = p.sb_shared("lfa_qr2", [128, S], BF16)
    wqkv = wd[f"wqkv{l}"]

    def rhs_sh(kb, b, t0, tw):
        off = 1 if kb >= 4 else 0
        return F[:, fcol(kb, b, t0) - off: fcol(kb, b, t0) - off + tw]

    def a_spec(qt):
        out = []
        k0 = max(0, qt * 128 - 512)
        kend = qt * 128 + 128
        while k0 < kend:
            kw = min(512, kend - k0)
            d0 = qt * 128 - k0
            out.append((k0, kw, (masks["l0A"], 512 - d0)))
            k0 += kw
        return out

    def c_spec(qc):
        out = []
        for kt in range(max(0, qc * 4 - 4), qc * 4 + 4):
            dc = qc * 512 - kt * 128
            out.append((kt * 128, (masks["l0C"], 512 + dc)))
        return out

    for g in range(4):
        Wq = load_weight_cols(p, wq_sl, wqkv, D, g * 128, 128, name=label + f"q{g}")
        Wk = load_weight_cols(p, wk_sl, wqkv, D, 512 + g * 128, 128, name=label + f"k{g}")
        Wv = load_weight_cols(p, wv_sl, wqkv, D, 1024 + g * 128, 128, name=label + f"v{g}")
        Wo = load_weight_rows(p, wo_sl, wd[f"wo{l}"], g * 128, 128, D,
                              name=label + f"o{g}")
        for b in range(NB):
            for dst, W_, lab in ((f.qT, Wq, "_q"), (f.kT, Wk, "_k")):
                for co in range(1):
                    for tg0 in range(0, 4, 2):
                        for kb in range(CT):
                            for ti in (tg0, tg0 + 1):
                                pb = c.psums[ti - tg0]
                                et(p, (lambda tt, o=pb[0:128, 0:512],
                                       w=W_.tile(kb, 0, 128),
                                       r=rhs_sh(kb, b, ti * 512, 512),
                                       st=(kb == 0), sp=(kb == CT - 1):
                                       tt.matmul(o, w, r, start=st, stop=sp)),
                                   reads=(W_.buf, F), writes=(pb,), label=label + lab)
                        for ti in (tg0, tg0 + 1):
                            pb = c.psums[ti - tg0]
                            eact(p, dst[:, ti * 512:(ti + 1) * 512], pb[0:128, 0:512],
                                 AF.Copy, reads=(pb,), writes=(dst,),
                                 label=label + lab + "ev")
            ev(p, lambda v: v.memset(f.vx.ap, 1.0), writes=(f.vx,), label=label + "_vo")
            for tt_i in range(16):
                pb = c.psums[2 + (tt_i % 2)]
                for kb in range(CT):
                    off = 1 if kb >= 4 else 0
                    lhs = F[:, fcol(kb, b, tt_i * 128) - off:
                            fcol(kb, b, tt_i * 128) - off + 128]
                    et(p, (lambda tt, o=pb[0:128, 0:128], w=lhs, r=Wv.tile(kb, 0, 128),
                           st=(kb == 0), sp=(kb == CT - 1):
                           tt.matmul(o, w, r, start=st, stop=sp)),
                       reads=(F, Wv.buf), writes=(pb,), label=label + "_v")
                for hh in range(2):
                    eact(p, f.vx[:, tt_i * 130 + hh * 65: tt_i * 130 + hh * 65 + 64],
                         pb[0:128, hh * 64:(hh + 1) * 64], AF.Copy,
                         reads=(pb,), writes=(f.vx,), label=label + "_vev")
            for buf, tabC, tabS, psc, extra in ((f.qT, QC, QSi, QS, 8.0), (f.kT, KC, KSi, KS, None)):
                eact(p, SQ.ap, buf.ap, AF.Square, reads=(buf,), writes=(SQ,),
                     label=label + "_sq")
                for hh in range(2):
                    for ch in range(4):
                        pr = c.prow[ch % 2]
                        et(p, (lambda tt, o=pr[0:1, 0:512],
                               w=c.ones_col_bf[hh * 64:hh * 64 + 64, 0:1],
                               r=SQ[hh * 64:hh * 64 + 64, ch * 512:(ch + 1) * 512]:
                               tt.matmul(o, w, r, start=True, stop=True)),
                           reads=(SQ, c.ones_col_bf), writes=(pr,), label=label + "_ss")
                        R0 = c.R[0]
                        eact(p, R0[0:1, 0:512], pr[0:1, 0:512], AF.Sqrt,
                             reads=(pr,), writes=(R0,), label=label + "_nrm")
                        ev(p, (lambda v, o=R0[0:1, 0:512], i=R0[0:1, 0:512]:
                               v.tensor_scalar_max(o, i, 1e-12)),
                           reads=(R0,), writes=(R0,), label=label + "_cl")
                        ev(p, (lambda v, o=R0[0:1, 0:512], i=R0[0:1, 0:512]:
                               v.reciprocal(o, i)),
                           reads=(R0,), writes=(R0,), label=label + "_rc")
                        if extra is not None:
                            ev(p, (lambda v, o=R0[0:1, 0:512], i=R0[0:1, 0:512],
                                   e_=extra: v.tensor_scalar_mul(o, i, e_)),
                               reads=(R0,), writes=(R0,), label=label + "_x8")
                        pbb = c.psums[4]
                        et(p, (lambda tt, o=pbb[0:128, 0:512], w=c.ones_row_f.ap,
                               r=R0[0:1, 0:512]:
                               tt.matmul(o, w, r, start=True, stop=True)),
                           reads=(R0, c.ones_row_f), writes=(pbb,), label=label + "_sb")
                        sl = slice(ch * 512, (ch + 1) * 512)
                        ev(p, (lambda v, o=buf[hh * 64:hh * 64 + 64, sl],
                               a=buf[hh * 64:hh * 64 + 64, sl],
                               bq=pbb[hh * 64:hh * 64 + 64, 0:512]:
                               v.tensor_tensor(o, a, bq, ALU.mult)),
                           reads=(buf, pbb), writes=(buf,), label=label + "_sm")
                ev2(p, (lambda v, o=buf.ap, i=buf.ap, s=psc.ap:
                        v.tensor_scalar_mul(o, i, s)),
                    reads=(buf,), sreads=(psc,), writes=(buf,), label=label + "_ds")
                ev(p, (lambda v, o=QR2.ap, a=buf.ap, bq=tabC.ap:
                       v.tensor_tensor(o, a, bq, ALU.mult)),
                   reads=(buf, tabC), writes=(QR2,), label=label + "_rc1")
                for hh in range(2):
                    for ch in range(4):
                        psw = c.psums[5]
                        et(p, (lambda tt, o=psw[hh * 64:hh * 64 + 64, 0:512],
                               w=SW[hh * 64:hh * 64 + 64, 0:64],
                               r=buf[hh * 64:hh * 64 + 64, ch * 512:(ch + 1) * 512]:
                               tt.matmul(o, w, r, start=True, stop=True)),
                           reads=(SW, buf), writes=(psw,), label=label + "_sw")
                        sl = slice(ch * 512, (ch + 1) * 512)
                        ev(p, (lambda v, o=SQ[hh * 64:hh * 64 + 64, sl],
                               a=psw[hh * 64:hh * 64 + 64, 0:512],
                               bq=tabS[hh * 64:hh * 64 + 64, sl]:
                               v.tensor_tensor(o, a, bq, ALU.mult)),
                           reads=(psw, tabS), writes=(SQ,), label=label + "_rs1")
                ev(p, (lambda v, o=buf.ap, a=QR2.ap, bq=SQ.ap:
                       v.tensor_tensor(o, a, bq, ALU.add)),
                   reads=(QR2, SQ), writes=(buf,), label=label + "_rfin")
            for hh in range(2):
                emit_flash_head(p, c, f, f.qT, f.kT, hh * 64, hh, 64,
                                f.att, hh * 64, a_spec, c_spec,
                                label=f"{label}f{b}{g}{hh}")
            for co in range(CT):
                for tch_i in range(4):
                    pb = c.psums[4 + (tch_i % 2)]
                    et(p, (lambda tt, o=pb[0:128, 0:512],
                           w=Wo.buf[:, co * 128:(co + 1) * 128],
                           r=f.att[:, tch_i * 512:(tch_i + 1) * 512]:
                           tt.matmul(o, w, r, start=True, stop=True)),
                       reads=(Wo.buf, f.att), writes=(pb,), label=label + "_ro")
                    dst = F[:, fcol(co, b, tch_i * 512): fcol(co, b, tch_i * 512) + 512]
                    ev(p, (lambda v, o=dst, x=dst, a=pb[0:128, 0:512]:
                           v.tensor_tensor(o, x, a, ALU.add)),
                       reads=(F, pb), writes=(F,), label=label + "_radd")


def emit_l0f_ffn(p, c, F, wd, l, label="lff"):
    LG = load_const_sb(p, label + "lg", wd[f"lng{l}"], [128, CT], F32)
    LB = load_const_sb(p, label + "lb", wd[f"lnb{l}"], [128, CT], F32)
    B2C = load_const_sb(p, label + "b2", wd[f"b2p{l}"], [128, CT], F32)
    B1A = load_const_sb(p, label + "b1a", wd[f"b1ap{l}"], [128, 22], F32)
    B1G = load_const_sb(p, label + "b1g", wd[f"b1gp{l}"], [128, 22], F32)
    SH = 1024
    LNH = p.sb_shared("lff_lnh", [128, CT * SH], BF16)
    AH = p.sb_shared("lff_ah", [128, 4 * 512], BF16)
    GH = p.sb_shared("lff_gh", [128, 4 * 512], BF16)
    w1a_b = p.sb_shared("lff_w1a", [128, CT * 512], BF16)
    w1g_b = p.sb_shared("lff_w1g", [128, CT * 512], BF16)
    w2_b = p.sb_shared("lff_w2", [128, 4 * D], BF16)
    HID = 2730
    BSH = p.sb(label + "bsh", [128, CT], BF16)
    for b, hf in [(b_, h_) for b_ in range(NB) for h_ in range(S // SH)]:
        if hf == 0 and S // SH > 1:
            for ct in range(CT):
                eact(p, BSH[:, ct:ct + 1],
                     F[:, fcol(ct, b, SH - 1): fcol(ct, b, SH - 1) + 1],
                     AF.Copy, reads=(F,), writes=(BSH,), label=label + "_bst")
        for ch in range(SH // 512):
            t0 = hf * SH + ch * 512
            lt0 = ch * 512
            for ct in range(CT):
                off = 1 if ct >= 4 else 0
                dstap = LNH[:, ct * SH + lt0: ct * SH + lt0 + 512]
                if off and hf > 0 and ch == 0:
                    eact(p, LNH[:, ct * SH + lt0: ct * SH + lt0 + 1],
                         BSH[:, ct:ct + 1], AF.Copy, reads=(BSH,),
                         writes=(LNH,), label=label + "_cpb")
                    eact(p, LNH[:, ct * SH + lt0 + 1: ct * SH + lt0 + 512],
                         F[:, fcol(ct, b, t0): fcol(ct, b, t0) + 511],
                         AF.Copy, reads=(F,), writes=(LNH,), label=label + "_cpr")
                else:
                    srcap = F[:, fcol(ct, b, t0) - off: fcol(ct, b, t0) - off + 512]
                    eact(p, dstap, srcap, AF.Copy, reads=(F,), writes=(LNH,),
                         label=label + "_cp")
            for sq_, ridx in ((False, 1), (True, 2)):
                pr = c.prow[0 if not sq_ else 1]
                for ct in range(CT):
                    ap = LNH[:, ct * SH + lt0: ct * SH + lt0 + 512]
                    if sq_:
                        sqsl = c.sqc[:, (ct % 2) * 512:(ct % 2) * 512 + 512]
                        eact(p, sqsl, ap, AF.Square, reads=(LNH,), writes=(c.sqc,),
                             label=label + "_sq")
                        ap2, rd = sqsl, (c.sqc, c.recip_bf)
                    else:
                        ap2, rd = ap, (LNH, c.recip_bf)
                    et(p, (lambda tt, o=pr[0:1, 0:512], w=c.recip_bf.ap, r=ap2,
                           st=(ct == 0), sp=(ct == CT - 1):
                           tt.matmul(o, w, r, start=st, stop=sp)),
                       reads=rd, writes=(pr,), label=label + "_st")
                eact(p, c.R[ridx][0:1, 0:512], pr[0:1, 0:512], AF.Copy,
                     reads=(pr,), writes=(c.R[ridx],), label=label + "_sev")
            R0, R1, R2 = c.R[0], c.R[1], c.R[2]
            ev(p, lambda v: v.tensor_tensor(R0[0:1, :], R1[0:1, :], R1[0:1, :], ALU.mult),
               reads=(R1,), writes=(R0,), label=label + "_m2")
            ev(p, lambda v: v.scalar_tensor_tensor(R0[0:1, :], R0[0:1, :], -1.0,
                                                   R2[0:1, :], ALU.mult, ALU.add),
               reads=(R0, R2), writes=(R0,), label=label + "_var")
            eact(p, R0[0:1, :], R0[0:1, :], AF.Ln, bias=1e-5, reads=(R0,), writes=(R0,),
                 label=label + "_l")
            eact(p, R0[0:1, :], R0[0:1, :], AF.Exp, scale=-0.5, reads=(R0,), writes=(R0,),
                 label=label + "_e")
            ev(p, lambda v: v.tensor_tensor(R1[0:1, :], R1[0:1, :], R0[0:1, :], ALU.mult),
               reads=(R1, R0), writes=(R1,), label=label + "_b")
            pA, pB = c.psums[0], c.psums[1]
            emit_bcast_chunk(p, c, R0, pA, label=label + "_bA")
            emit_bcast_chunk(p, c, R1, pB, label=label + "_bB")
            for ct in range(CT):
                dstap = LNH[:, ct * SH + lt0: ct * SH + lt0 + 512]

                def app(v, o=dstap, a=pA[:, 0:512], bb=pB[:, 0:512],
                        g=LG[:, ct:ct + 1], b_=LB[:, ct:ct + 1]):
                    v.tensor_tensor(o, o, a, ALU.mult)
                    v.tensor_tensor(o, o, bb, ALU.subtract)
                    return v.tensor_scalar(o, o, g, b_, ALU.mult, ALU.add)
                p.emit("vector", app, reads=(pA, pB, LNH), sreads=(LG, LB),
                       writes=(LNH,), label=label + "_app")
        for hidg in range(6):
            w0 = hidg * 512
            wdt = min(512, HID - w0)
            nct = (wdt + 127) // 128
            W1a = load_weight_cols(p, w1a_b, wd[f"w1_{l}"], D, w0, wdt,
                                   name=label + f"a{hidg}")
            W1g = load_weight_cols(p, w1g_b, wd[f"w1_{l}"], D, HID + w0, wdt,
                                   name=label + f"g{hidg}")
            for kb in range(nct):
                kh = min(128, wdt - kb * 128)
                p.dma("gpsimd", w2_b[0:kh, kb * D:kb * D + D],
                      wd[f"w2_{l}"].ap()[w0 + kb * 128: w0 + kb * 128 + kh, :],
                      reads=(w2_b,), writes=(w2_b,), label=label + f"ldw2{hidg}")
            for tch_i in range(SH // 512):
                t0 = hf * SH + tch_i * 512
                for sub, W_, dstb, b1c, func in (("a", W1a, AH, B1A, AF.Identity),
                                                 ("g", W1g, GH, B1G, AF.Gelu)):
                    for coi in range(nct):
                        cw_ = min(128, wdt - coi * 128)
                        pb = c.psums[2 + (coi % 2)]
                        for kb in range(CT):
                            et(p, (lambda tt, o=pb[0:cw_, 0:512],
                                   w=W_.tile(kb, coi * 128, cw_),
                                   r=LNH[:, kb * SH + tch_i * 512: kb * SH + tch_i * 512 + 512],
                                   st=(kb == 0), sp=(kb == CT - 1):
                                   tt.matmul(o, w, r, start=st, stop=sp)),
                               reads=(W_.buf, LNH), writes=(pb,), label=label + "_w1" + sub)
                        eact(p, dstb[0:cw_, coi * 512:coi * 512 + 512], pb[0:cw_, 0:512],
                             func, bias=b1c[0:cw_, hidg * 4 + coi:hidg * 4 + coi + 1],
                             reads=(pb, b1c), writes=(dstb,), label=label + "_ev" + sub)
                ev(p, (lambda v, o=AH[0:128, 0:nct * 512], a=AH[0:128, 0:nct * 512],
                       bq=GH[0:128, 0:nct * 512]:
                       v.tensor_tensor(o, a, bq, ALU.mult)),
                   reads=(AH, GH), writes=(AH,), label=label + "_hh")
                for co in range(CT):
                    pb = c.psums[4 + (co % 2)]
                    for coi in range(nct):
                        kw_ = min(128, wdt - coi * 128)
                        et(p, (lambda tt, o=pb[0:128, 0:512],
                               w=w2_b[0:kw_, coi * D + co * 128: coi * D + (co + 1) * 128],
                               r=AH[0:kw_, coi * 512:coi * 512 + 512],
                               st=(coi == 0), sp=(coi == nct - 1):
                               tt.matmul(o, w, r, start=st, stop=sp)),
                           reads=(w2_b, AH), writes=(pb,), label=label + "_w2")
                    dst = F[:, fcol(co, b, t0): fcol(co, b, t0) + 512]
                    ev(p, (lambda v, o=dst, x=dst, a=pb[0:128, 0:512]:
                           v.tensor_tensor(o, x, a, ALU.add)),
                       reads=(F, pb), writes=(F,), label=label + "_fadd")
        if hf == S // SH - 1:
            for ct in range(CT):
                dst = F[:, fcol(ct, b, 0): fcol(ct, b, 0) + S]
                ev2(p, (lambda v, o=dst, i=dst, s=B2C[:, ct:ct + 1]:
                        v.tensor_scalar_add(o, i, s)),
                    reads=(F,), sreads=(B2C,), writes=(F,), label=label + "_b2")


def l0f_masks_host():
    out = {}
    MA = np.zeros((128, 1024 + 128), np.float32)
    for qf in range(128):
        for v in range(1024 + 128):
            d = qf - (v - 512)
            if not (0 <= d <= 512):
                MA[qf, v] = -1e30
    out["l0A"] = MA
    MC = np.zeros((128, 1024 + 512), np.float32)
    for kp in range(128):
        for v in range(1024 + 512):
            d = (v - 512) - kp
            if not (0 <= d <= 512):
                MC[kp, v] = -1e30
    out["l0C"] = MC
    return out


def l0f_tabs_host():
    DH, W = 64, 512
    half = np.arange(0, DH, 2, dtype=np.float64)
    inv = 1.0 / 10000.0 ** (half / DH)
    t = np.arange(S, dtype=np.float64)
    fr = t[:, None] * inv[None, :]
    fr = np.concatenate([fr, fr], 1)
    base = (half + 0.4 * DH) / (1.4 * DH)
    power = (t - S // 2) / (W // 2)
    sc = base[None, :] ** power[:, None]
    sc = np.concatenate([sc, sc], 1)
    cos, sin = np.cos(fr), np.sin(fr)
    qc = (cos * sc).T
    qs = (sin * sc).T
    kc = (cos / sc).T
    ks = (sin / sc).T
    qs2 = qs.copy(); qs2[0:32] *= -1.0
    ks2 = ks.copy(); ks2[0:32] *= -1.0
    dup = lambda a: np.concatenate([a, a], 0).astype(np.float32)
    return {"tab_qc": dup(qc), "tab_qs": dup(qs2), "tab_kc": dup(kc),
            "tab_ks": dup(ks2)}


def l0f_swap_host():
    sw = np.zeros((64, 64), np.float32)
    for d_ in range(64):
        sw[(d_ + 32) % 64, d_] = 1.0
    return np.concatenate([sw, sw], 0).astype(np.float32)


def emit_l2_ffn(p, c, wd, nc, label="l2f"):
    cwT = load_weight(p, label + "cwt", wd["choice_wT"], D, 8)
    identf = p.sb(label + "idf", [128, 128], F32)
    p.emit("gpsimd", lambda g: g.iota(identf.ap, [[-1, 128]], base=0,
                                      channel_multiplier=1,
                                      allow_small_or_imprecise_dtypes=True),
           writes=(identf,), label=label + "_idio")
    ev(p, lambda v: v.tensor_scalar(identf.ap, identf.ap, 0.0, None, ALU.is_equal),
       reads=(identf,), writes=(identf,), label=label + "_ideq")
    PB = p.sb(label + "pbm", [128, 16 * 8], F32)
    P2 = p.sb(label + "p2", [64, S], F32)
    JK = p.sb(label + "jk", [64, S], BF16)
    LO = p.sb(label + "lo", [128, 6], F32)
    BIDX = p.sb(label + "bidx", [128, 16 * 8], F32)
    GIX = p.sb(label + "gix", [128, 16 * 8], F32)
    IDS = p.sb(label + "ids", [128, 4], F32)
    UID = p.sb(label + "uid", [128, 2], U32)
    XTOK = p.sb(label + "xtok", [128, 1024], BF16)
    TMPG = XTOK
    IOTC = p.sb(label + "iotc", [128, 1], F32)
    ACC = p.sb(label + "acc", [128, D], F32)
    YBB = p.sb(label + "ybb", [128, D], BF16)
    XE = p.sb(label + "xe", [128, CT * 512], BF16)
    ZB = XE
    HE = p.sb(label + "he", [128, 4 * 512], BF16)
    w1b = p.sb(label + "w1b", [128, CT * 512], BF16)
    w2b = p.sb(label + "w2b", [128, 4 * 1024], BF16)

    p.emit("gpsimd", lambda g: g.iota(IOTC.ap, [[0, 1]], base=0, channel_multiplier=1,
                                      allow_small_or_imprecise_dtypes=True),
           writes=(IOTC,), label=label + "_io")
    xin = [Buf(nc.dram_tensor(f"l2f_xin{b_}", [8 * 512 + 128, 1024], BF16), f"xin{b_}")
           for b_ in range(NB)]
    ydr = [Buf(nc.dram_tensor(f"l2f_y{b_}", [8 * 513, D], F32), f"y{b_}")
           for b_ in range(NB)]

    for b in range(NB):
        for tt_i in range(16):
            pb_ = c.psums[tt_i % 2]
            for kb in range(CT):
                lhs = c.XN[:, kb * T + b * S + tt_i * 128:
                           kb * T + b * S + (tt_i + 1) * 128]
                et(p, (lambda tt, o=pb_[0:128, 0:8], w=lhs, r=cwT.tile(kb, 0, 8),
                       st=(kb == 0), sp=(kb == CT - 1):
                       tt.matmul(o, w, r, start=st, stop=sp)),
                   reads=(c.XN, cwT.buf), writes=(pb_,), label=label + "_lg")
            sl = PB[:, tt_i * 8:(tt_i + 1) * 8]
            eact(p, sl, pb_[0:128, 0:8], AF.Copy, reads=(pb_,), writes=(PB,),
                 label=label + "_le")
            ev(p, (lambda v, o=LO[:, 3:4], i=sl:
                   v.reduce_max(o, i, AX.X, negate=True)),
               reads=(PB,), writes=(LO,), label=label + "_lm")
            p.emit("scalar", (lambda s_, o=sl, i=sl, bb=LO[:, 3:4]:
                              s_.activation(o, i, AF.Exp, bias=bb,
                                            accum_out=LO[:, 2:3])),
                   reads=(PB,), sreads=(LO,), writes=(PB, LO), label=label + "_lx")
            ev2(p, (lambda v, o=LO[:, 2:3], i=LO[:, 2:3]: v.reciprocal(o, i)),
                reads=(LO,), writes=(LO,), label=label + "_lr")
            ev2(p, (lambda v, o=sl, i=sl, s=LO[:, 2:3]: v.tensor_scalar_mul(o, i, s)),
                reads=(PB,), sreads=(LO,), writes=(PB,), label=label + "_lp")
            pt_ = c.psums[2]
            et(p, (lambda tt, o=pt_[0:8, 0:128], i=sl, idf=identf.ap:
                   tt.transpose(o, i, idf)),
               reads=(PB, identf), writes=(pt_,), label=label + "_lt")
            eact(p, P2[0:8, tt_i * 128:(tt_i + 1) * 128], pt_[0:8, 0:128], AF.Copy,
                 reads=(pt_,), writes=(P2,), label=label + "_lte")
        ev(p, lambda v: v.memset(LO[0:64, 0:1], 0.0), writes=(LO,), label=label + "_b0")
        ev(p, lambda v: v.memset(LO[0:64, 1:2], 1.0), writes=(LO,), label=label + "_b1")
        for it in range(40):
            ev(p, (lambda v: v.tensor_tensor(LO[0:64, 2:3], LO[0:64, 0:1], LO[0:64, 1:2],
                                             ALU.add)),
               reads=(LO,), writes=(LO,), label=label + "_bt")
            ev(p, (lambda v: v.tensor_scalar_mul(LO[0:64, 2:3], LO[0:64, 2:3], 0.5)),
               reads=(LO,), writes=(LO,), label=label + "_bh")
            ev2(p, (lambda v: v.tensor_scalar(JK.ap, P2.ap, LO[0:64, 2:3], 0.0,
                                              ALU.is_ge, ALU.add,
                                              accum_out=LO[0:64, 3:4])),
                reads=(P2,), sreads=(LO,), writes=(JK, LO), label=label + "_bc")
            ev(p, (lambda v: v.tensor_scalar(LO[0:64, 3:4], LO[0:64, 3:4], 512.0, None,
                                             ALU.is_ge)),
               reads=(LO,), writes=(LO,), label=label + "_bg")
            ev(p, (lambda v: v.tensor_tensor(LO[0:64, 4:5], LO[0:64, 2:3], LO[0:64, 0:1],
                                             ALU.subtract)),
               reads=(LO,), writes=(LO,), label=label + "_b4")
            ev(p, (lambda v: v.tensor_tensor(LO[0:64, 5:6], LO[0:64, 1:2], LO[0:64, 2:3],
                                             ALU.subtract)),
               reads=(LO,), writes=(LO,), label=label + "_b5")
            ev2(p, (lambda v: v.scalar_tensor_tensor(LO[0:64, 0:1], LO[0:64, 4:5],
                                                     LO[0:64, 3:4], LO[0:64, 0:1],
                                                     ALU.mult, ALU.add)),
                reads=(LO,), sreads=(LO,), writes=(LO,), label=label + "_bl")
            ev2(p, (lambda v: v.scalar_tensor_tensor(LO[0:64, 1:2], LO[0:64, 5:6],
                                                     LO[0:64, 3:4], LO[0:64, 2:3],
                                                     ALU.mult, ALU.add)),
                reads=(LO,), sreads=(LO,), writes=(LO,), label=label + "_bu")
        ev2(p, (lambda v: v.tensor_scalar(JK.ap, P2.ap, LO[0:64, 0:1], None, ALU.is_ge)),
            reads=(P2,), sreads=(LO,), writes=(JK,), label=label + "_mk")
        ev(p, (lambda v: v.memset(ZB[0:64, 0:S], 0.0)), writes=(ZB,),
           label=label + "_zb")
        for e_ in range(8):
            p.dma("gpsimd", ydr[b].t.ap()[e_ * 513 + 512: e_ * 513 + 513, :],
                  ZB[0:1, 0:D], reads=(ZB, ydr[b]), writes=(ydr[b],),
                  label=label + "_yz2")
        ev(p, (lambda v: v.tensor_tensor_scan(P2.ap, ZB[0:64, 0:S], JK.ap, 0.0,
                                              ALU.max, ALU.add)),
           reads=(ZB, JK), writes=(P2,), label=label + "_psc")
        ev(p, (lambda v: v.tensor_scalar_add(P2.ap, P2.ap, -1.0)),
           reads=(P2,), writes=(P2,), label=label + "_pm1")
        for tt_i in range(16):
            pt_ = c.psums[3]
            et(p, (lambda tt, o=pt_[0:128, 0:8],
                   i=P2[0:8, tt_i * 128:(tt_i + 1) * 128], idf=identf[0:8, 0:8]:
                   tt.transpose(o, i, idf)),
               reads=(P2, identf), writes=(pt_,), label=label + "_tp")
            eact(p, BIDX[:, tt_i * 8:(tt_i + 1) * 8], pt_[0:128, 0:8], AF.Copy,
                 reads=(pt_,), writes=(BIDX,), label=label + "_tpe")
        if not hasattr(p, "_l2stg"):
            p._l2stg = p.sb(label + "_stg", [128, 128], F32)
        STG = p._l2stg
        for tt_i in range(16):
            eact(p, STG[0:8, 0:128], JK[0:8, tt_i * 128:(tt_i + 1) * 128], AF.Copy,
                 reads=(JK,), writes=(STG,), label=label + "_tmc")
            pmk = c.psums[4]
            et(p, (lambda tt, o=pmk[0:128, 0:8], i=STG[0:8, 0:128], idf=identf[0:8, 0:8]:
                   tt.transpose(o, i, idf)),
               reads=(STG, identf), writes=(pmk,), label=label + "_tm")
            eact(p, GIX[:, tt_i * 8:(tt_i + 1) * 8], pmk[0:128, 0:8], AF.Copy,
                 reads=(pmk,), writes=(GIX,), label=label + "_tme")
        for tt_i in range(16):
            for kb in range(CT):
                lhs = c.XN[:, kb * T + b * S + tt_i * 128:
                           kb * T + b * S + (tt_i + 1) * 128]
                STG = p._l2stg
                eact(p, STG.ap, lhs, AF.Copy, reads=(c.XN,), writes=(STG,),
                     label=label + "_xtc")
                pmx = c.psums[5]
                et(p, (lambda tt, o=pmx[0:128, 0:128], i=STG.ap, idf=identf.ap:
                       tt.transpose(o, i, idf)),
                   reads=(STG, identf), writes=(pmx,), label=label + "_xt")
                eact(p, XTOK[:, kb * 128:(kb + 1) * 128], pmx[0:128, 0:128], AF.Copy,
                     reads=(pmx,), writes=(XTOK,), label=label + "_xte")
            for e in range(8):
                pc_ = slice(tt_i * 8 + e, tt_i * 8 + e + 1)
                ev(p, (lambda v, o=IDS[:, 0:1], a=BIDX[:, pc_], e_=float(e * 512):
                       v.tensor_scalar_add(o, a, e_)),
                   reads=(BIDX,), writes=(IDS,), label=label + "_u1")
                ev(p, (lambda v, o=IDS[:, 0:1], a=IDS[:, 0:1], bb=GIX[:, pc_]:
                       v.tensor_tensor(o, a, bb, ALU.mult)),
                   reads=(IDS, GIX), writes=(IDS,), label=label + "_u2")
                ev(p, (lambda v, o=IDS[:, 1:2], a=GIX[:, pc_]:
                       v.tensor_scalar(o, a, -4096.0, 4096.0, ALU.mult, ALU.add)),
                   reads=(GIX,), writes=(IDS,), label=label + "_u3")
                ev(p, (lambda v, o=IDS[:, 2:3], a=GIX[:, pc_]:
                       v.tensor_scalar(o, a, -1.0, 1.0, ALU.mult, ALU.add)),
                   reads=(GIX,), writes=(IDS,), label=label + "_u3b")
                ev2(p, (lambda v, o=IDS[:, 1:2], i=IDS[:, 2:3], s=IOTC.ap,
                        g=IDS[:, 1:2]: v.scalar_tensor_tensor(o, i, s, g,
                                                              ALU.mult, ALU.add)),
                    reads=(IDS,), sreads=(IOTC,), writes=(IDS,), label=label + "_u3c")
                ev(p, (lambda v, o=IDS[:, 0:1], a=IDS[:, 0:1], bb=IDS[:, 1:2]:
                       v.tensor_tensor(o, a, bb, ALU.add)),
                   reads=(IDS,), writes=(IDS,), label=label + "_u4")
                ev(p, (lambda v, o=UID[:, 0:1], i=IDS[:, 0:1]: v.tensor_copy(o, i)),
                   reads=(IDS,), writes=(UID,), label=label + "_u5")
                p.emit("gpsimd",
                       (lambda g, o=xin[b].t.ap(), io=UID[:, 0:1], i=XTOK.ap:
                        g.indirect_dma_start(o, IOA(ap=io, axis=0), i, None)),
                       reads=(XTOK, UID, xin[b]), writes=(xin[b],), dma=True,
                       label=label + "_sc")
        for e in range(8):
            for kb in range(CT):
                p.emit("sync",
                       (lambda s_, o=XE[:, kb * 512:(kb + 1) * 512],
                        i=xin[b].t.ap()[e * 512:(e + 1) * 512, kb * 128:(kb + 1) * 128]:
                        s_.dma_start_transpose(o, i)),
                       reads=(xin[b], XE), writes=(XE,), dma=True,
                       label=label + "_xet")
            for q4 in range(4):
                W1h = load_weight_cols(p, w1b, wd["w1"][e], D, q4 * 512, 512,
                                       name=label + f"w1{b}{e}{q4}", stride=512)
                ap_in = wd["w2"][e].ap()[q4 * 512:(q4 + 1) * 512, :] \
                    .rearrange("(kb q) n -> q kb n", q=128)
                ap_out = w2b.ap.rearrange("q (kb n) -> q kb n", kb=4)
                p.dma("gpsimd", ap_out, ap_in, reads=(w2b,), writes=(w2b,),
                      label=label + f"ldw2{b}{e}{q4}")
                W2h = Weights(w2b, 512, 1024)
                for co in range(4):
                    pb_ = c.psums[co % 2]
                    for kb in range(CT):
                        et(p, (lambda tt, o=pb_[0:128, 0:512],
                               w=W1h.tile(kb, co * 128, 128),
                               r=XE[:, kb * 512:(kb + 1) * 512],
                               st=(kb == 0), sp=(kb == CT - 1):
                               tt.matmul(o, w, r, start=st, stop=sp)),
                           reads=(w1b, XE), writes=(pb_,), label=label + "_h")
                    eact(p, HE[:, co * 512:(co + 1) * 512], pb_[0:128, 0:512],
                         AF.Silu, reads=(pb_,), writes=(HE,), label=label + "_hs")
                for ss in range(4):
                    for cc in range(2):
                        pb_ = c.psums[2 + (2 * ss + cc) % 4]
                        for kb in range(4):
                            et(p, (lambda tt, o=pb_[0:128, 0:512],
                                   w=HE[:, kb * 512 + ss * 128: kb * 512 + ss * 128 + 128],
                                   r=W2h.tile(kb, cc * 512, 512),
                                   st=(kb == 0), sp=(kb == 3):
                                   tt.matmul(o, w, r, start=st, stop=sp)),
                               reads=(HE, w2b), writes=(pb_,), label=label + "_y")
                        eact(p, YBB[:, cc * 512:(cc + 1) * 512], pb_[0:128, 0:512],
                             AF.Copy, reads=(pb_,), writes=(YBB,), label=label + "_ye")
                    kw = dict(accum_op=ALU.add) if q4 > 0 else {}
                    p.dma("gpsimd",
                          ydr[b].t.ap()[e * 513 + ss * 128: e * 513 + (ss + 1) * 128, :],
                          YBB.ap, reads=(YBB, ydr[b]), writes=(ydr[b],),
                          label=label + "_yst", **kw)
        for tt_i in range(16):
            ev(p, lambda v: v.memset(ACC.ap, 0.0), writes=(ACC,), label=label + "_az")
            for e in range(8):
                pc_ = slice(tt_i * 8 + e, tt_i * 8 + e + 1)
                ev(p, (lambda v, o=IDS[:, 0:1], a=BIDX[:, pc_], bb=GIX[:, pc_]:
                       v.tensor_tensor(o, a, bb, ALU.mult)),
                   reads=(BIDX, GIX), writes=(IDS,), label=label + "_g1")
                ev(p, (lambda v, o=IDS[:, 1:2], a=GIX[:, pc_]:
                       v.tensor_scalar(o, a, -512.0, 512.0, ALU.mult, ALU.add)),
                   reads=(GIX,), writes=(IDS,), label=label + "_g2")
                ev(p, (lambda v, o=IDS[:, 0:1], a=IDS[:, 0:1], bb=IDS[:, 1:2]:
                       v.tensor_tensor(o, a, bb, ALU.add)),
                   reads=(IDS,), writes=(IDS,), label=label + "_g3")
                ev(p, (lambda v, o=UID[:, 1:2], i=IDS[:, 0:1]: v.tensor_copy(o, i)),
                   reads=(IDS,), writes=(UID,), label=label + "_g4")
                p.emit("gpsimd",
                       (lambda g, o=TMPG.ap, io=UID[:, 1:2],
                        i=ydr[b].t.ap(), eo=e * 513 * D:
                        g.indirect_dma_start(o, None, i, IOA(ap=io, axis=0),
                                             element_offset=eo)),
                       reads=(ydr[b], UID, TMPG), writes=(TMPG,), dma=True,
                       label=label + "_ga")
                ev2(p, (lambda v, o=ACC.ap, i=TMPG.ap,
                        s=PB[:, tt_i * 8 + e: tt_i * 8 + e + 1], g=ACC.ap:
                        v.scalar_tensor_tensor(o, i, s, g, ALU.mult, ALU.add)),
                    reads=(TMPG, ACC), sreads=(PB,), writes=(ACC,),
                    label=label + "_gacc")
            for kb in range(CT):
                pt_ = c.psums[5]
                et(p, (lambda tt, o=pt_[0:128, 0:128],
                       i=ACC[:, kb * 128:(kb + 1) * 128], idf=identf.ap:
                       tt.transpose(o, i, idf)),
                   reads=(ACC, identf), writes=(pt_,), label=label + "_at")
                sl = slice(kb * T + b * S + tt_i * 128,
                           kb * T + b * S + (tt_i + 1) * 128)
                ev(p, (lambda v, o=c.XT[:, sl], x=c.XT[:, sl], a=pt_[0:128, 0:128]:
                       v.tensor_tensor(o, x, a, ALU.add)),
                   reads=(c.XT, pt_), writes=(c.XT,), label=label + "_aa")


def _pcol(v, ncols=None):
    v = np.asarray(v, dtype=np.float32).reshape(-1)
    n = v.shape[0]
    nb = (n + 127) // 128 if ncols is None else ncols
    out = np.zeros((128, nb), dtype=np.float32)
    for cb in range(min(nb, (n + 127) // 128)):
        seg = v[cb * 128:(cb + 1) * 128]
        out[: seg.shape[0], cb] = seg
    return out


def _ct(a):
    return np.ascontiguousarray(a, dtype=np.float32)


def _dram_ins(nc, spec):
    return {k: nc.dram_tensor(k, list(v.shape), F32, kind="ExternalInput")
            for k, v in spec.items()}


def _act_io(nc, names_in, names_out):
    ins = {n: nc.dram_tensor(n, [NB, D, S], F32, kind="ExternalInput")
           for n in names_in}
    outs = {n: nc.dram_tensor(n, [NB, D, S], F32, kind="ExternalOutput")
            for n in names_out}
    return ins, outs


def _run(nc, const_ins, act_ins_percore, core_ids=None):
    core_ids = core_ids if core_ids is not None else list(range(8))
    in_maps = []
    for ci in range(len(core_ids)):
        m = dict(const_ins)
        for k, arrs in act_ins_percore.items():
            m[k] = arrs[ci]
        in_maps.append(m)
    return run_bass_kernel_spmd(nc, in_maps, core_ids=core_ids).results


def emit_add_from_dram(p, c, dram, label="afd"):
    tmp = p.sb(label + "_t", [128, 2048], BF16)
    for b in range(NB):
        for ct in range(CT):
            p.dma("gpsimd", tmp.ap, dram.ap()[b, ct * 128:(ct + 1) * 128, :],
                  reads=(tmp,), writes=(tmp,), label=label + "_ld")
            sl = slice(ct * T + b * S, ct * T + (b + 1) * S)
            ev(p, (lambda v, o=c.XT[:, sl], x=c.XT[:, sl], a=tmp.ap:
                   v.tensor_tensor(o, x, a, ALU.add)),
               reads=(c.XT, tmp), writes=(c.XT,), label=label + "_add")


def _host_prep(inputs):
    NC = 8
    x = np.asarray(inputs["L0_attn_x"])
    xT = np.ascontiguousarray(x.transpose(0, 2, 1))
    xs = [np.ascontiguousarray(xT[2 * ci:2 * ci + 2]) for ci in range(NC)]

    W = {k: np.asarray(v) for k, v in inputs.items()}
    l0 = {
        "Wx": _ct(W["L0_attn_Wx"]), "Wy": _ct(W["L0_attn_Wy"]),
        "Wi": _ct(W["L0_attn_Wi"]), "Wa": _ct(W["L0_attn_Wa"]),
        "Wo": _ct(W["L0_attn_Wo"]),
        "n1p": _pcol(W["L0_attn_norm1_w"]),
        "cwp": np.concatenate([_pcol(W["L0_attn_conv_w"][k]) for k in range(4)], 1),
        "cbp": _pcol(W["L0_attn_conv_b"]),
        "lamp": _pcol(W["L0_attn_lam"]),
        "Wg": _ct(W["L0_attn_Wg"]), "Wu": _ct(W["L0_attn_Wu"]),
        "Wd": _ct(W["L0_attn_Wd"]), "n2p": _pcol(W["L0_attn_norm2_w"]),
    }
    l0f = {"swap64": l0f_swap_host()}
    l0f.update(l0f_tabs_host())
    l0f.update(l0f_masks_host())
    for l in range(2):
        l0f[f"wqkv{l}"] = _ct(W["L0_ffn_wqkv"][l])
        l0f[f"wo{l}"] = _ct(W["L0_ffn_wo"][l])
        l0f[f"qs{l}"] = _ct(np.concatenate([W["L0_ffn_q_scale"][l]] * 2))[:, None]
        l0f[f"ks{l}"] = _ct(np.concatenate([W["L0_ffn_k_scale"][l]] * 2))[:, None]
        l0f[f"lng{l}"] = _pcol(W["L0_ffn_ln_g"][l])
        l0f[f"lnb{l}"] = _pcol(W["L0_ffn_ln_b"][l])
        l0f[f"w1_{l}"] = _ct(W["L0_ffn_w1"][l])
        l0f[f"b1a{l}"] = _pcol(W["L0_ffn_b1"][l][:2730], 22)
        l0f[f"b1g{l}"] = _pcol(W["L0_ffn_b1"][l][2730:], 22)
        l0f[f"w2_{l}"] = _ct(W["L0_ffn_w2"][l])
        l0f[f"b2{l}"] = _pcol(W["L0_ffn_b2"][l])
    l1 = {
        "wq": _ct(W["L1_attn_Wq"].T), "wk": _ct(W["L1_attn_Wk"].T),
        "wv": _ct(W["L1_attn_Wv"].T), "wo": _ct(W["L1_attn_Wo"].T),
        "win": _ct(W["L1_ffn_Win"]), "bin": _ct(W["L1_ffn_bin_"])[:, None],
        "w1": _ct(W["L1_ffn_W1"]), "b1": _ct(W["L1_ffn_b1"]),
        "w2": _ct(W["L1_ffn_W2"]), "b2": _ct(W["L1_ffn_b2"])[:, None],
        "wout": _ct(W["L1_ffn_Wout"]), "bout": _pcol(W["L1_ffn_bout"]),
    }
    l1.update(l1_masks_host())
    l2 = {
        "wq2": _ct(W["L2_attn_W_q"]), "wk2": _ct(W["L2_attn_W_k"]),
        "wv2": _ct(W["L2_attn_W_v"]), "wo2": _ct(W["L2_attn_W_o"]),
        "bq2": _pcol(W["L2_attn_b_q"]), "bk2": _pcol(W["L2_attn_b_k"]),
        "bv2": _pcol(W["L2_attn_b_v"]), "bo2": _pcol(W["L2_attn_b_o"]),
        "cwt2": _ct(W["L2_ffn_choice_w"].T),
    }
    for e in range(8):
        l2[f"w1_{e}"] = _ct(W["L2_ffn_w1"][e])
        l2[f"w2_{e}"] = _ct(W["L2_ffn_w2"][e])
    l3 = {
        "kvd": _ct(W["L3_attn_W_kv_down"]), "kup": _ct(W["L3_attn_W_k_up"]),
        "vup": _ct(W["L3_attn_W_v_up"]), "wq3": _ct(W["L3_attn_W_q"]),
        "wqr": _ct(W["L3_attn_W_q_rope"]), "wkr": _ct(W["L3_attn_W_k_rope"]),
        "wo3": _ct(W["L3_attn_W_o"]), "rope_t": l3_rope_host(),
        "wr": _ct(W["L3_ffn_Wr"]), "rb": _ct(W["L3_ffn_router_bias"]).reshape(1, 8),
        "cw3": np.concatenate([_pcol(W["L3_ffn_conv_w"][k, 0]) for k in range(4)], 1),
        "cb3": _pcol(W["L3_ffn_conv_b"]),
    }
    l3.update(l3_masks_host())
    for e in range(4):
        l3[f"wg{e}"] = _ct(W["L3_ffn_Wg"][e])
        l3[f"wu{e}"] = _ct(W["L3_ffn_Wu"][e])
        l3[f"wd{e}"] = _ct(W["L3_ffn_Wd"][e])
    return xs, l0, l0f, l1, l2, l3


def kernel_unfused(**inputs):
    xs, l0, l0f, l1, l2, l3 = _host_prep(inputs)

    nc = bass.Bass("TRN2", target_bir_lowering=False)
    ai, ao = _act_io(nc, ["x"], ["xn"])
    ao["s1"] = nc.dram_tensor("s1o", [128, T], F32, kind="ExternalOutput")
    with ExitStack() as stack:
        p = Prog(nc, stack)
        c = alloc_common(p, need_ident=False)
        S1 = p.sb("s1rep", [128, T], BF16)
        emit_load_xt(p, c.XT, ai["x"], queue="gpsimd")
        emit_ln(p, c, c.XT, c.XN, eps=1e-5, s1rep=S1, s1eps=1e-6)
        emit_store_xt(p, c.XN, ao["xn"], queue="gpsimd")
        p.dma("gpsimd", ao["s1"].ap(), S1.ap, reads=(S1,), label="sts1")
        p.finalize()
    r = _run(nc, {}, {"x": xs})
    xn_s = [np.asarray(rr["xn"]) for rr in r]
    s1_s = [np.asarray(rr["s1o"]) for rr in r]

    nc = bass.Bass("TRN2", target_bir_lowering=False)
    ai, ao = _act_io(nc, ["xn"], ["h1"])
    s1h = nc.dram_tensor("s1", [128, T], F32, kind="ExternalInput")
    wd = {"Wx": nc.dram_tensor("Wx", [D, D], F32, kind="ExternalInput"),
          "Wy": nc.dram_tensor("Wy", [D, D], F32, kind="ExternalInput"),
          "Wi": nc.dram_tensor("Wi", [D, D], F32, kind="ExternalInput"),
          "Wa": nc.dram_tensor("Wa", [D, D], F32, kind="ExternalInput"),
          "Wo": nc.dram_tensor("Wo", [D, D], F32, kind="ExternalInput"),
          "norm1_pcol": nc.dram_tensor("n1p", [128, CT], F32, kind="ExternalInput"),
          "conv_w_pcol": nc.dram_tensor("cwp", [128, 4 * CT], F32, kind="ExternalInput"),
          "conv_b_pcol": nc.dram_tensor("cbp", [128, CT], F32, kind="ExternalInput"),
          "lam_pcol": nc.dram_tensor("lamp", [128, CT], F32, kind="ExternalInput")}
    with ExitStack() as stack:
        p = Prog(nc, stack)
        p.init_consts()
        c = Ctx(); c.p = p
        c.XN = p.sb("XN", [128, CT * T], BF16)
        c.R = [p.sb(f"R{i}", [128, TCH], F32) for i in range(3)]
        c.psums = [p.ps(f"pp{i}", [128, 512], F32) for i in range(6)]
        c.prow = [p.ps(f"pr{i}", [1, 512], F32) for i in range(2)]
        S1 = p.sb("s1rep", [128, T], BF16)
        emit_load_xt(p, c.XN, ai["xn"], queue="gpsimd")
        p.dma("gpsimd", S1.ap, s1h.ap(), writes=(S1,), label="lds1")
        emit_l0_mix(p, c, wd, S1)
        emit_store_xt(p, c.XN, ao["h1"], queue="gpsimd")
        p.finalize()
    cst = {"Wx": l0["Wx"], "Wy": l0["Wy"], "Wi": l0["Wi"], "Wa": l0["Wa"],
           "Wo": l0["Wo"], "n1p": l0["n1p"], "cwp": l0["cwp"], "cbp": l0["cbp"],
           "lamp": l0["lamp"]}
    r = _run(nc, cst, {"xn": xn_s, "s1": s1_s})
    h1_s = [np.asarray(rr["h1"]) for rr in r]

    nc = bass.Bass("TRN2", target_bir_lowering=False)
    ai, ao = _act_io(nc, ["x", "h1"], ["xp0"])
    wd = {"Wg": nc.dram_tensor("Wg", [D, 4096], F32, kind="ExternalInput"),
          "Wu": nc.dram_tensor("Wu", [D, 4096], F32, kind="ExternalInput"),
          "Wd": nc.dram_tensor("Wd", [4096, D], F32, kind="ExternalInput"),
          "norm2_pcol": nc.dram_tensor("n2p", [128, CT], F32, kind="ExternalInput")}
    with ExitStack() as stack:
        p = Prog(nc, stack)
        c = alloc_common(p, nR=2, need_ident=False)
        emit_load_xt(p, c.XT, ai["x"], queue="gpsimd")
        emit_load_xt(p, c.XN, ai["h1"], queue="gpsimd")
        emit_l0_ffn_int(p, c, wd)
        emit_store_xt(p, c.XT, ao["xp0"], queue="gpsimd")
        p.finalize()
    r = _run(nc, {"Wg": l0["Wg"], "Wu": l0["Wu"], "Wd": l0["Wd"],
                  "n2p": l0["n2p"]}, {"x": xs, "h1": h1_s})
    xp0_s = [np.asarray(rr["xp0"]) for rr in r]

    nc = bass.Bass("TRN2", target_bir_lowering=False)
    ai, ao = _act_io(nc, ["xn"], ["f"])
    wd = {}
    for l in range(2):
        wd[f"wqkv{l}"] = nc.dram_tensor(f"wqkv{l}", [D, 1536], F32, kind="ExternalInput")
        wd[f"wo{l}"] = nc.dram_tensor(f"wo{l}", [512, D], F32, kind="ExternalInput")
        wd[f"qscale{l}"] = nc.dram_tensor(f"qs{l}", [128, 1], F32, kind="ExternalInput")
        wd[f"kscale{l}"] = nc.dram_tensor(f"ks{l}", [128, 1], F32, kind="ExternalInput")
        wd[f"lng{l}"] = nc.dram_tensor(f"lng{l}", [128, CT], F32, kind="ExternalInput")
        wd[f"lnb{l}"] = nc.dram_tensor(f"lnb{l}", [128, CT], F32, kind="ExternalInput")
        wd[f"w1_{l}"] = nc.dram_tensor(f"w1_{l}", [D, 5460], F32, kind="ExternalInput")
        wd[f"b1ap{l}"] = nc.dram_tensor(f"b1a{l}", [128, 22], F32, kind="ExternalInput")
        wd[f"b1gp{l}"] = nc.dram_tensor(f"b1g{l}", [128, 22], F32, kind="ExternalInput")
        wd[f"w2_{l}"] = nc.dram_tensor(f"w2_{l}", [2730, D], F32, kind="ExternalInput")
        wd[f"b2p{l}"] = nc.dram_tensor(f"b2{l}", [128, CT], F32, kind="ExternalInput")
    wd["swap64"] = nc.dram_tensor("swap64", [128, 64], F32, kind="ExternalInput")
    for nm in ("tab_qc", "tab_qs", "tab_kc", "tab_ks"):
        wd[nm] = nc.dram_tensor(nm, [128, S], F32, kind="ExternalInput")
    mh = {nm: nc.dram_tensor(nm, list(l0f[nm].shape), F32, kind="ExternalInput")
          for nm in ("l0A", "l0C")}
    with ExitStack() as stack:
        p = Prog(nc, stack)
        p.init_consts()
        c = Ctx(); c.p = p
        c.R = [p.sb(f"R{i}", [128, TCH], F32) for i in range(3)]
        c.sqc = p.sb("sqc", [128, 2 * TCH], BF16)
        c.psums = [p.ps(f"pp{i}", [128, 512], F32) for i in range(6)]
        c.prow = [p.ps(f"pr{i}", [1, 512], F32) for i in range(2)]
        c.ones_col_bf = p.sb("ones_col_bf", [128, 1], BF16)
        c.ones_row_f = p.sb("ones_row_f", [1, 128], F32)
        c.ones_row_bf = p.sb("ones_row_bf", [1, 128], BF16)
        c.recip_bf = p.sb("recip_bf", [128, 1], BF16)
        c.ident_bf = p.sb("ident_bf", [128, 128], BF16)
        g = "gpsimd"
        p.emit(g, lambda h: h.memset(c.ones_col_bf.ap, 1.0), writes=(c.ones_col_bf,), label="c1")
        p.emit(g, lambda h: h.memset(c.ones_row_f.ap, 1.0), writes=(c.ones_row_f,), label="c2")
        p.emit(g, lambda h: h.memset(c.ones_row_bf.ap, 1.0), writes=(c.ones_row_bf,), label="c2b")
        p.emit(g, lambda h: h.memset(c.recip_bf.ap, 1.0 / D), writes=(c.recip_bf,), label="c3")
        p.emit(g, lambda h: h.iota(c.ident_bf.ap, [[-1, 128]], base=0, channel_multiplier=1,
                                   allow_small_or_imprecise_dtypes=True),
               writes=(c.ident_bf,), label="idio")
        ev(p, lambda v: v.tensor_scalar(c.ident_bf.ap, c.ident_bf.ap, 0.0, None,
                                        ALU.is_equal),
           reads=(c.ident_bf,), writes=(c.ident_bf,), label="ideq")
        f = alloc_flash(p, c)
        F = alloc_f(p)
        masks = {nm: load_const_sb(p, nm, mh[nm], list(l0f[nm].shape))
                 for nm in ("l0A", "l0C")}
        emit_load_f(p, F, ai["xn"])
        for l in range(2):
            emit_l0f_attn(p, c, f, F, wd, masks, l, label=f"a{l}")
            emit_l0f_ffn(p, c, F, wd, l, label=f"f{l}")
        emit_store_f(p, F, ao["f"])
        p.finalize()
    cst = {k: l0f[k] for k in l0f}
    cst2 = {}
    for l in range(2):
        cst2[f"wqkv{l}"] = l0f[f"wqkv{l}"]; cst2[f"wo{l}"] = l0f[f"wo{l}"]
        cst2[f"qs{l}"] = l0f[f"qs{l}"]; cst2[f"ks{l}"] = l0f[f"ks{l}"]
        cst2[f"lng{l}"] = l0f[f"lng{l}"]; cst2[f"lnb{l}"] = l0f[f"lnb{l}"]
        cst2[f"w1_{l}"] = l0f[f"w1_{l}"]; cst2[f"b1a{l}"] = l0f[f"b1a{l}"]
        cst2[f"b1g{l}"] = l0f[f"b1g{l}"]; cst2[f"w2_{l}"] = l0f[f"w2_{l}"]
        cst2[f"b2{l}"] = l0f[f"b2{l}"]
    for nm in ("swap64", "tab_qc", "tab_qs", "tab_kc", "tab_ks", "l0A", "l0C"):
        cst2[nm] = l0f[nm]
    r = _run(nc, cst2, {"xn": xn_s})
    f_s = [np.asarray(rr["f"]) for rr in r]

    nc = bass.Bass("TRN2", target_bir_lowering=False)
    ai, ao = _act_io(nc, ["xp0", "f"], ["x2"])
    wdl1 = {nm: nc.dram_tensor(nm, [D, D], F32, kind="ExternalInput")
            for nm in ("wq", "wk", "wv", "wo")}
    fsh = {"win": [D, 64], "bin": [64, 1], "w1": [64, 16], "b1": [64, 16],
           "w2": [64, 16], "b2": [64, 1], "wout": [64, D], "bout": [128, 8]}
    fd = {nm: nc.dram_tensor("f_" + nm, sh, F32, kind="ExternalInput")
          for nm, sh in fsh.items()}
    mh = {nm: nc.dram_tensor("m_" + nm, list(l1[nm].shape), F32, kind="ExternalInput")
          for nm in ("a0", "a1", "cm")}
    with ExitStack() as stack:
        p = Prog(nc, stack)
        c = alloc_common(p)
        fl = alloc_flash(p, c)
        masks = {nm: load_const_sb(p, nm, mh[nm], list(l1[nm].shape))
                 for nm in ("a0", "a1", "cm")}
        emit_load_xt(p, c.XT, ai["xp0"], queue="gpsimd")
        emit_add_from_dram(p, c, ai["f"])
        emit_prenorm(p, c)
        emit_l1_attn(p, c, fl, {"q": wdl1["wq"], "k": wdl1["wk"], "v": wdl1["wv"], "o": wdl1["wo"]}, masks)
        emit_l1_ffn(p, c, {nm: fd[nm] for nm in fsh})
        emit_store_xt(p, c.XT, ao["x2"], queue="gpsimd")
        p.finalize()
    cst = {"wq": l1["wq"], "wk": l1["wk"], "wv": l1["wv"], "wo": l1["wo"],
           "m_a0": l1["a0"], "m_a1": l1["a1"], "m_cm": l1["cm"]}
    for nm in fsh:
        cst["f_" + nm] = l1[nm]
    r = _run(nc, cst, {"xp0": xp0_s, "f": f_s})
    x2_s = [np.asarray(rr["x2"]) for rr in r]

    nc = bass.Bass("TRN2", target_bir_lowering=False)
    ai, ao = _act_io(nc, ["x2"], ["xp2", "xn2"])
    wdl2 = {"q": nc.dram_tensor("wq2", [D, D], F32, kind="ExternalInput"),
            "k": nc.dram_tensor("wk2", [D, D], F32, kind="ExternalInput"),
            "v": nc.dram_tensor("wv2", [D, D], F32, kind="ExternalInput"),
            "o": nc.dram_tensor("wo2", [D, D], F32, kind="ExternalInput"),
            "b_q": nc.dram_tensor("bq2", [128, CT], F32, kind="ExternalInput"),
            "b_k": nc.dram_tensor("bk2", [128, CT], F32, kind="ExternalInput"),
            "b_v": nc.dram_tensor("bv2", [128, CT], F32, kind="ExternalInput"),
            "b_o": nc.dram_tensor("bo2", [128, CT], F32, kind="ExternalInput")}
    with ExitStack() as stack:
        p = Prog(nc, stack)
        c = alloc_common(p)
        fl = alloc_flash(p, c)
        emit_load_xt(p, c.XT, ai["x2"], queue="gpsimd")
        emit_prenorm(p, c)
        emit_l2_attn(p, c, fl, wdl2)
        emit_store_xt(p, c.XT, ao["xp2"], queue="gpsimd")
        emit_store_xt(p, c.XN, ao["xn2"], queue="gpsimd")
        p.finalize()
    cst = {"wq2": l2["wq2"], "wk2": l2["wk2"], "wv2": l2["wv2"], "wo2": l2["wo2"],
           "bq2": l2["bq2"], "bk2": l2["bk2"], "bv2": l2["bv2"], "bo2": l2["bo2"]}
    r = _run(nc, cst, {"x2": x2_s})
    xp2_s = [np.asarray(rr["xp2"]) for rr in r]
    xn2_s = [np.asarray(rr["xn2"]) for rr in r]

    nc = bass.Bass("TRN2", target_bir_lowering=False)
    ai, ao = _act_io(nc, ["xp2", "xn2"], ["x3"])
    wd = {"choice_wT": nc.dram_tensor("cwt2", [D, 8], F32, kind="ExternalInput"),
          "w1": [nc.dram_tensor(f"w1_{e}", [D, 2048], F32, kind="ExternalInput")
                 for e in range(8)],
          "w2": [nc.dram_tensor(f"w2_{e}", [2048, D], F32, kind="ExternalInput")
                 for e in range(8)]}
    with ExitStack() as stack:
        p = Prog(nc, stack)
        c = alloc_common(p, need_ident=False)
        emit_load_xt(p, c.XT, ai["xp2"], queue="gpsimd")
        emit_load_xt(p, c.XN, ai["xn2"], queue="gpsimd")
        emit_l2_ffn(p, c, wd, nc)
        emit_store_xt(p, c.XT, ao["x3"], queue="gpsimd")
        p.finalize()
    cst = {"cwt2": l2["cwt2"]}
    for e in range(8):
        cst[f"w1_{e}"] = l2[f"w1_{e}"]
        cst[f"w2_{e}"] = l2[f"w2_{e}"]
    r = _run(nc, cst, {"xp2": xp2_s, "xn2": xn2_s})
    x3_s = [np.asarray(rr["x3"]) for rr in r]

    nc = bass.Bass("TRN2", target_bir_lowering=False)
    ai, ao = _act_io(nc, ["x3"], ["xp3", "xn3"])
    wd = {"kv_down": nc.dram_tensor("kvd", [D, 128], F32, kind="ExternalInput"),
          "k_up": nc.dram_tensor("kup", [128, D], F32, kind="ExternalInput"),
          "v_up": nc.dram_tensor("vup", [128, D], F32, kind="ExternalInput"),
          "q": nc.dram_tensor("wq3", [D, D], F32, kind="ExternalInput"),
          "q_rope": nc.dram_tensor("wqr", [D, 32], F32, kind="ExternalInput"),
          "k_rope": nc.dram_tensor("wkr", [128, 32], F32, kind="ExternalInput"),
          "o": nc.dram_tensor("wo3", [D, D], F32, kind="ExternalInput"),
          "rope_t": nc.dram_tensor("rope_t", [128, 2 * S], F32, kind="ExternalInput")}
    mh = {nm: nc.dram_tensor(nm, [128, 896], F32, kind="ExternalInput")
          for nm in ("camA", "camC")}
    with ExitStack() as stack:
        p = Prog(nc, stack)
        c = alloc_common(p)
        fl = alloc_flash(p, c)
        masks = {nm: load_const_sb(p, nm, mh[nm], [128, 896]) for nm in mh}
        emit_load_xt(p, c.XT, ai["x3"], queue="gpsimd")
        emit_prenorm(p, c)
        emit_l3_attn(p, c, fl, wd, masks)
        emit_store_xt(p, c.XT, ao["xp3"], queue="gpsimd")
        emit_store_xt(p, c.XN, ao["xn3"], queue="gpsimd")
        p.finalize()
    cst = {"kvd": l3["kvd"], "kup": l3["kup"], "vup": l3["vup"], "wq3": l3["wq3"],
           "wqr": l3["wqr"], "wkr": l3["wkr"], "wo3": l3["wo3"],
           "rope_t": l3["rope_t"], "camA": l3["camA"], "camC": l3["camC"]}
    r = _run(nc, cst, {"x3": x3_s})
    xp3_s = [np.asarray(rr["xp3"]) for rr in r]
    xn3_s = [np.asarray(rr["xn3"]) for rr in r]

    nc = bass.Bass("TRN2", target_bir_lowering=False)
    ai, ao = _act_io(nc, ["xp3", "xn3"], ["y"])
    wd = {"Wr": nc.dram_tensor("wr", [D, 8], F32, kind="ExternalInput"),
          "router_bias": nc.dram_tensor("rb", [1, 8], F32, kind="ExternalInput"),
          "conv_w": nc.dram_tensor("cw3", [128, 32], F32, kind="ExternalInput"),
          "conv_b": nc.dram_tensor("cb3", [128, CT], F32, kind="ExternalInput")}
    for e in range(4):
        wd[f"Wg{e}"] = nc.dram_tensor(f"wg{e}", [D, 512], F32, kind="ExternalInput")
        wd[f"Wu{e}"] = nc.dram_tensor(f"wu{e}", [D, 512], F32, kind="ExternalInput")
        wd[f"Wd{e}"] = nc.dram_tensor(f"wd{e}", [512, D], F32, kind="ExternalInput")
    with ExitStack() as stack:
        p = Prog(nc, stack)
        c = alloc_common(p)
        emit_load_xt(p, c.XT, ai["xp3"], queue="gpsimd")
        emit_load_xt(p, c.XN, ai["xn3"], queue="gpsimd")
        emit_l3_ffn(p, c, wd)
        emit_store_xt(p, c.XT, ao["y"], queue="gpsimd")
        p.finalize()
    cst = {"wr": l3["wr"], "rb": l3["rb"], "cw3": l3["cw3"], "cb3": l3["cb3"]}
    for e in range(4):
        cst[f"wg{e}"] = l3[f"wg{e}"]
        cst[f"wu{e}"] = l3[f"wu{e}"]
        cst[f"wd{e}"] = l3[f"wd{e}"]
    r = _run(nc, cst, {"xp3": xp3_s, "xn3": xn3_s})
    y_s = [np.asarray(rr["y"]) for rr in r]

    out = np.concatenate([y.transpose(0, 2, 1) for y in y_s], axis=0)
    return np.ascontiguousarray(out.astype(np.float32))


def _mk_alias():
    A = {}

    def a(name, pool, c0, w):
        A[name] = (pool, c0, w)

    a("XN", "PXN", 0, 32768)
    a("Fstream", "PXN", 0, 32784)
    a("fl_qT", "PFL", 0, 2048); a("fl_kT", "PFL", 2048, 2048)
    a("fl_vx", "PFL", 4096, 2080); a("fl_ncol", "PFL", 6176, 16)
    a("fl_ob", "PFL", 6192, 512); a("fl_pt0", "PFL", 6704, 512)
    a("fl_pt1", "PFL", 7216, 512)
    a("lfa_sqb", "PFL", 7728, 2048); a("lfa_qr2", "PFL", 9776, 2048)
    a("fl_att", "PFL", 11824, 2048); a("fl_nrow", "PFL", 13872, 2048)
    a("s1rep", "PFL", 0, 4096)
    a("l0muc", "PFL", 0, 2048); a("l0mxb", "PFL", 2048, 2080)
    a("l0mcv", "PFL", 4128, 2048); a("l0mrs", "PFL", 6176, 2048)
    a("l0mis", "PFL", 8224, 2048); a("l0myb", "PFL", 10272, 2048)
    a("l0mtmp", "PFL", 12320, 256)
    a("l0iuc", "PFL", 0, 4096); a("l0igh", "PFL", 4096, 2048)
    a("l0is2", "PFL", 8192, 4096)
    a("lff_lnh", "PFL", 0, 8192)
    a("lff_ah", "PFL", 8192, 2048); a("lff_gh", "PFL", 10240, 2048)
    a("l1fss", "PFL", 0, 2048); a("l1fh", "PFL", 2048, 2048)
    a("l1fg", "PFL", 4096, 2048); a("l1fob", "PFL", 6144, 512)
    a("cst_a0", "PFL", 7728, 128); a("cst_a1", "PFL", 7856, 256)
    a("cst_cm", "PFL", 8112, 1024)
    a("l2fxe", "PFL", 0, 4096); a("l2fhe", "PFL", 4096, 2048)
    a("l2fxtok", "PFL", 6144, 1024); a("l2fybb", "PFL", 7168, 1024)
    a("l2fjk", "PFL", 8192, 2048)
    a("l3aqcat", "PFL", 7728, 2048); a("l3akcat", "PFL", 9776, 2048)
    a("l3fcwrep", "PFL", 0, 4096); a("l3fgs", "PFL", 4096, 2048)
    a("l3fus", "PFL", 6144, 2048); a("l3fcvb", "PFL", 8192, 2048)
    a("l3fcwr", "PFL", 10240, 4096)
    a("l0mpre", "PW", 0, 8192)
    a("l0mwsl0", "PW", 0, 4096); a("l0mwsl1", "PW", 4096, 4096)
    a("l0iwgb", "PW", 0, 4096); a("l0iwub", "PW", 4096, 4096)
    a("l0iwdb", "PW", 8192, 4096)
    for l in range(2):
        a(f"cst_l0ftqc{l}", "PW", 0, 2048)
        a(f"cst_l0ftqs{l}", "PW", 2048, 2048)
        a(f"cst_l0ftkc{l}", "PW", 4096, 2048)
        a(f"cst_l0ftks{l}", "PW", 6144, 2048)
        a(f"cst_l0A_{l}", "PW", 8192, 1152)
        a(f"cst_l0C_{l}", "PW", 9344, 1536)
    a("lfa_wq", "PW", 10880, 1024); a("lfa_wk", "PW", 11904, 1024)
    a("lfa_wv", "PW", 12928, 1024); a("lfa_wo", "PW", 13952, 1024)
    a("lff_w1a", "PW", 0, 4096); a("lff_w1g", "PW", 4096, 4096)
    a("lff_w2", "PW", 8192, 4096)
    for i in range(2):
        a(f"l1wq{i}", "PW", 0 + 1024 * i, 1024)
        a(f"l1wk{i}", "PW", 2048 + 1024 * i, 1024)
        a(f"l1wv{i}", "PW", 4096 + 1024 * i, 1024)
        a(f"l1wo{i}", "PW", 6144 + 1024 * i, 1024)
    a("w_l1fwin", "PW", 8192, 512); a("l1fwout", "PW", 8704, 1024)
    a("l2awq", "PW", 0, 1024); a("l2awk", "PW", 1024, 1024)
    a("l2awv", "PW", 2048, 1024); a("l2awo", "PW", 3072, 1024)
    a("l2akvs", "PW", 4096, 66)
    a("l2fw1b", "PW", 0, 4096); a("l2fw2b", "PW", 4096, 4096)
    a("w_l3akvd", "PW", 0, 1024); a("l3awq0", "PW", 1024, 1024)
    a("l3awku0", "PW", 2048, 128); a("l3awvu0", "PW", 2176, 128)
    a("l3awo0", "PW", 2304, 1024); a("w_l3awqr", "PW", 3328, 256)
    a("l3awkr", "PW", 3584, 32)
    a("cst_camA", "PW", 3616, 896); a("cst_camC", "PW", 4512, 896)
    a("l3aqrot", "PW", 5408, 4096); a("l3akrot", "PW", 9504, 4096)
    a("l3ackv", "PW", 13600, 2048)
    a("l3fwg", "PW", 0, 4096); a("l3fwu", "PW", 4096, 4096)
    a("l3fwd", "PW", 8192, 4096)
    a("R3", "PF32", 0, 512); a("fl_ms", "PF32", 0, 512)
    a("fl_ms2", "PF32", 2052, 512)
    a("fl_mcol", "PF32", 512, 4)
    a("R0", "PF32", 516, 512); a("R1", "PF32", 1028, 512)
    a("R2", "PF32", 1540, 512)
    a("l2fp2", "PF32", 0, 2048); a("l2facc", "PF32", 0, 1024)
    a("l2fpbm", "PF32", 2052, 128); a("l2fbidx", "PF32", 2180, 128)
    a("l2fgix", "PF32", 2308, 128); a("l2fids", "PF32", 2436, 4)
    a("l2fiotc", "PF32", 2440, 1); a("l2flo", "PF32", 2441, 6)
    a("l2f_stg", "PF32", 2447, 128)
    a("l2fidf", "PF32", 2575, 128); a("l3fidf", "PF32", 2575, 128)
    a("l3fcw", "PF32", 0, 256); a("l3fscb", "PF32", 2052, 8)
    a("l3frb", "PF32", 2060, 8)
    return A


def kernel(**inputs):
    xs, l0, l0f, l1, l2, l3 = _host_prep(inputs)

    nc = bass.Bass("TRN2", target_bir_lowering=False)
    ei = lambda n, sh: nc.dram_tensor(n, sh, F32, kind="ExternalInput")
    x_in = ei("x", [NB, D, S])
    y_out = nc.dram_tensor("y", [NB, D, S], F32, kind="ExternalOutput")
    xn_dram = nc.dram_tensor("xnb", [NB, D, S], BF16, kind="Internal")
    s1_dram = nc.dram_tensor("s1b", [128, T], BF16, kind="Internal")

    wd_l0m = {nm: ei(nm, [D, D]) for nm in ("Wx", "Wy", "Wi", "Wa", "Wo")}
    wd_l0m["norm1_pcol"] = ei("n1p", [128, CT])
    wd_l0m["conv_w_pcol"] = ei("cwp", [128, 4 * CT])
    wd_l0m["conv_b_pcol"] = ei("cbp", [128, CT])
    wd_l0m["lam_pcol"] = ei("lamp", [128, CT])
    wd_l0i = {"Wg": ei("Wg", [D, 4096]), "Wu": ei("Wu", [D, 4096]),
              "Wd": ei("Wd", [4096, D]), "norm2_pcol": ei("n2p", [128, CT])}
    wd_l0f = {}
    for l in range(2):
        wd_l0f[f"wqkv{l}"] = ei(f"wqkv{l}", [D, 1536])
        wd_l0f[f"wo{l}"] = ei(f"wo{l}", [512, D])
        wd_l0f[f"qscale{l}"] = ei(f"qs{l}", [128, 1])
        wd_l0f[f"kscale{l}"] = ei(f"ks{l}", [128, 1])
        wd_l0f[f"lng{l}"] = ei(f"lng{l}", [128, CT])
        wd_l0f[f"lnb{l}"] = ei(f"lnb{l}", [128, CT])
        wd_l0f[f"w1_{l}"] = ei(f"w1_{l}", [D, 5460])
        wd_l0f[f"b1ap{l}"] = ei(f"b1a{l}", [128, 22])
        wd_l0f[f"b1gp{l}"] = ei(f"b1g{l}", [128, 22])
        wd_l0f[f"w2_{l}"] = ei(f"w2_{l}", [2730, D])
        wd_l0f[f"b2p{l}"] = ei(f"b2{l}", [128, CT])
    wd_l0f["swap64"] = ei("swap64", [128, 64])
    for nm in ("tab_qc", "tab_qs", "tab_kc", "tab_ks"):
        wd_l0f[nm] = ei(nm, [128, S])
    mh_l0f = {nm: ei(nm, list(l0f[nm].shape)) for nm in ("l0A", "l0C")}
    wd_l1a = {"q": ei("wq", [D, D]), "k": ei("wk", [D, D]),
              "v": ei("wv", [D, D]), "o": ei("wo", [D, D])}
    fsh = {"win": [D, 64], "bin": [64, 1], "w1": [64, 16], "b1": [64, 16],
           "w2": [64, 16], "b2": [64, 1], "wout": [64, D], "bout": [128, 8]}
    fd_l1f = {nm: ei("f_" + nm, sh) for nm, sh in fsh.items()}
    mh_l1 = {nm: ei("m_" + nm, list(l1[nm].shape)) for nm in ("a0", "a1", "cm")}
    wd_l2a = {"q": ei("wq2", [D, D]), "k": ei("wk2", [D, D]),
              "v": ei("wv2", [D, D]), "o": ei("wo2", [D, D]),
              "b_q": ei("bq2", [128, CT]), "b_k": ei("bk2", [128, CT]),
              "b_v": ei("bv2", [128, CT]), "b_o": ei("bo2", [128, CT])}
    wd_l2f = {"choice_wT": ei("cwt2", [D, 8]),
              "w1": [ei(f"e1_{e}", [D, 2048]) for e in range(8)],
              "w2": [ei(f"e2_{e}", [2048, D]) for e in range(8)]}
    wd_l3a = {"kv_down": ei("kvd", [D, 128]), "k_up": ei("kup", [128, D]),
              "v_up": ei("vup", [128, D]), "q": ei("wq3", [D, D]),
              "q_rope": ei("wqr", [D, 32]), "k_rope": ei("wkr", [128, 32]),
              "o": ei("wo3", [D, D]), "rope_t": ei("rope_t", [128, 2 * S])}
    mh_l3 = {nm: ei(nm, [128, 896]) for nm in ("camA", "camC")}
    wd_l3f = {"Wr": ei("wr", [D, 8]), "router_bias": ei("rb", [1, 8]),
              "conv_w": ei("cw3", [128, 32]), "conv_b": ei("cb3", [128, CT])}
    for e in range(4):
        wd_l3f[f"Wg{e}"] = ei(f"wg{e}", [D, 512])
        wd_l3f[f"Wu{e}"] = ei(f"wu{e}", [D, 512])
        wd_l3f[f"Wd{e}"] = ei(f"wd{e}", [512, D])

    with ExitStack() as stack:
        p = Prog(nc, stack)
        p.mkpool("PXN", 32784, BF16)
        p.mkpool("PFL", 15920, BF16)
        p.mkpool("PW", 15648, BF16)
        p.mkpool("PF32", 2703, F32)
        p.alias = _mk_alias()
        c = alloc_common(p)
        fl = alloc_flash(p, c)
        F = alloc_f(p)
        S1 = p.sb("s1rep", [128, T], BF16)
        emit_load_xt(p, c.XT, x_in, queue="gpsimd")
        emit_ln(p, c, c.XT, c.XN, eps=1e-5, s1rep=S1, s1eps=1e-6)
        emit_store_xt(p, c.XN, xn_dram, queue="sync")
        p.dma("sync", s1_dram.ap(), S1.ap, reads=(S1,), label="sts1")
        emit_l0_mix(p, c, wd_l0m, None, chw=256, stream_w=True, nc=nc,
                    s1_dram=s1_dram)
        emit_l0_ffn_int(p, c, wd_l0i)
        emit_load_f(p, F, xn_dram)
        for l in range(2):
            masks = {nm: load_const_sb(p, f"{nm}_{l}", mh_l0f[nm],
                                       list(l0f[nm].shape))
                     for nm in ("l0A", "l0C")}
            emit_l0f_attn(p, c, fl, F, wd_l0f, masks, l, label=f"a{l}")
            emit_l0f_ffn(p, c, F, wd_l0f, l, label=f"f{l}")
        for b in range(NB):
            for ct in range(CT):
                sl = slice(ct * T + b * S, ct * T + (b + 1) * S)
                ev(p, (lambda v, o=c.XT[:, sl], x2=c.XT[:, sl],
                       aa=F[:, fcol(ct, b, 0): fcol(ct, b, 0) + S]:
                       v.tensor_tensor(o, x2, aa, ALU.add)),
                   reads=(c.XT, F), writes=(c.XT,), label="addF")
        emit_prenorm(p, c)
        masks1 = {nm: load_const_sb(p, nm, mh_l1[nm], list(l1[nm].shape))
                  for nm in ("a0", "a1", "cm")}
        emit_l1_attn(p, c, fl, wd_l1a, masks1)
        emit_l1_ffn(p, c, fd_l1f)
        emit_prenorm(p, c)
        emit_l2_attn(p, c, fl, wd_l2a)
        emit_l2_ffn(p, c, wd_l2f, nc)
        emit_prenorm(p, c)
        masks3 = {nm: load_const_sb(p, nm, mh_l3[nm], [128, 896])
                  for nm in mh_l3}
        emit_l3_attn(p, c, fl, wd_l3a, masks3)
        emit_l3_ffn(p, c, wd_l3f)
        emit_store_xt(p, c.XT, y_out, queue="gpsimd")
        p.finalize()

    cst = {"Wx": l0["Wx"], "Wy": l0["Wy"], "Wi": l0["Wi"], "Wa": l0["Wa"],
           "Wo": l0["Wo"], "n1p": l0["n1p"], "cwp": l0["cwp"],
           "cbp": l0["cbp"], "lamp": l0["lamp"],
           "Wg": l0["Wg"], "Wu": l0["Wu"], "Wd": l0["Wd"], "n2p": l0["n2p"]}
    for l in range(2):
        for k in ("wqkv", "wo", "qs", "ks", "lng", "lnb", "b2"):
            cst[f"{k}{l}"] = l0f[f"{k}{l}"]
        for k in ("w1_", "w2_"):
            cst[f"{k}{l}"] = l0f[f"{k}{l}"]
        cst[f"b1a{l}"] = l0f[f"b1a{l}"]
        cst[f"b1g{l}"] = l0f[f"b1g{l}"]
    for nm in ("swap64", "tab_qc", "tab_qs", "tab_kc", "tab_ks", "l0A", "l0C"):
        cst[nm] = l0f[nm]
    cst.update({"wq": l1["wq"], "wk": l1["wk"], "wv": l1["wv"], "wo": l1["wo"],
                "m_a0": l1["a0"], "m_a1": l1["a1"], "m_cm": l1["cm"]})
    for nm in fsh:
        cst["f_" + nm] = l1[nm]
    cst.update({"wq2": l2["wq2"], "wk2": l2["wk2"], "wv2": l2["wv2"],
                "wo2": l2["wo2"], "bq2": l2["bq2"], "bk2": l2["bk2"],
                "bv2": l2["bv2"], "bo2": l2["bo2"], "cwt2": l2["cwt2"]})
    for e in range(8):
        cst[f"e1_{e}"] = l2[f"w1_{e}"]
        cst[f"e2_{e}"] = l2[f"w2_{e}"]
    cst.update({"kvd": l3["kvd"], "kup": l3["kup"], "vup": l3["vup"],
                "wq3": l3["wq3"], "wqr": l3["wqr"], "wkr": l3["wkr"],
                "wo3": l3["wo3"], "rope_t": l3["rope_t"],
                "camA": l3["camA"], "camC": l3["camC"],
                "wr": l3["wr"], "rb": l3["rb"], "cw3": l3["cw3"],
                "cb3": l3["cb3"]})
    for e in range(4):
        cst[f"wg{e}"] = l3[f"wg{e}"]
        cst[f"wu{e}"] = l3[f"wu{e}"]
        cst[f"wd{e}"] = l3[f"wd{e}"]

    r = _run(nc, cst, {"x": xs})
    y_s = [np.asarray(rr["y"]) for rr in r]
    out = np.concatenate([yy.transpose(0, 2, 1) for yy in y_s], axis=0)
    return np.ascontiguousarray(out.astype(np.float32))
